# version 08
# baseline (speedup 1.0000x reference)
import contextlib
import numpy as np
import ml_dtypes
import concourse.bass as bass
import concourse.mybir as mybir
from concourse.bass_utils import run_bass_kernel_spmd
from concourse.expressions_rust import RuntimeValue

BFNP = ml_dtypes.bfloat16
F32 = mybir.dt.float32
BF16 = mybir.dt.bfloat16
U8 = mybir.dt.uint8
U32 = mybir.dt.uint32
AF = mybir.ActivationFunctionType
ALU = mybir.AluOpType
AX = mybir.AxisListType

D = 2048
H = 16
QR, KVR = 768, 256
P_TOT = 9216
NKS = 72
NINT = 56
QL = 2048
NB = 18
EPS_LN = 1e-5
EPS_RMS = 1e-6
LAM = float(96.0 ** -0.25)
SKR = float((16.0 / np.sqrt(96.0)) ** 0.5)
SQRP = SKR / 16.0
DBG = ()
SPLIT_EVAC = False

ENGS = ("tensor", "vector", "scalar", "sync", "gpsimd")


class Sem:
    __slots__ = ("h", "n")

    def __init__(self, h):
        self.h = h
        self.n = 0


class Prog:
    def __init__(self):
        self.q = {e: [] for e in ENGS}

    def emit(self, e, fn, inc=None, k=1):
        if inc is None:
            self.q[e].append(fn)
            return

        def g(E, fn=fn, sem=inc, k=k):
            fn(E).then_inc(sem.h, k)
        self.q[e].append(g)
        inc.n += k

    def dma(self, e, fn, sem):
        self.emit(e, fn, inc=sem, k=16)

    def wait(self, e, sem, val=None):
        v = sem.n if val is None else val
        if v <= 0:
            return
        self.q[e].append(lambda E, h=sem.h, v=v: E.wait_ge(h, v))

    def replay(self, block):
        for e in ENGS:
            ops = self.q[e]
            getattr(block, e)(lambda E, ops=ops: [f(E) for f in ops])


def _rope_tab(positions):
    f = np.arange(32) // 2
    invf = 10000.0 ** (-(f / 16.0))
    return np.cos(positions[None, :].astype(np.float64) * invf[:, None])


def core_inputs(c, a):
    QA0 = 1024 * c
    QB0 = 1024 * (7 - c)
    PA = QA0 + 1024
    PB = QB0 + 1024
    x = np.asarray(a["L0_attn_x"])
    d = {}
    d["xp"] = np.ascontiguousarray(
        np.concatenate([x[0, :QA0], x[1, :QB0], x[0, QA0:PA], x[1, QB0:PB]], axis=0))
    pos_k = np.concatenate([np.arange(QA0), np.arange(QB0),
                            np.arange(QA0, PA), np.arange(QB0, PB)])
    pos_q = np.concatenate([np.arange(QA0, PA), np.arange(QB0, PB)])
    ck = _rope_tab(pos_k)
    cq = _rope_tab(pos_q)
    tabK = np.zeros((128, P_TOT), np.float32)
    tabK[32:64] = tabK[64:96] = (np.sqrt(2.0) * (SKR / 16.0) * ck).astype(np.float32)
    tabQ = np.zeros((128, QL), np.float32)
    tabQ[32:64] = tabQ[64:96] = (np.sqrt(2.0) * SQRP * cq).astype(np.float32)
    d["tabK"] = tabK
    d["tabQ"] = tabQ
    qt = np.zeros((1, 128), np.uint32)
    for g in range(NINT // 8):
        qt[0, g] = 0 if g < c else 1
    d["qtab"] = qt
    d["w_cq"] = np.asarray(a["L0_attn_w_cq"])
    d["w_qn"] = np.asarray(a["L0_attn_w_q_nope"])
    d["w_qr_pad"] = np.ascontiguousarray(np.concatenate(
        [np.zeros((QR, 64), np.float32), np.asarray(a["L0_attn_w_q_rope"])], axis=1))
    d["q_g"] = np.ascontiguousarray(np.asarray(a["L0_attn_q_g"]).reshape(6, 128).T)
    d["w_ckv"] = np.asarray(a["L0_attn_w_ckv"])
    d["w_kn"] = np.asarray(a["L0_attn_w_k_nope"])
    d["w_v"] = np.asarray(a["L0_attn_w_v"])
    d["kv_g"] = np.ascontiguousarray(np.asarray(a["L0_attn_kv_g"]).reshape(2, 128).T)
    wkr = np.asarray(a["L0_attn_w_k_rope"])
    d["w_kr_pad"] = np.ascontiguousarray(np.concatenate(
        [np.zeros((D, 32), np.float32), wkr, wkr], axis=1))
    d["w_proj"] = np.asarray(a["L0_attn_w_proj"])
    wgT = np.asarray(a["L0_ffn_gate_w"]).astype(np.uint8).T
    wuT = np.asarray(a["L0_ffn_up_w"]).astype(np.uint8).T
    wdT = np.asarray(a["L0_ffn_down_w"]).astype(np.uint8).T
    d["wg_t"] = np.ascontiguousarray(
        wgT.reshape(16, 128, 64, 128).transpose(2, 1, 0, 3).reshape(64, 128, 2048))
    d["wu_t"] = np.ascontiguousarray(
        wuT.reshape(16, 128, 64, 128).transpose(2, 1, 0, 3).reshape(64, 128, 2048))
    d["wd_t"] = np.ascontiguousarray(
        wdT.reshape(64, 128, 16, 128).transpose(2, 1, 0, 3).reshape(16, 128, 8192))
    d["gsc"] = np.ascontiguousarray(np.asarray(a["L0_ffn_gate_scale"]).reshape(64, 128).T)
    d["usc"] = np.ascontiguousarray(np.asarray(a["L0_ffn_up_scale"]).reshape(64, 128).T)
    d["dsc"] = np.ascontiguousarray(np.asarray(a["L0_ffn_down_scale"]).reshape(16, 128).T)
    stair = np.zeros((128, 2048), np.float32)
    for i in range(128):
        stair[i, 1024 + i:] = 1.0
    d["stair"] = stair.astype(BFNP)
    d["ident"] = np.eye(128, dtype=BFNP)
    return d


class MK:
    def __init__(self, dbg=()):
        self.dbg = dbg
        self.nc = bass.Bass("TRN2", target_bir_lowering=False)
        self.P = Prog()

    def sem(self, name):
        s = Sem(self.st.enter_context(self.nc.semaphore(name)))
        setattr(self, "s_" + name, s)
        return s

    def sb(self, stack, name, shape, dt):
        return stack.enter_context(self.nc.sbuf_tensor(name, shape, dt))

    def build(self):
        nc, P = self.nc, self.P
        dbg = self.dbg
        self.inp = {}
        for name, shape, dt in [
            ("xp", [P_TOT, D], F32), ("tabK", [128, P_TOT], F32),
            ("tabQ", [128, QL], F32), ("qtab", [1, 128], U32),
            ("w_cq", [D, QR], F32), ("w_qn", [QR, 1024], F32),
            ("w_qr_pad", [QR, 576], F32), ("q_g", [128, 6], F32),
            ("w_ckv", [D, KVR], F32), ("w_kn", [KVR, 1024], F32),
            ("w_v", [KVR, 1024], F32), ("kv_g", [128, 2], F32),
            ("w_kr_pad", [D, 96], F32), ("w_proj", [1024, D], F32),
            ("wg_t", [64, 128, 2048], U8), ("wu_t", [64, 128, 2048], U8),
            ("wd_t", [16, 128, 8192], U8),
            ("gsc", [128, 64], F32), ("usc", [128, 64], F32), ("dsc", [128, 16], F32),
            ("stair", [128, 2048], BF16), ("ident", [128, 128], BF16),
        ]:
            self.inp[name] = nc.dram_tensor(name, shape, dt, kind="ExternalInput").ap()
        inp = self.inp
        self.out = nc.dram_tensor("out", [QL, D], F32, kind="ExternalOutput").ap()
        self.x2_d = nc.dram_tensor(
            "x2_scr", [QL, D], F32,
            kind="ExternalOutput" if "x2" in dbg else "Internal").ap()
        self.ao_d = nc.dram_tensor(
            "ao_scr", [H, 64, QL], BF16,
            kind="ExternalOutput" if "ao" in dbg else "Internal").ap()
        self.dbg_t = {}
        for key, nm, shape, dt, want in [
            ("ckv", "dbg_ckv", [128, 2 * P_TOT], BF16, "ckv"),
            ("k0", "dbg_k0", [128, P_TOT], BF16, "kaug"),
            ("k1", "dbg_k1", [128, P_TOT], BF16, "kaug"),
            ("vv", "dbg_v", [128, NKS * 130], BF16, "kaug"),
            ("normq", "dbg_normq", [128, 6 * QL], BF16, "normq"),
            ("q0", "dbg_q0", [128, QL], BF16, "qcur"),
            ("q1", "dbg_q1", [128, QL], BF16, "qcur"),
            ("den", "dbg_den", [16, QL], F32, "den"),
            ("xlnT", "dbg_xlnT", [128, 16 * 512], BF16, ("p1x", "prx")),
            ("stm", "dbg_stm", [128, 8], F32, "p1x"),
            ("rstdc", "dbg_rstdc", [128, 4], F32, "p1x"),
            ("rstdb", "dbg_rstdb", [128, 512], F32, ("p1x", "prx")),
            ("bm", "dbg_bm", [2, 16], F32, "bm"),
            ("pn", "dbg_pn", [128, 8 * QL], BF16, "pn"),
            ("f1", "dbg_f1", [128, 16 * 512], BF16, "ffx"),
            ("f2", "dbg_f2", [128, 64 * 512], BF16, "ffx"),
            ("f3", "dbg_f3", [128, 16 * 512], BF16, "ffx"),
            ("rcp", "dbg_rcp", [16, QL], F32, "pn"),
        ]:
            wl = want if isinstance(want, tuple) else (want,)
            if any(w in dbg for w in wl):
                self.dbg_t[key] = nc.dram_tensor(nm, shape, dt, kind="ExternalOutput").ap()

        self.ln_at = [None, None]
        self.sq_at = [None, None]
        self.krow_at = 0
        self.st = contextlib.ExitStack()
        with self.st:
            st = self.st
            for nm in ["ini", "x", "tk", "sc", "vc", "ap", "tp", "te", "mm", "ax",
                       "cw", "wc", "qe", "ex", "xv", "bn", "sg", "eg", "mk", "pv",
                       "oe", "od", "mh", "pj", "pq", "fw", "fg", "fh", "fy", "fo",
                       "ou", "x2", "rc", "ma", "fz"]:
                self.sem(nm)

            self.ones_bf = self.sb(st, "ones_bf", [128, 1], BF16)
            self.ones_row = self.sb(st, "ones_row", [1, 128], F32)
            self.eps_t = self.sb(st, "eps_t", [128, 2], F32)
            self.stair_sb = self.sb(st, "stair_sb", [128, 2048], BF16)
            self.ident_sb = self.sb(st, "ident_sb", [128, 128], BF16)
            self.qtab_sb = self.sb(st, "qtab_sb", [1, 128], U32)
            self.zer_bf = self.sb(st, "zer_bf", [128, 512], BF16)
            self.stm = self.sb(st, "stm", [128, 8], F32)
            self.rstd_c = self.sb(st, "rstd_c", [128, 4], F32)
            self.mneg_bf = self.sb(st, "mneg_bf", [128, 2], F32)
            self.rstd_bf = self.sb(st, "rstd_bf", [128, 4], BF16)
            self.ones_rbf = self.sb(st, "ones_rbf", [1, 128], BF16)
            self.bm2 = self.sb(st, "bm2", [2, 16], F32)
            self.bmo = self.sb(st, "bmo", [1, 16], F32)
            self.bm_scr = self.sb(st, "bm_scr", [2, 1], F32)
            self.krm = self.sb(st, "krm", [1, 2], F32)
            self.lnb = self.sb(st, "lnb", [1, 2], F32)
            self.gsc_sb = self.sb(st, "gsc_sb", [128, 64], F32)
            self.usc_sb = self.sb(st, "usc_sb", [128, 64], F32)
            self.dsc_sb = self.sb(st, "dsc_sb", [128, 16], F32)
            self.kvg_sb = self.sb(st, "kvg_sb", [128, 2], F32)
            self.qg_sb = self.sb(st, "qg_sb", [128, 6], F32)

            for nm, dst in [("stair", self.stair_sb), ("ident", self.ident_sb),
                            ("qtab", self.qtab_sb), ("gsc", self.gsc_sb),
                            ("usc", self.usc_sb), ("dsc", self.dsc_sb),
                            ("kv_g", self.kvg_sb), ("q_g", self.qg_sb)]:
                P.dma("sync", lambda E, nm=nm, dst=dst: E.dma_start(
                    out=dst[:], in_=inp[nm]), self.s_ini)
            P.emit("vector", lambda E: E.memset(self.ones_bf[:], 1.0))
            P.emit("vector", lambda E: E.memset(self.ones_row[:], 1.0))
            P.emit("vector", lambda E: E.memset(self.ones_rbf[:], 1.0))
            P.emit("vector", lambda E: E.memset(self.eps_t[:, 0:1], EPS_LN))
            P.emit("vector", lambda E: E.memset(self.eps_t[:, 1:2], EPS_RMS))
            P.emit("vector", lambda E: E.memset(self.zer_bf[:], 0.0))
            P.emit("vector", lambda E: E.memset(self.bm2[:], 0.0))
            P.emit("vector", lambda E: E.memset(self.krm[:], 0.0), inc=self.s_vc)

            dn_sc = contextlib.ExitStack()
            with dn_sc:
              self.denoms = self.sb(dn_sc, "denoms", [16, QL], F32)
              self.recip = self.sb(dn_sc, "recip", [16, QL], F32)
              ca = contextlib.ExitStack()
              with ca:
                  self.ckv_sb = self.sb(ca, "ckv_sb", [128, 2 * P_TOT], BF16)
                  self.kaug = [self.sb(ca, f"kaug{i}", [128, P_TOT], BF16) for i in range(2)]
                  self.v_sb = self.sb(ca, "v_sb", [128, NKS * 130], BF16)
                  self.normq = self.sb(ca, "normq", [128, 6 * QL], BF16)
                  self.tabq_sb = self.sb(ca, "tabq_sb", [128, QL], F32)

                  P.emit("vector", lambda E: E.memset(self.v_sb[:], 1.0))
                  P.emit("vector", lambda E: E.memset(self.kaug[0][96:128, :], 0.0))
                  P.emit("vector", lambda E: E.memset(self.kaug[0][96:97, :], -1.0))
                  P.emit("vector", lambda E: E.memset(self.kaug[1][0:32, :], 0.0))
                  P.emit("vector", lambda E: E.memset(self.kaug[1][0:1, :], -1.0),
                         inc=self.s_vc)

                  self.phase1()
                  if "p1only" not in dbg:
                      self.prep()
                  if "p1only" not in dbg and "noattn" not in dbg:
                      self.attention()

                  if "ckv" in dbg:
                      P.wait("gpsimd", self.s_cw)
                      P.dma("gpsimd", lambda E: E.dma_start(
                          out=self.dbg_t["ckv"], in_=self.ckv_sb[:]), self.s_ou)
                  if "kaug" in dbg:
                      P.wait("gpsimd", self.s_xv)
                      P.dma("gpsimd", lambda E: E.dma_start(
                          out=self.dbg_t["k0"], in_=self.kaug[0][:]), self.s_ou)
                      P.dma("gpsimd", lambda E: E.dma_start(
                          out=self.dbg_t["k1"], in_=self.kaug[1][:]), self.s_ou)
                      P.dma("gpsimd", lambda E: E.dma_start(
                          out=self.dbg_t["vv"], in_=self.v_sb[:]), self.s_ou)
                  if "normq" in dbg and "p1only" not in dbg:
                      P.wait("gpsimd", self.s_qe)
                      P.dma("gpsimd", lambda E: E.dma_start(
                          out=self.dbg_t["normq"], in_=self.normq[:]), self.s_ou)
                  if "bm" in dbg:
                      P.wait("gpsimd", self.s_bn)
                      P.dma("gpsimd", lambda E: E.dma_start(
                          out=self.dbg_t["bm"], in_=self.bm2[:]), self.s_ou)

              if "den" in dbg:
                  P.wait("gpsimd", self.s_od)
                  P.dma("gpsimd", lambda E: E.dma_start(
                      out=self.dbg_t["den"], in_=self.denoms[:]), self.s_ou)
              if "noftail" not in dbg:
                  self.proj()
            if "noftail" not in dbg:
                self.ffn()

            P.wait("gpsimd", self.s_ou)
            P.wait("sync", self.s_x2)

            with nc.Block() as block:
                P.replay(block)
        return nc

    def emit_ln(self, src, dst, dump, pa=0):
        P = self.P
        stm = self.stm[:, pa * 4:pa * 4 + 4]
        rsc = self.rstd_c[:, pa * 2:pa * 2 + 2]
        if self.ln_at[pa] is not None:
            P.wait("scalar", self.s_ap, self.ln_at[pa])
            P.wait("vector", self.s_ap, self.ln_at[pa])
        P.emit("scalar", lambda E: E.activation(
            dump[:], src, AF.Square, accum_out=stm[:, 1:2]), inc=self.s_sc)
        P.emit("vector", lambda E: E.reduce_sum(stm[:, 0:1], src, axis=AX.X),
               inc=self.s_vc)
        P.wait("scalar", self.s_vc)
        P.emit("scalar", lambda E: E.activation(
            stm[:, 2:3], stm[:, 0:1], AF.Square, bias=0.0, scale=1.0 / D),
            inc=self.s_sc)
        P.wait("scalar", self.s_sc)
        P.emit("scalar", lambda E: E.activation(
            stm[:, 2:3], stm[:, 2:3], AF.Identity, bias=0.0, scale=-1.0),
            inc=self.s_sc)
        P.wait("scalar", self.s_sc)
        P.emit("scalar", lambda E: E.activation(
            stm[:, 3:4], stm[:, 1:2], AF.Identity, bias=stm[:, 2:3], scale=1.0 / D),
            inc=self.s_sc)
        P.wait("scalar", self.s_sc)
        P.emit("scalar", lambda E: E.activation(
            rsc[:, 0:1], stm[:, 3:4], AF.Ln, bias=self.eps_t[:, 0:1]),
            inc=self.s_sc)
        P.wait("scalar", self.s_sc)
        P.emit("scalar", lambda E: E.activation(
            rsc[:, 0:1], rsc[:, 0:1], AF.Exp, bias=0.0, scale=-0.5),
            inc=self.s_sc)
        P.wait("vector", self.s_sc)
        P.emit("vector", lambda E: E.scalar_tensor_tensor(
            rsc[:, 1:2], stm[:, 0:1], -1.0 / D, rsc[:, 0:1],
            ALU.mult, ALU.mult), inc=self.s_vc)
        P.wait("vector", self.s_vc)
        P.emit("vector", lambda E: E.tensor_scalar(
            dst, src, rsc[:, 0:1], rsc[:, 1:2], ALU.mult, ALU.add),
            inc=self.s_ap)
        self.ln_at[pa] = self.s_ap.n

    def emit_center(self, src, dst, dump, pa, rslot=None):
        P = self.P
        stm = self.stm[:, pa * 4:pa * 4 + 4]
        if self.ln_at[pa] is not None:
            P.wait("vector", self.s_ap, self.ln_at[pa])
        P.emit("vector", lambda E: E.reduce_sum(stm[:, 0:1], src, axis=AX.X),
               inc=self.s_vc)
        P.wait("scalar", self.s_vc)
        P.emit("scalar", lambda E: E.activation(
            self.mneg_bf[:, pa:pa + 1], stm[:, 0:1], AF.Identity,
            bias=0.0, scale=-1.0 / D), inc=self.s_sc)
        P.wait("scalar", self.s_sc)
        P.emit("scalar", lambda E: E.activation(
            dst, src, AF.Identity, bias=self.mneg_bf[:, pa:pa + 1], scale=1.0),
            inc=self.s_ap)
        self.ln_at[pa] = self.s_ap.n
        if rslot is not None:
            if self.krow_at:
                P.wait("scalar", self.s_ax, self.krow_at)
            P.emit("scalar", lambda E: E.activation(
                dump[:], dst, AF.Square, accum_out=stm[:, 1:2]), inc=self.s_sc)
            self.sq_at[pa] = self.s_sc.n
            P.wait("scalar", self.s_sc)
            P.emit("scalar", lambda E: E.activation(
                stm[:, 2:3], stm[:, 1:2], AF.Ln, bias=self.eps_t[:, 0:1],
                scale=1.0 / D), inc=self.s_sc)
            P.wait("scalar", self.s_sc)
            P.emit("scalar", lambda E, rslot=rslot: E.activation(
                self.rstd_bf[:, rslot:rslot + 1], stm[:, 2:3], AF.Exp,
                bias=0.0, scale=-0.5), inc=self.s_sc)

    def emit_transpose(self, src_t, xlnT_t, it, tp_ps):
        P = self.P
        for grp in range(4):
            P.wait("tensor", self.s_te, self.s_te.n - 1)
            for j4 in range(4):
                j = grp * 4 + j4
                P.emit("tensor", lambda E, j=j, grp=grp, j4=j4: E.matmul(
                    tp_ps[:, (grp % 2) * 512 + j4 * 128:(grp % 2) * 512 + (j4 + 1) * 128],
                    src_t[:, j * 128:(j + 1) * 128], self.ident_sb[:],
                    start=True, stop=True),
                    inc=(self.s_tp if j4 == 3 else None))
            dst = (xlnT_t[:].rearrange("p (j n) -> p j n", n=512)
                   [:, grp * 4:(grp + 1) * 4, it * 128:(it + 1) * 128])
            srcv = (tp_ps[:, (grp % 2) * 512:((grp % 2) + 1) * 512]
                    .rearrange("p (j n) -> p j n", n=128))
            if grp % 2 == 0 or not SPLIT_EVAC:
                P.wait("vector", self.s_tp)
                P.emit("vector", lambda E, dst=dst, srcv=srcv: E.tensor_copy(
                    dst, srcv), inc=self.s_te)
            else:
                P.wait("scalar", self.s_tp)
                P.emit("scalar", lambda E, dst=dst, srcv=srcv: E.activation(
                    dst, srcv, AF.Copy, bias=0.0, scale=1.0), inc=self.s_te)

    def phase1(self):
        nc, P, inp = self.nc, self.P, self.inp
        p1 = contextlib.ExitStack()
        with p1:
            xst = [self.sb(p1, f"xst{i}", [128, D], BF16) for i in range(3)]
            dump = self.sb(p1, "dump", [128, D], BF16)
            rxrow = [self.sb(p1, f"rxrow{i}", [1, 512], BF16) for i in range(2)]
            xlnbf = [self.sb(p1, f"xlnbf{i}", [128, D], BF16) for i in range(2)]
            xlnT = self.sb(p1, "xlnT", [128, 16 * 512], BF16)
            wckv_sb = self.sb(p1, "wckv_sb", [128, 16 * KVR], BF16)
            wkr_sb = self.sb(p1, "wkr_sb", [128, 16 * 96], BF16)
            tabk_sb = [self.sb(p1, f"tabk_sb{i}", [128, 512], F32) for i in range(2)]
            sqc = self.sb(p1, "sqc", [128, 512], BF16)
            rmsr = self.sb(p1, "rmsr", [1, 512], F32)
            rstdb = self.sb(p1, "rstdb", [128, 512], F32)

            with (nc.psum_tensor("tp_ps", [128, 1024], F32) as tp_ps,
                  nc.psum_tensor("ckv_ps", [128, 1024], F32) as ckv_ps,
                  nc.psum_tensor("kr_ps", [128, 1024], F32) as kr_ps,
                  nc.psum_tensor("ax_ps", [128, 1024], F32) as ax_ps):

                P.dma("gpsimd", lambda E: E.dma_start(
                    out=wckv_sb[:].rearrange("p (j n) -> p j n", n=KVR),
                    in_=inp["w_ckv"].rearrange("(j p) n -> p j n", p=128)), self.s_ini)
                P.dma("gpsimd", lambda E: E.dma_start(
                    out=wkr_sb[:].rearrange("p (j n) -> p j n", n=96),
                    in_=inp["w_kr_pad"].rearrange("(j p) n -> p j n", p=128)), self.s_ini)
                for e in ENGS:
                    P.wait(e, self.s_ini)
                P.wait("tensor", self.s_vc)
                P.wait("scalar", self.s_vc)

                krm_red_done = 0
                rstdb_done = 0
                cache_done = 0
                rope_done = 0
                tabk_mul_done = 0
                rx_done = [0, 0]
                rope_hist = {}

                def tiles(b):
                    P.wait("sync", self.s_cw, rope_hist.get(b - 2, 0))
                    P.dma("sync", lambda E, b=b: E.dma_start(
                        out=tabk_sb[b % 2][:],
                        in_=inp["tabK"][:, b * 512:(b + 1) * 512]), self.s_tk)
                    for it in range(4):
                        g = b * 4 + it
                        buf = g % 3
                        if g >= 3:
                            P.wait("gpsimd", self.s_ap, self.s_ap.n - 2)
                        P.dma("gpsimd", lambda E, g=g, buf=buf: E.dma_start(
                            out=xst[buf][:], in_=inp["xp"][g * 128:(g + 1) * 128, :]),
                            self.s_x)
                        P.wait("vector", self.s_x)
                        P.wait("vector", self.s_tp, self.s_tp.n - 4)
                        self.emit_center(xst[buf][:], xlnbf[g % 2][:], dump, g % 2,
                                         rslot=it)
                        P.wait("tensor", self.s_ap)
                        self.emit_transpose(xlnbf[g % 2], xlnT, it, tp_ps)

                def mms(b):
                    P.wait("tensor", self.s_sc)
                    if b > 0:
                        P.wait("tensor", self.s_sc, rx_done[(b - 1) % 2])
                        P.wait("tensor", self.s_cw, tabk_mul_done)
                    for it2 in range(4):
                        P.emit("tensor", lambda E, it2=it2: E.matmul(
                            kr_ps[64:65, 512 + it2 * 128:512 + (it2 + 1) * 128],
                            self.rstd_bf[:, it2:it2 + 1], self.ident_sb[:],
                            start=True, stop=True, skip_group_check=True),
                            inc=(self.s_ax if it2 == 3 else None))
                    self.krow_at = self.s_ax.n
                    P.wait("scalar", self.s_ax)
                    P.emit("scalar", lambda E, b=b: E.activation(
                        rxrow[b % 2][:], kr_ps[64:65, 512:1024], AF.Copy,
                        bias=0.0, scale=1.0), inc=self.s_sc)
                    rx_done[b % 2] = self.s_sc.n
                    P.wait("tensor", self.s_te)
                    if b > 0:
                        P.wait("tensor", self.s_cw, cache_done)
                    for ot in range(2):
                        for j in range(16):
                            P.emit("tensor", lambda E, ot=ot, j=j: E.matmul(
                                ckv_ps[:, ot * 512:(ot + 1) * 512],
                                wckv_sb[:, j * KVR + ot * 128:j * KVR + (ot + 1) * 128],
                                xlnT[:, j * 512:(j + 1) * 512],
                                start=(j == 0), stop=(j == 15)),
                                inc=(self.s_mm if j == 15 else None))
                    if b > 0:
                        P.wait("tensor", self.s_cw, rope_done)
                    for j in range(16):
                        P.emit("tensor", lambda E, j=j: E.matmul(
                            kr_ps[0:96, 0:512],
                            wkr_sb[:, j * 96:j * 96 + 96],
                            xlnT[:, j * 512:(j + 1) * 512],
                            start=(j == 0), stop=(j == 15)),
                            inc=(self.s_mm if j == 15 else None))
                    P.wait("scalar", self.s_mm, self.s_mm.n - 1)
                    P.emit("scalar", lambda E: E.activation(
                        sqc[:], ckv_ps[:, 0:512], AF.Square), inc=self.s_sc)
                    P.wait("tensor", self.s_sc)
                    P.wait("tensor", self.s_vc, krm_red_done)
                    P.emit("tensor", lambda E: E.matmul(
                        ax_ps[0:1, 0:512], self.ones_bf[:], sqc[:],
                        start=True, stop=False), inc=self.s_ax)
                    P.wait("scalar", self.s_ax)
                    P.emit("scalar", lambda E: E.activation(
                        sqc[:], ckv_ps[:, 512:1024], AF.Square), inc=self.s_sc)
                    P.wait("tensor", self.s_sc)
                    P.emit("tensor", lambda E: E.matmul(
                        ax_ps[0:1, 0:512], self.ones_bf[:], sqc[:],
                        start=False, stop=True), inc=self.s_ax)
                    P.wait("scalar", self.s_ax)
                    P.emit("scalar", lambda E: E.activation(
                        rmsr[:], ax_ps[0:1, 0:512], AF.Ln,
                        bias=self.eps_t[0:1, 1:2], scale=1.0 / KVR), inc=self.s_sc)
                    P.wait("scalar", self.s_sc)
                    P.emit("scalar", lambda E: E.activation(
                        rmsr[:], rmsr[:], AF.Exp, bias=0.0, scale=-0.5), inc=self.s_sc)
                    P.wait("tensor", self.s_sc, rx_done[b % 2])
                    P.emit("tensor", lambda E, b=b: E.matmul(
                        kr_ps[0:96, 512:1024], self.ones_rbf[0:1, 0:96],
                        rxrow[b % 2][:],
                        start=True, stop=True, skip_group_check=True), inc=self.s_ax)
                    P.wait("tensor", self.s_sc)
                    P.wait("tensor", self.s_vc, rstdb_done)
                    P.emit("tensor", lambda E: E.matmul(
                        ax_ps[:, 512:1024], self.ones_row[:], rmsr[:],
                        start=True, stop=True, skip_group_check=True),
                        inc=self.s_ax)

                def block_end(b):
                    nonlocal krm_red_done, rstdb_done, cache_done, rope_done
                    nonlocal tabk_mul_done
                    P.wait("vector", self.s_tk)
                    P.wait("vector", self.s_ax)
                    P.emit("vector", lambda E, b=b: E.tensor_mul(
                        tabk_sb[b % 2][32:64, :], tabk_sb[b % 2][32:64, :],
                        kr_ps[32:64, 512:1024]),
                        inc=self.s_cw)
                    P.emit("vector", lambda E, b=b: E.tensor_mul(
                        tabk_sb[b % 2][64:96, :], tabk_sb[b % 2][64:96, :],
                        kr_ps[64:96, 512:1024]),
                        inc=self.s_cw)
                    tabk_mul_done = self.s_cw.n
                    P.emit("vector", lambda E, b=b: E.tensor_mul(
                        self.kaug[0][64:96, b * 512:(b + 1) * 512],
                        kr_ps[64:96, 0:512],
                        tabk_sb[b % 2][64:96, :]), inc=self.s_cw)
                    P.emit("vector", lambda E, b=b: E.tensor_mul(
                        self.kaug[1][32:64, b * 512:(b + 1) * 512],
                        kr_ps[32:64, 0:512],
                        tabk_sb[b % 2][32:64, :]), inc=self.s_cw)
                    rope_done = self.s_cw.n
                    rope_hist[b] = rope_done
                    P.emit("vector", lambda E: E.tensor_copy(
                        rstdb[:], ax_ps[:, 512:1024]), inc=self.s_vc)
                    rstdb_done = self.s_vc.n
                    for ot in range(2):
                        P.emit("vector", lambda E, b=b, ot=ot: E.scalar_tensor_tensor(
                            self.ckv_sb[:, ot * P_TOT + b * 512:ot * P_TOT + (b + 1) * 512],
                            ckv_ps[:, ot * 512:(ot + 1) * 512],
                            self.kvg_sb[:, ot:ot + 1], rstdb[:], ALU.mult, ALU.mult),
                            inc=self.s_cw)
                    cache_done = self.s_cw.n
                    P.wait("scalar", self.s_cw)
                    P.emit("scalar", lambda E, b=b: E.activation(
                        sqc[64:96, :], self.kaug[0][64:96, b * 512:(b + 1) * 512],
                        AF.Square), inc=self.s_sc)
                    P.wait("tensor", self.s_sc)
                    P.emit("tensor", lambda E: E.matmul(
                        ax_ps[0:1, 0:512], self.ones_bf[64:96, :], sqc[64:96, :],
                        start=True, stop=True), inc=self.s_ax)
                    P.wait("vector", self.s_ax)
                    P.emit("vector", lambda E: E.reduce_max(
                        self.krm[0:1, 1:2], ax_ps[0:1, 0:512], axis=AX.X),
                        inc=self.s_vc)
                    krm_red_done = self.s_vc.n
                    P.wait("vector", self.s_vc)
                    P.emit("vector", lambda E: E.tensor_max(
                        self.krm[0:1, 0:1], self.krm[0:1, 0:1], self.krm[0:1, 1:2]),
                        inc=self.s_cw)
                    return

                pend = None
                for b in range(NB):
                    tiles(b)
                    if pend is not None:
                        block_end(pend)
                    mms(b)
                    pend = b
                block_end(pend)
                if "p1x" in self.dbg:
                    P.wait("gpsimd", self.s_cw)
                    P.wait("gpsimd", self.s_ap)
                    for nm2, src2 in [("xlnT", xlnT), ("stm", self.stm),
                                      ("rstdc", self.rstd_c), ("rstdb", rstdb)]:
                        P.dma("gpsimd", lambda E, nm2=nm2, src2=src2: E.dma_start(
                            out=self.dbg_t[nm2], in_=src2[:]), self.s_ou)

    def prep(self):
        nc, P, inp = self.nc, self.P, self.inp
        pr = contextlib.ExitStack()
        with pr:
            xst = [self.sb(pr, f"pxst{i}", [128, D], F32) for i in range(2)]
            dump = self.sb(pr, "pdump", [128, D], BF16)
            xlnbf = [self.sb(pr, f"pxlnbf{i}", [128, D], BF16) for i in range(2)]
            xlnT = self.sb(pr, "pxlnT", [128, 16 * 512], BF16)
            wcq = [self.sb(pr, f"wcq{i}", [128, QR], BF16) for i in range(2)]
            sq2 = [self.sb(pr, f"sq2{i}", [128, 512], BF16) for i in range(2)]
            rmsr2 = self.sb(pr, "rmsr2", [1, 512], F32)
            rstdb2 = self.sb(pr, "rstdb2", [128, 512], F32)

            with (nc.psum_tensor("cq_ps", [128, 6 * 512], F32) as cq_ps,
                  nc.psum_tensor("pax_ps", [128, 1024], F32) as pax_ps):
                self.phase_barrier(("sync", "gpsimd", "tensor"),
                                   ("sc", "vc", "ap", "cw", "ax", "te", "tk"))
                P.dma("sync", lambda E: E.dma_start(
                    out=self.tabq_sb[:], in_=inp["tabQ"]), self.s_ini)
                P.dma("gpsimd", lambda E: E.dma_start(
                    out=wcq[0][:], in_=inp["w_cq"][0:128, :]), self.s_wc)
                self.cq_at = []
                for qb in range(4):
                    for it in range(4):
                        g = qb * 4 + it
                        buf = g % 2
                        if g >= 2:
                            P.wait("sync", self.s_ap, self.s_ap.n - 1)
                        P.dma("sync", lambda E, g=g, buf=buf: E.dma_start(
                            out=xst[buf][:],
                            in_=inp["xp"][(56 + g) * 128:(57 + g) * 128, :]), self.s_x)
                        P.wait("vector", self.s_x)
                        P.wait("vector", self.s_tp, self.s_tp.n - 4)
                        self.emit_center(xst[buf][:], xlnbf[g % 2][:], dump, g % 2)
                        P.wait("tensor", self.s_ap)
                        self.emit_transpose(xlnbf[g % 2], xlnT, it, pax_ps)
                    P.wait("tensor", self.s_te)
                    P.wait("tensor", self.s_qe)
                    for j in range(16):
                        if not (qb == 3 and j == 15):
                            jn = (j + 1) % 16
                            if self.cq_at:
                                P.wait("gpsimd", self.s_pq, self.cq_at[-1])
                            P.dma("gpsimd", lambda E, jn=jn: E.dma_start(
                                out=wcq[jn % 2][:],
                                in_=inp["w_cq"][jn * 128:(jn + 1) * 128, :]), self.s_wc)
                        P.wait("tensor", self.s_wc, 16 * (16 * qb + j + 1))
                        for m in range(6):
                            P.emit("tensor", lambda E, j=j, m=m: E.matmul(
                                cq_ps[:, m * 512:(m + 1) * 512],
                                wcq[j % 2][:, m * 128:(m + 1) * 128],
                                xlnT[:, j * 512:(j + 1) * 512],
                                start=(j == 0), stop=(j == 15)),
                                inc=(self.s_pq if m == 5 else None))
                        self.cq_at.append(self.s_pq.n)
                    P.wait("scalar", self.s_pq)
                    ax_at = []
                    for m in range(6):
                        if m >= 2:
                            P.wait("scalar", self.s_ax, ax_at[m - 2])
                        P.emit("scalar", lambda E, m=m: E.activation(
                            sq2[m % 2][:], cq_ps[:, m * 512:(m + 1) * 512], AF.Square),
                            inc=self.s_sc)
                        P.wait("tensor", self.s_sc)
                        P.emit("tensor", lambda E, m=m: E.matmul(
                            pax_ps[0:1, 0:512], self.ones_bf[:], sq2[m % 2][:],
                            start=(m == 0), stop=(m == 5)), inc=self.s_ax)
                        ax_at.append(self.s_ax.n)
                    P.wait("scalar", self.s_ax)
                    P.emit("scalar", lambda E: E.activation(
                        rmsr2[:], pax_ps[0:1, 0:512], AF.Ln,
                        bias=self.eps_t[0:1, 1:2], scale=1.0 / QR), inc=self.s_sc)
                    P.wait("scalar", self.s_sc)
                    P.emit("scalar", lambda E: E.activation(
                        rmsr2[:], rmsr2[:], AF.Exp, bias=0.0, scale=-0.5), inc=self.s_sc)
                    P.wait("tensor", self.s_sc)
                    P.emit("tensor", lambda E: E.matmul(
                        pax_ps[:, 512:1024], self.ones_row[:], rmsr2[:],
                        start=True, stop=True), inc=self.s_ax)
                    P.wait("vector", self.s_ax)
                    P.emit("vector", lambda E: E.tensor_copy(
                        rstdb2[:], pax_ps[:, 512:1024]), inc=self.s_vc)
                    for m in range(6):
                        P.emit("vector", lambda E, m=m, qb=qb: E.scalar_tensor_tensor(
                            self.normq[:, m * QL + qb * 512:m * QL + (qb + 1) * 512],
                            cq_ps[:, m * 512:(m + 1) * 512],
                            self.qg_sb[:, m:m + 1], rstdb2[:], ALU.mult, ALU.mult),
                            inc=self.s_qe)
                if "prx" in self.dbg:
                    P.wait("gpsimd", self.s_qe)
                    P.dma("gpsimd", lambda E: E.dma_start(
                        out=self.dbg_t["xlnT"], in_=xlnT[:]), self.s_ou)
                    P.dma("gpsimd", lambda E: E.dma_start(
                        out=self.dbg_t["rstdb"], in_=rstdb2[:]), self.s_ou)

    def attention(self):
        nc, P, inp = self.nc, self.P, self.inp
        p2 = contextlib.ExitStack()
        with p2:
            qcur = [self.sb(p2, f"qcur{i}", [128, QL], BF16) for i in range(2)]
            pp = [self.sb(p2, f"pp{i}", [128, 2048], BF16) for i in range(2)]
            qsq = self.sb(p2, "qsq", [128, QL], BF16)
            dn_tmp = self.sb(p2, "dn_tmp", [128, QL], F32)
            aostg = self.sb(p2, "aostg", [64, QL], BF16)
            sqb = [self.sb(p2, f"sqb{i}", [128, 512], BF16) for i in range(2)]
            wvec = self.sb(p2, "wvec", [128, 2], BF16)
            wkn_sb = self.sb(p2, "wkn_sb", [128, 2 * 1024], BF16)
            wv_sb = self.sb(p2, "wv_sb", [128, 2 * 1024], BF16)
            wqn_sb = self.sb(p2, "wqn_sb", [128, 6 * 1024], BF16)
            wqr_sb = self.sb(p2, "wqr_sb", [128, 6 * 576], BF16)

            self.phase_barrier(("sync", "gpsimd", "tensor", "vector", "scalar"),
                               ("sc", "vc", "ap", "qe", "ax", "te", "pq", "mm"))
            P.emit("vector", lambda E: E.memset(wvec[:], 0.0))
            P.emit("vector", lambda E: E.memset(qcur[0][96:128, :], 0.0))
            P.emit("vector", lambda E: E.memset(qcur[1][0:32, :], 0.0))
            P.emit("vector", lambda E: E.memset(wvec[0:64, 0:1], float(LAM * LAM)))
            P.emit("vector", lambda E: E.memset(wvec[64:128, 1:2], float(LAM * LAM)),
                   inc=self.s_vc)
            for nm, dst, inner in [("w_kn", wkn_sb, 1024), ("w_v", wv_sb, 1024),
                                   ("w_qn", wqn_sb, 1024), ("w_qr_pad", wqr_sb, 576)]:
                P.dma("gpsimd", lambda E, nm=nm, dst=dst, inner=inner: E.dma_start(
                    out=dst[:].rearrange("p (j n) -> p j n", n=inner),
                    in_=inp[nm].rearrange("(j p) n -> p j n", p=128)), self.s_wc)

            with (nc.psum_tensor("s_ps0", [128, 1024], F32) as s0,
                  nc.psum_tensor("s_ps1", [128, 1024], F32) as s1,
                  nc.psum_tensor("o_ps", [65, 2048], F32) as o_ps):
                self.S = [s0, s1]
                P.wait("tensor", self.s_wc)
                P.wait("tensor", self.s_qe)
                P.wait("tensor", self.s_cw)
                P.wait("tensor", self.s_vc)
                self.selreg = nc.alloc_register(mybir.EngineType.PE, name="chsel")
                self.reg_free = [(None, 0)] * 4
                self.pv_done = {}
                self.eg_hist = []
                self.git_base = 0
                self.mh_qsq_at = []
                self.od_at = []

                mrow = dn_tmp[0:1, :]
                for p in range(H // 2):
                    self.emit_pair(p, qcur, sqb, wvec, wkn_sb, wv_sb,
                                   wqn_sb, wqr_sb)
                    self.emit_mhat(2 * p, 0, p, qcur, qsq, mrow)
                    self.emit_mhat(2 * p + 1, 1, p, qcur, qsq, mrow)
                    if "qcur" in self.dbg and p == 0:
                        P.wait("gpsimd", self.s_mh)
                        P.dma("gpsimd", lambda E: E.dma_start(
                            out=self.dbg_t["q0"], in_=qcur[0][:]), self.s_ou)
                        P.dma("gpsimd", lambda E: E.dma_start(
                            out=self.dbg_t["q1"], in_=qcur[1][:]), self.s_ou)
                    self.emit_head(2 * p, 0, qcur, pp, o_ps, dn_tmp, aostg)
                    self.emit_head(2 * p + 1, 1, qcur, pp, o_ps, dn_tmp, aostg)

        P.wait("vector", self.s_od)
        P.emit("vector", lambda E: E.reciprocal(self.recip[:], self.denoms[:]),
               inc=self.s_rc)

    def phase_barrier(self, engines, sem_names):
        for e in engines:
            for nm in sem_names:
                self.P.wait(e, getattr(self, "s_" + nm))

    def sreg(self, r, p0=0, p1=128, c0=0, w=512):
        base = (r % 2) * 512 + c0
        return self.S[r // 2][p0:p1, base:base + w]

    def use_region(self, r):
        sem, tgt = self.reg_free[r]
        if sem is not None:
            self.P.wait("tensor", sem, tgt)

    def release_region(self, r, sem):
        self.reg_free[r] = (sem, sem.n)

    def emit_pair(self, p, qcur, sqb, wvec, wkn_sb, wv_sb, wqn_sb, wqr_sb):
        P = self.P
        kaug, v_sb, ckv, normq = self.kaug, self.v_sb, self.ckv_sb, self.normq
        sq_hist = {}
        bm_hist = {}

        def emit_bm(bb):
            r = bb % 4
            P.wait("tensor", self.s_sc, sq_hist[bb])
            P.emit("tensor", lambda E, r=r, bb=bb: E.matmul(
                self.sreg(r, 0, 2),
                wvec[:], sqb[bb % 2][:], start=True, stop=True,
                skip_group_check=True), inc=self.s_bn)
            P.wait("vector", self.s_bn)
            P.emit("vector", lambda E, r=r: E.reduce_max(
                self.bm_scr[:], self.sreg(r, 0, 2), axis=AX.X),
                inc=self.s_vc)
            P.wait("vector", self.s_vc)
            P.emit("vector", lambda E, p=p: E.tensor_max(
                self.bm2[:, p:p + 1], self.bm2[:, p:p + 1], self.bm_scr[:]),
                inc=self.s_bn)
            bm_hist[bb] = self.s_bn.n
            self.release_region(r, self.s_bn)

        for blk in range(NB):
            r = blk % 4
            self.use_region(r)
            for ot in range(2):
                P.emit("tensor", lambda E, blk=blk, ot=ot, r=r, p=p: E.matmul(
                    self.sreg(r),
                    wkn_sb[:, ot * 1024 + p * 128:ot * 1024 + (p + 1) * 128],
                    ckv[:, ot * P_TOT + blk * 512:ot * P_TOT + (blk + 1) * 512],
                    start=(ot == 0), stop=(ot == 1), skip_group_check=True),
                    inc=(self.s_ex if ot == 1 else None))
            P.wait("vector", self.s_ex)
            P.emit("vector", lambda E, blk=blk, r=r: E.tensor_scalar(
                kaug[0][0:64, blk * 512:(blk + 1) * 512],
                self.sreg(r, 0, 64), LAM, None, ALU.mult))
            P.emit("vector", lambda E, blk=blk, r=r: E.tensor_scalar(
                kaug[1][64:128, blk * 512:(blk + 1) * 512],
                self.sreg(r, 64, 128), LAM, None, ALU.mult),
                inc=self.s_xv)
            P.wait("scalar", self.s_xv)
            if blk >= 2:
                P.wait("scalar", self.s_bn, bm_hist.get(blk - 2, 0))
            P.emit("scalar", lambda E, r=r, blk=blk: E.activation(
                sqb[blk % 2][:], self.sreg(r), AF.Square), inc=self.s_sc)
            sq_hist[blk] = self.s_sc.n
            if blk >= 1:
                emit_bm(blk - 1)
        emit_bm(NB - 1)

        for gk in range(NB):
            r = gk % 4
            self.use_region(r)
            for sub in range(4):
                kt = gk * 4 + sub
                for ot in range(2):
                    P.emit("tensor", lambda E, kt=kt, sub=sub, ot=ot, r=r, p=p: E.matmul(
                        self.sreg(r, c0=sub * 128, w=128),
                        ckv[:, ot * P_TOT + kt * 128:ot * P_TOT + (kt + 1) * 128],
                        wv_sb[:, ot * 1024 + p * 128:ot * 1024 + (p + 1) * 128],
                        start=(ot == 0), stop=(ot == 1), skip_group_check=True),
                        inc=(self.s_ex if (ot == 1 and sub == 3) else None))
            P.wait("vector", self.s_ex)
            for lane in range(2):
                P.emit("vector", lambda E, gk=gk, r=r, lane=lane: E.tensor_copy(
                    v_sb[:].rearrange("p (k c) -> p k c", c=130)
                    [:, gk * 4:(gk + 1) * 4, lane * 65:lane * 65 + 64],
                    self.sreg(r)
                    .rearrange("p (k c) -> p k c", c=128)
                    [:, :, lane * 64:(lane + 1) * 64]),
                    inc=(self.s_xv if lane == 1 else None))
            self.release_region(r, self.s_xv)

        for qc in range(4):
            r = qc
            self.use_region(r)
            for m in range(6):
                P.emit("tensor", lambda E, qc=qc, m=m, r=r, p=p: E.matmul(
                    self.sreg(r),
                    wqn_sb[:, m * 1024 + p * 128:m * 1024 + (p + 1) * 128],
                    normq[:, m * QL + qc * 512:m * QL + (qc + 1) * 512],
                    start=(m == 0), stop=(m == 5), skip_group_check=True),
                    inc=(self.s_ex if m == 5 else None))
            P.wait("vector", self.s_ex)
            P.emit("vector", lambda E, qc=qc, r=r: E.tensor_scalar(
                qcur[0][0:64, qc * 512:(qc + 1) * 512],
                self.sreg(r, 0, 64), LAM, None, ALU.mult))
            P.emit("vector", lambda E, qc=qc, r=r: E.tensor_scalar(
                qcur[1][64:128, qc * 512:(qc + 1) * 512],
                self.sreg(r, 64, 128), LAM, None, ALU.mult),
                inc=self.s_qe)
            self.release_region(r, self.s_qe)
        for par in range(2):
            hh = 2 * p + par
            if par == 0:
                c0, rl, rh, mw = hh * 32, 64, 96, 96
            else:
                c0, rl, rh, mw = 32 + hh * 32, 32, 64, 64
            for qc in range(4):
                r = qc
                self.use_region(r)
                for m in range(6):
                    P.emit("tensor", lambda E, qc=qc, m=m, r=r, c0=c0, mw=mw: E.matmul(
                        self.sreg(r, 0, mw),
                        wqr_sb[:, m * 576 + c0:m * 576 + c0 + mw],
                        normq[:, m * QL + qc * 512:m * QL + (qc + 1) * 512],
                        start=(m == 0), stop=(m == 5), skip_group_check=True),
                        inc=(self.s_ex if m == 5 else None))
                P.wait("vector", self.s_ex)
                P.emit("vector", lambda E, qc=qc, r=r, par=par, rl=rl, rh=rh: E.tensor_mul(
                    qcur[par][rl:rh, qc * 512:(qc + 1) * 512],
                    self.sreg(r, rl, rh),
                    self.tabq_sb[rl:rh, qc * 512:(qc + 1) * 512]), inc=self.s_qe)
                self.release_region(r, self.s_qe)
        P.wait("gpsimd", self.s_bn)
        P.dma("gpsimd", lambda E, p=p: E.dma_start(
            out=self.bmo[0:1, p:p + 1], in_=self.bm2[1:2, p:p + 1]), self.s_od)

    def emit_mhat(self, h, par, p, qcur, qsq, mrow):
        P = self.P
        rl, rh = (0, 96) if par == 0 else (32, 128)
        arow = 96 if par == 0 else 0
        if par == 0:
            bsrc = self.bm2[0:1, p:p + 1]
            P.wait("vector", self.s_bn)
        else:
            P.wait("vector", self.s_od)
            bsrc = self.bmo[0:1, p:p + 1]
        P.emit("vector", lambda E, bsrc=bsrc: E.tensor_add(
            self.lnb[0:1, 0:1], bsrc, self.krm[0:1, 0:1]), inc=self.s_vc)
        P.wait("vector", self.s_qe)
        if len(self.mh_qsq_at) >= 1:
            P.wait("vector", self.s_mh, self.mh_qsq_at[-1])
        qpieces = [(rl, rh)] if rl == 0 else [(32, 64), (64, 128)]
        for pi, (a0, a1) in enumerate(qpieces):
            P.emit("vector", lambda E, par=par, a0=a0, a1=a1: E.tensor_mul(
                qsq[a0:a1, :], qcur[par][a0:a1, :], qcur[par][a0:a1, :]),
                inc=(self.s_vc if pi == len(qpieces) - 1 else None))
        P.wait("scalar", self.s_vc)
        P.emit("scalar", lambda E: E.activation(
            self.lnb[0:1, 1:2], self.lnb[0:1, 0:1], AF.Ln), inc=self.s_sc)
        P.wait("scalar", self.s_sc)
        P.emit("scalar", lambda E: E.activation(
            self.lnb[0:1, 1:2], self.lnb[0:1, 1:2], AF.Copy, bias=0.0, scale=0.5),
            inc=self.s_sc)
        for qc in range(4):
            self.use_region(qc)
            P.wait("tensor", self.s_vc)
            pieces = [(rl, rh)] if rl == 0 else [(32, 64), (64, 128)]
            for pi, (a0, a1) in enumerate(pieces):
                P.emit("tensor", lambda E, qc=qc, a0=a0, a1=a1, pi=pi,
                       np_=len(pieces): E.matmul(
                    self.sreg(qc, 0, 1),
                    self.ones_bf[a0:a1, :], qsq[a0:a1, qc * 512:(qc + 1) * 512],
                    start=(pi == 0), stop=(pi == np_ - 1), skip_group_check=True),
                    inc=(self.s_mh if pi == len(pieces) - 1 else None))
        self.mh_qsq_at.append(self.s_mh.n)
        P.wait("scalar", self.s_mh)
        P.emit("scalar", lambda E: E.activation(
            mrow[0:1, 0:1024], self.S[0][0:1, 0:1024], AF.Ln), inc=self.s_sc)
        P.emit("scalar", lambda E: E.activation(
            mrow[0:1, 1024:2048], self.S[1][0:1, 0:1024], AF.Ln), inc=self.s_sc)
        P.wait("scalar", self.s_sc)
        P.emit("scalar", lambda E, par=par, arow=arow: E.activation(
            qcur[par][arow:arow + 1, :], mrow[:], AF.Exp,
            bias=self.lnb[0:1, 1:2], scale=0.5), inc=self.s_mh)
        for r in range(4):
            self.release_region(r, self.s_mh)

    def emit_head(self, h, par, qcur, pp, o_ps, dn_tmp, aostg):
        P = self.P
        kaug, v_sb = self.kaug, self.v_sb
        gbase = self.git_base
        P.wait("tensor", self.s_oe)
        for r in range(4):
            P.emit("tensor", lambda E, r=r: E.matmul(
                o_ps[0:65, r * 512:(r + 1) * 512],
                self.zer_bf[:, 0:65], self.zer_bf[:, 0:512],
                start=True, stop=False, skip_group_check=True), inc=self.s_pv)
        P.wait("tensor", self.s_mh)
        P.wait("tensor", self.s_qe)
        P.wait("tensor", self.s_xv)

        class Sub:

            def __init__(self, sems):
                self.ops = []
                self.sems = sems

            def emit(self, fn, inc=None, k=1, count=True):
                if inc is None:
                    self.ops.append(fn)
                    return

                def g(E, fn=fn, sem=inc, k=k):
                    fn(E).then_inc(sem.h, k)
                self.ops.append(g)
                if count:
                    inc.n += k

            def wait(self, sem, val):
                if val > 0:
                    self.ops.append(lambda E, hh=sem.h, v=val: E.wait_ge(hh, v))

        def pps(t, hf=None):
            j = t % 4
            buf = pp[j // 2]
            c0 = (j % 2) * 1024
            if hf is None:
                return buf[:, c0:c0 + 1024]
            return buf[:, c0 + hf * 512:c0 + hf * 512 + 512]

        def emit_exp(t, hf=None):
            P.wait("scalar", self.s_sg)
            gi = gbase + t
            if gi - 4 in self.pv_done:
                P.wait("scalar", self.s_pv, self.pv_done[gi - 4])
            if hf is None:
                P.emit("scalar", lambda E, t=t: E.activation(
                    pps(t), self.S[t % 2][:, 0:1024], AF.Exp), inc=self.s_eg)
            else:
                P.emit("scalar", lambda E, t=t, hf=hf: E.activation(
                    pps(t, hf), self.S[t % 2][:, hf * 512:(hf + 1) * 512],
                    AF.Exp), inc=self.s_eg)
            self.eg_hist.append(self.s_eg.n)

        for sg in range(7):
            bodyA = Sub(self)
            bodyB = Sub(self)
            sinks = [bodyA, bodyB]

            def rec_pv(t):
                for s in sinks:
                    s.wait(self.s_eg, self.eg_hist[-1])
                for hf in range(2):
                    for si, s in enumerate(sinks):
                        cqv = si * 2 + hf
                        s.emit(lambda E, t=t, par=par, cqv=cqv, hf=hf: E.matmul(
                            o_ps[0:65, cqv * 512:(cqv + 1) * 512],
                            v_sb[:, t * 130 + par * 65:t * 130 + par * 65 + 65],
                            pps(t, hf),
                            start=False, stop=False, skip_group_check=True),
                            inc=(self.s_pv if hf == 1 else None), count=(si == 0))
                self.pv_done[gbase + t] = self.s_pv.n

            for i in range(8):
                t = sg * 8 + i
                if len(self.eg_hist) >= 2:
                    for s in sinks:
                        s.wait(self.s_eg, self.eg_hist[-2])
                for hf in range(2):
                    for si, s in enumerate(sinks):
                        cqv = si * 2 + hf
                        s.emit(lambda E, t=t, par=par, cqv=cqv, hf=hf: E.matmul(
                            self.sreg((t % 2) * 2 + hf),
                            kaug[par][:, t * 128:(t + 1) * 128],
                            qcur[par][:, cqv * 512:(cqv + 1) * 512],
                            start=True, stop=True, skip_group_check=True),
                            inc=(self.s_sg if hf == 1 else None), count=(si == 0))
                if i > 0:
                    rec_pv(t - 1)
                emit_exp(t)
            rec_pv(sg * 8 + 7)

            def run_branch(E, sg=sg, A=bodyA.ops, B=bodyB.ops):
                E.reg_load(self.selreg, self.qtab_sb[0:1, sg:sg + 1])
                with E.If_eq(self.selreg, 0):
                    for f in A:
                        f(E)
                with E.Else():
                    for f in B:
                        f(E)
            P.q["tensor"].append(run_branch)

        def iter_info(t):
            i = t - NINT
            cq = 0 if i < 8 else 2
            rr = (i % 8) * 128
            if rr < 512:
                return [(0, cq, 1024 - rr), (1, cq + 1, None)]
            return [(1, cq + 1, 1024 - (rr - 512))]

        def rec_pv_diag(prev):
            t, info, (sem, val) = prev
            P.wait("tensor", sem, val)
            for idx, (hf, cqv, moff) in enumerate(info):
                last = idx == len(info) - 1
                P.emit("tensor", lambda E, t=t, par=par, hf=hf, cqv=cqv: E.matmul(
                    o_ps[0:65, cqv * 512:(cqv + 1) * 512],
                    v_sb[:, t * 130 + par * 65:t * 130 + par * 65 + 65],
                    pps(t, hf),
                    start=False, stop=False, skip_group_check=True),
                    inc=(self.s_pv if last else None))
            self.pv_done[gbase + t] = self.s_pv.n

        prev = None
        for t in range(NINT, NKS):
            info = iter_info(t)
            if len(self.eg_hist) >= 2:
                P.wait("tensor", self.s_eg, self.eg_hist[-2])
            for idx, (hf, cqv, moff) in enumerate(info):
                last = idx == len(info) - 1
                P.emit("tensor", lambda E, t=t, par=par, hf=hf, cqv=cqv: E.matmul(
                    self.sreg((t % 2) * 2 + hf),
                    kaug[par][:, t * 128:(t + 1) * 128],
                    qcur[par][:, cqv * 512:(cqv + 1) * 512],
                    start=True, stop=True, skip_group_check=True),
                    inc=(self.s_sg if last else None))
            if prev is not None:
                rec_pv_diag(prev)
            emit_exp(t, hf=(info[0][0] if len(info) == 1 else None))
            ready = (self.s_eg, self.eg_hist[-1])
            nm = 0
            for (hf, cqv, moff) in info:
                if moff is None:
                    continue
                P.wait("vector", self.s_eg)
                P.emit("vector", lambda E, t=t, hf=hf, moff=moff: E.tensor_mul(
                    pps(t, hf), pps(t, hf),
                    self.stair_sb[:, moff:moff + 512]), inc=self.s_mk)
                nm += 1
            if nm:
                ready = (self.s_mk, self.s_mk.n)
            prev = (t, info, ready)
        rec_pv_diag(prev)
        self.git_base += NKS
        for r in range(4):
            self.release_region(r, self.s_eg)

        P.wait("vector", self.s_pv)
        if len(self.od_at) >= 1:
            P.wait("vector", self.s_od, self.od_at[-1])
        P.emit("vector", lambda E, h=h: E.tensor_copy(
            aostg[:], o_ps[0:64, :]), inc=self.s_oe)
        P.emit("vector", lambda E: E.tensor_copy(
            dn_tmp[64:65, :], o_ps[64:65, :]), inc=self.s_oe)
        P.wait("gpsimd", self.s_oe)
        P.dma("gpsimd", lambda E, h=h: E.dma_start(
            out=self.ao_d[h], in_=aostg[:]), self.s_od)
        P.dma("gpsimd", lambda E, h=h: E.dma_start(
            out=self.denoms[h:h + 1, :], in_=dn_tmp[64:65, :]), self.s_od)
        self.od_at.append(self.s_od.n)

    def proj(self):
        nc, P, inp = self.nc, self.P, self.inp
        pj = contextlib.ExitStack()
        with pj:
            attn = self.sb(pj, "attn_all", [128, 8 * QL], BF16)
            wpj = self.sb(pj, "wproj_sb", [128, 8 * D], BF16)
            xown = [self.sb(pj, f"xown{i}", [128, D], F32) for i in range(2)]
            self.rrow = self.sb(pj, "rrow", [1, QL], F32)
            x2st = [self.sb(pj, f"x2st{i}", [128, D], F32) for i in range(2)]
            with (nc.psum_tensor("pj_ps", [128, 2048], F32) as pj_ps,
                  nc.psum_tensor("bc_ps", [128, 1024], F32) as bc_ps):
                self.phase_barrier(("sync", "gpsimd", "tensor"),
                                   ("eg", "pv", "oe", "od", "mh", "xv", "bn",
                                    "mk", "qe", "rc"))
                P.dma("gpsimd", lambda E: E.dma_start(
                    out=wpj[:].rearrange("p (j n) -> p j n", n=D),
                    in_=inp["w_proj"].rearrange("(j p) n -> p j n", p=128)), self.s_wc)
                P.wait("gpsimd", self.s_od)
                for j in range(8):
                    for sub in range(2):
                        P.dma("gpsimd", lambda E, j=j, sub=sub: E.dma_start(
                            out=attn[sub * 64:(sub + 1) * 64, j * QL:(j + 1) * QL],
                            in_=self.ao_d[2 * j + sub]), self.s_pj)
                P.wait("gpsimd", self.s_rc)
                ma_at = []
                for hh in range(H):
                    P.wait("gpsimd", self.s_pq)
                    P.dma("gpsimd", lambda E, hh=hh: E.dma_start(
                        out=self.rrow[:], in_=self.recip[hh:hh + 1, :]), self.s_pj)
                    P.wait("tensor", self.s_pj)
                    for k4 in range(4):
                        reg = k4 % 2
                        if len(ma_at) >= 2:
                            P.wait("tensor", self.s_ma, ma_at[-2])
                        P.emit("tensor", lambda E, k4=k4, reg=reg: E.matmul(
                            bc_ps[:, reg * 512:(reg + 1) * 512],
                            self.ones_row[:],
                            self.rrow[0:1, k4 * 512:(k4 + 1) * 512],
                            start=True, stop=True, skip_group_check=True),
                            inc=self.s_pq)
                        P.wait("vector", self.s_pq)
                        r0 = (hh % 2) * 64
                        P.emit("vector", lambda E, hh=hh, k4=k4, reg=reg, r0=r0:
                               E.tensor_mul(
                            attn[r0:r0 + 64,
                                 (hh // 2) * QL + k4 * 512:(hh // 2) * QL + (k4 + 1) * 512],
                            attn[r0:r0 + 64,
                                 (hh // 2) * QL + k4 * 512:(hh // 2) * QL + (k4 + 1) * 512],
                            bc_ps[r0:r0 + 64, reg * 512:(reg + 1) * 512]),
                            inc=self.s_ma)
                        ma_at.append(self.s_ma.n)
                if "pn" in self.dbg:
                    P.wait("gpsimd", self.s_ma)
                    P.dma("gpsimd", lambda E: E.dma_start(
                        out=self.dbg_t["pn"], in_=attn[:]), self.s_ou)
                    P.dma("gpsimd", lambda E: E.dma_start(
                        out=self.dbg_t["rcp"], in_=self.recip[:]), self.s_ou)
                P.wait("tensor", self.s_ma)
                P.wait("tensor", self.s_wc)
                adds_at = []
                for tt in range(16):
                    if tt >= 2:
                        P.wait("sync", self.s_vc, adds_at[tt - 2])
                    P.dma("sync", lambda E, tt=tt: E.dma_start(
                        out=xown[tt % 2][:],
                        in_=inp["xp"][(56 + tt) * 128:(57 + tt) * 128, :]), self.s_x)
                    P.wait("tensor", self.s_vc)
                    for j in range(8):
                        for dc in range(4):
                            P.emit("tensor", lambda E, tt=tt, j=j, dc=dc: E.matmul(
                                pj_ps[:, dc * 512:(dc + 1) * 512],
                                attn[:, j * QL + tt * 128:j * QL + (tt + 1) * 128],
                                wpj[:, j * D + dc * 512:j * D + (dc + 1) * 512],
                                start=(j == 0), stop=(j == 7)),
                                inc=(self.s_mm if j == 7 and dc == 3 else None))
                    P.wait("vector", self.s_mm)
                    P.wait("vector", self.s_x)
                    if tt >= 2:
                        P.wait("vector", self.s_x2, 16 * (tt - 1))
                    P.emit("vector", lambda E, tt=tt: E.tensor_add(
                        x2st[tt % 2][:], pj_ps[:], xown[tt % 2][:]), inc=self.s_vc)
                    adds_at.append(self.s_vc.n)
                    P.wait("sync", self.s_vc)
                    P.dma("sync", lambda E, tt=tt: E.dma_start(
                        out=self.x2_d[tt * 128:(tt + 1) * 128, :],
                        in_=x2st[tt % 2][:]), self.s_x2)

    def ffn(self):
        nc, P, inp = self.nc, self.P, self.inp
        ff = contextlib.ExitStack()
        with ff:
            x2t = [self.sb(ff, f"fx2t{i}", [128, D], F32) for i in range(2)]
            dump = self.sb(ff, "fdump", [128, D], BF16)
            xlnbf = [self.sb(ff, f"fxlnbf{i}", [128, D], BF16) for i in range(2)]
            xlnT = self.sb(ff, "fxlnT", [128, 16 * 512], BF16)
            hbuf = self.sb(ff, "hbuf", [128, 64 * 512], BF16)
            wg8 = [self.sb(ff, f"wg8{i}", [128, 2048], U8) for i in range(2)]
            wu8 = [self.sb(ff, f"wu8{i}", [128, 2048], U8) for i in range(2)]
            wgb = [self.sb(ff, f"wgb{i}", [128, 2048], BF16) for i in range(2)]
            wub = [self.sb(ff, f"wub{i}", [128, 2048], BF16) for i in range(2)]
            wd8 = self.sb(ff, "wd8", [128, 8192], U8)
            wdb = [self.sb(ff, f"wdb{i}", [128, 4096], BF16) for i in range(2)]
            sil = self.sb(ff, "sil", [128, 512], F32)
            ytmp = self.sb(ff, "ytmp", [128, 16 * 512], BF16)
            x3st = [self.sb(ff, f"x3st{i}", [128, D], F32) for i in range(2)]

            with (nc.psum_tensor("big_ps", [128, 2048], F32) as big_ps,
                  nc.psum_tensor("g_ps", [128, 512], F32) as g_ps,
                  nc.psum_tensor("u_ps", [128, 512], F32) as u_ps,
                  nc.psum_tensor("y_ps", [128, 512], F32) as y_ps):
                self.phase_barrier(("sync", "gpsimd", "tensor"),
                                    ("x2", "vc", "mm", "ma", "pq", "x"))
                fo_at = []
                fw0 = self.s_fw.n
                fz0 = self.s_fz.n
                te_mark = [0]

                def stage_a(pa):
                    for it in range(4):
                        g = pa * 4 + it
                        buf = g % 2
                        if g >= 2:
                            P.wait("gpsimd", self.s_ap, self.s_ap.n - 1)
                            if fo_at:
                                P.wait("gpsimd", self.s_fo, fo_at[-1])
                        P.dma("gpsimd", lambda E, g=g, buf=buf: E.dma_start(
                            out=x2t[buf][:],
                            in_=self.x2_d[g * 128:(g + 1) * 128, :]), self.s_x)
                        P.wait("scalar", self.s_x)
                        P.wait("vector", self.s_x)
                        P.wait("vector", self.s_tp, self.s_tp.n - 4)
                        self.emit_ln(x2t[buf][:], xlnbf[g % 2][:], dump, g % 2)
                        P.wait("tensor", self.s_ap)
                        if fo_at:
                            P.wait("tensor", self.s_fo, fo_at[-1])
                        self.emit_transpose(xlnbf[g % 2], xlnT, it, big_ps)
                    te_mark[0] = self.s_te.n

                def stage_b(pa):
                    P.wait("tensor", self.s_te)
                    P.dma("sync", lambda E: E.dma_start(
                        out=wg8[0][:], in_=inp["wg_t"][0]), self.s_fw)
                    P.dma("sync", lambda E: E.dma_start(
                        out=wu8[0][:], in_=inp["wu_t"][0]), self.s_fw)
                    for i in range(64):
                        if i + 1 < 64:
                            P.wait("sync", self.s_sc, self.cast_at[-1]
                                   if getattr(self, "cast_at", None) else 0)
                            P.dma("sync", lambda E, i=i: E.dma_start(
                                out=wg8[(i + 1) % 2][:], in_=inp["wg_t"][i + 1]),
                                self.s_fw)
                            P.dma("sync", lambda E, i=i: E.dma_start(
                                out=wu8[(i + 1) % 2][:], in_=inp["wu_t"][i + 1]),
                                self.s_fw)
                        P.wait("scalar", self.s_fw, fw0 + 32 * (64 * pa + i + 1))
                        P.wait("scalar", self.s_fg, self.s_fg.n - 2)
                        P.emit("scalar", lambda E, i=i: E.activation(
                            wgb[i % 2][:], wg8[i % 2][:], AF.Copy, bias=-128.0,
                            scale=1.0))
                        P.emit("scalar", lambda E, i=i: E.activation(
                            wub[i % 2][:], wu8[i % 2][:], AF.Copy, bias=-128.0,
                            scale=1.0), inc=self.s_sc)
                        if not hasattr(self, "cast_at"):
                            self.cast_at = []
                        self.cast_at.append(self.s_sc.n)
                        P.wait("tensor", self.s_sc)
                        P.wait("tensor", self.s_fh)
                        for j in range(16):
                            P.emit("tensor", lambda E, i=i, j=j: E.matmul(
                                g_ps[:], wgb[i % 2][:, j * 128:(j + 1) * 128],
                                xlnT[:, j * 512:(j + 1) * 512],
                                start=(j == 0), stop=(j == 15)),
                                inc=(self.s_fg if j == 15 else None))
                        for j in range(16):
                            P.emit("tensor", lambda E, i=i, j=j: E.matmul(
                                u_ps[:], wub[i % 2][:, j * 128:(j + 1) * 128],
                                xlnT[:, j * 512:(j + 1) * 512],
                                start=(j == 0), stop=(j == 15)),
                                inc=(self.s_fg if j == 15 else None))
                        P.wait("scalar", self.s_fg, self.s_fg.n - 1)
                        P.emit("scalar", lambda E, i=i: E.activation(
                            sil[:], g_ps[:], AF.Silu, bias=0.0,
                            scale=self.gsc_sb[:, i:i + 1]), inc=self.s_sc)
                        P.wait("vector", self.s_sc)
                        P.wait("vector", self.s_fg)
                        P.emit("vector", lambda E, i=i: E.scalar_tensor_tensor(
                            hbuf[:, i * 512:(i + 1) * 512], u_ps[:],
                            self.usc_sb[:, i:i + 1], sil[:], ALU.mult, ALU.mult),
                            inc=self.s_fh)

                def stage_c(pa):
                    P.dma("sync", lambda E: E.dma_start(
                        out=wd8[:], in_=inp["wd_t"][0]), self.s_fz)
                    dcast_at = []
                    for ht in range(16):
                        P.wait("scalar", self.s_fz, fz0 + 16 * (16 * pa + ht + 1))
                        for half in range(2):
                            P.wait("scalar", self.s_fy, self.s_fy.n - 1)
                            P.emit("scalar", lambda E, ht=ht, half=half: E.activation(
                                wdb[half][:], wd8[:, half * 4096:(half + 1) * 4096],
                                AF.Copy, bias=-128.0, scale=1.0), inc=self.s_sc)
                            dcast_at.append(self.s_sc.n)
                        if ht + 1 < 16:
                            P.wait("sync", self.s_sc, dcast_at[-1])
                            P.dma("sync", lambda E, ht=ht: E.dma_start(
                                out=wd8[:], in_=inp["wd_t"][ht + 1]), self.s_fz)
                        P.wait("tensor", self.s_sc, dcast_at[-2])
                        P.wait("tensor", self.s_fh)
                        P.wait("tensor", self.s_fy)
                        for j in range(64):
                            if j == 32:
                                P.wait("tensor", self.s_sc, dcast_at[-1])
                            P.emit("tensor", lambda E, ht=ht, j=j: E.matmul(
                                y_ps[:], wdb[j // 32][:, (j % 32) * 128:(j % 32 + 1) * 128],
                                hbuf[:, j * 512:(j + 1) * 512],
                                start=(j == 0), stop=(j == 63)),
                                inc=(self.s_fy if j == 63 else None))
                        P.wait("vector", self.s_fy)
                        P.emit("vector", lambda E, ht=ht: E.tensor_scalar(
                            ytmp[:, ht * 512:(ht + 1) * 512], y_ps[:],
                            self.dsc_sb[:, ht:ht + 1], None, ALU.mult), inc=self.s_fy)
                    if "ffx" in self.dbg and pa == 0:
                        P.wait("gpsimd", self.s_fy)
                        P.wait("gpsimd", self.s_te)
                        P.dma("gpsimd", lambda E: E.dma_start(
                            out=self.dbg_t["f1"], in_=xlnT[:]), self.s_ou)
                        P.dma("gpsimd", lambda E: E.dma_start(
                            out=self.dbg_t["f2"], in_=hbuf[:]), self.s_ou)
                        P.dma("gpsimd", lambda E: E.dma_start(
                            out=self.dbg_t["f3"], in_=ytmp[:]), self.s_ou)

                def stage_d(pa):
                    for it in range(4):
                        g = pa * 4 + it
                        P.wait("tensor", self.s_fy)
                        P.wait("tensor", self.s_te, te_mark[0])
                        if fo_at:
                            P.wait("tensor", self.s_fo, fo_at[-1])
                        for ht in range(16):
                            P.emit("tensor", lambda E, it=it, ht=ht: E.matmul(
                                big_ps[:, ht * 128:(ht + 1) * 128],
                                ytmp[:, ht * 512 + it * 128:ht * 512 + (it + 1) * 128],
                                self.ident_sb[:], start=True, stop=True),
                                inc=self.s_tp)
                        P.wait("gpsimd", self.s_ap)
                        P.wait("gpsimd", self.s_fo)
                        P.dma("gpsimd", lambda E, g=g: E.dma_start(
                            out=x2t[g % 2][:],
                            in_=self.x2_d[g * 128:(g + 1) * 128, :]), self.s_x)
                        P.wait("vector", self.s_tp)
                        P.wait("vector", self.s_x)
                        if len(fo_at) >= 2:
                            P.wait("vector", self.s_ou, 16 * (len(fo_at) - 1))
                        P.emit("vector", lambda E, g=g: E.tensor_add(
                            x3st[g % 2][:], big_ps[:], x2t[g % 2][:]), inc=self.s_fo)
                        fo_at.append(self.s_fo.n)
                        P.wait("gpsimd", self.s_fo)
                        P.dma("gpsimd", lambda E, g=g: E.dma_start(
                            out=self.out[g * 128:(g + 1) * 128, :],
                            in_=x3st[g % 2][:]), self.s_ou)

                stage_a(0)
                for pa in range(4):
                    stage_b(pa)
                    stage_c(pa)
                    if pa + 1 < 4:
                        stage_a(pa + 1)
                    stage_d(pa)


def kernel(**inputs):
    mk = MK(DBG)
    nc = mk.build()
    maps = [core_inputs(c, inputs) for c in range(8)]
    res = run_bass_kernel_spmd(nc, maps, core_ids=list(range(8)))
    outp = np.zeros((2, 8192, D), np.float32)
    for c in range(8):
        o = res.results[c]["out"]
        outp[0, 1024 * c:1024 * (c + 1)] = o[0:1024]
        outp[1, 1024 * (7 - c):1024 * (8 - c)] = o[1024:2048]
    return outp


# version 09
# speedup vs baseline: 1.0986x; 1.0986x over previous
import contextlib
import numpy as np
import ml_dtypes
import concourse.bass as bass
import concourse.mybir as mybir
from concourse.bass_utils import run_bass_kernel_spmd
from concourse.expressions_rust import RuntimeValue

BFNP = ml_dtypes.bfloat16
F32 = mybir.dt.float32
BF16 = mybir.dt.bfloat16
U8 = mybir.dt.uint8
U32 = mybir.dt.uint32
AF = mybir.ActivationFunctionType
ALU = mybir.AluOpType
AX = mybir.AxisListType

D = 2048
H = 16
QR, KVR = 768, 256
P_TOT = 9216
NKS = 72
NINT = 56
QL = 2048
NB = 18
EPS_LN = 1e-5
EPS_RMS = 1e-6
LAM = float(96.0 ** -0.25)
SKR = float((16.0 / np.sqrt(96.0)) ** 0.5)
SQRP = SKR / 16.0
DBG = ()
SPLIT_EVAC = False

ENGS = ("tensor", "vector", "scalar", "sync", "gpsimd")


class Sem:
    __slots__ = ("h", "n")

    def __init__(self, h):
        self.h = h
        self.n = 0


class Prog:
    def __init__(self):
        self.q = {e: [] for e in ENGS}

    def emit(self, e, fn, inc=None, k=1):
        if inc is None:
            self.q[e].append(fn)
            return

        def g(E, fn=fn, sem=inc, k=k):
            fn(E).then_inc(sem.h, k)
        self.q[e].append(g)
        inc.n += k

    def dma(self, e, fn, sem):
        self.emit(e, fn, inc=sem, k=16)

    def wait(self, e, sem, val=None):
        v = sem.n if val is None else val
        if v <= 0:
            return
        self.q[e].append(lambda E, h=sem.h, v=v: E.wait_ge(h, v))

    def replay(self, block):
        for e in ENGS:
            ops = self.q[e]
            getattr(block, e)(lambda E, ops=ops: [f(E) for f in ops])


def _rope_tab(positions):
    f = np.arange(32) // 2
    invf = 10000.0 ** (-(f / 16.0))
    return np.cos(positions[None, :].astype(np.float64) * invf[:, None])


def core_inputs(c, a):
    QA0 = 1024 * c
    QB0 = 1024 * (7 - c)
    PA = QA0 + 1024
    PB = QB0 + 1024
    x = np.asarray(a["L0_attn_x"])
    d = {}
    d["xp"] = np.ascontiguousarray(
        np.concatenate([x[0, :QA0], x[1, :QB0], x[0, QA0:PA], x[1, QB0:PB]], axis=0))
    pos_k = np.concatenate([np.arange(QA0), np.arange(QB0),
                            np.arange(QA0, PA), np.arange(QB0, PB)])
    pos_q = np.concatenate([np.arange(QA0, PA), np.arange(QB0, PB)])
    ck = _rope_tab(pos_k)
    cq = _rope_tab(pos_q)
    tabK = np.zeros((128, P_TOT), np.float32)
    tabK[32:64] = tabK[64:96] = (np.sqrt(2.0) * (SKR / 16.0) * ck).astype(np.float32)
    tabQ = np.zeros((128, QL), np.float32)
    tabQ[32:64] = tabQ[64:96] = (np.sqrt(2.0) * SQRP * cq).astype(np.float32)
    d["tabK"] = tabK
    d["tabQ"] = tabQ
    qt = np.zeros((1, 128), np.uint32)
    for g in range(NINT // 8):
        qt[0, g] = 0 if g < c else 1
    d["qtab"] = qt
    d["w_cq"] = np.asarray(a["L0_attn_w_cq"])
    d["w_qn"] = np.asarray(a["L0_attn_w_q_nope"])
    d["w_qr_pad"] = np.ascontiguousarray(np.concatenate(
        [np.zeros((QR, 64), np.float32), np.asarray(a["L0_attn_w_q_rope"])], axis=1))
    d["q_g"] = np.ascontiguousarray(np.asarray(a["L0_attn_q_g"]).reshape(6, 128).T)
    d["w_ckv"] = np.asarray(a["L0_attn_w_ckv"])
    d["w_kn"] = np.asarray(a["L0_attn_w_k_nope"])
    d["w_v"] = np.asarray(a["L0_attn_w_v"])
    d["kv_g"] = np.ascontiguousarray(np.asarray(a["L0_attn_kv_g"]).reshape(2, 128).T)
    wkr = np.asarray(a["L0_attn_w_k_rope"])
    d["w_kr_pad"] = np.ascontiguousarray(np.concatenate(
        [np.zeros((D, 32), np.float32), wkr, wkr], axis=1))
    d["w_proj"] = np.asarray(a["L0_attn_w_proj"])
    wgT = np.asarray(a["L0_ffn_gate_w"]).astype(np.uint8).T
    wuT = np.asarray(a["L0_ffn_up_w"]).astype(np.uint8).T
    wdT = np.asarray(a["L0_ffn_down_w"]).astype(np.uint8).T
    d["wg_t"] = np.ascontiguousarray(
        wgT.reshape(16, 128, 64, 128).transpose(2, 1, 0, 3).reshape(64, 128, 2048))
    d["wu_t"] = np.ascontiguousarray(
        wuT.reshape(16, 128, 64, 128).transpose(2, 1, 0, 3).reshape(64, 128, 2048))
    d["wd_t"] = np.ascontiguousarray(
        wdT.reshape(64, 128, 16, 128).transpose(2, 1, 0, 3).reshape(16, 128, 8192))
    d["gsc"] = np.ascontiguousarray(np.asarray(a["L0_ffn_gate_scale"]).reshape(64, 128).T)
    d["usc"] = np.ascontiguousarray(np.asarray(a["L0_ffn_up_scale"]).reshape(64, 128).T)
    d["dsc"] = np.ascontiguousarray(np.asarray(a["L0_ffn_down_scale"]).reshape(16, 128).T)
    stair = np.zeros((128, 2048), np.float32)
    for i in range(128):
        stair[i, 1024 + i:] = 1.0
    d["stair"] = stair.astype(BFNP)
    d["ident"] = np.eye(128, dtype=BFNP)
    return d


class MK:
    def __init__(self, dbg=()):
        self.dbg = dbg
        self.nc = bass.Bass("TRN2", target_bir_lowering=False)
        self.P = Prog()

    def sem(self, name):
        s = Sem(self.st.enter_context(self.nc.semaphore(name)))
        setattr(self, "s_" + name, s)
        return s

    def sb(self, stack, name, shape, dt):
        return stack.enter_context(self.nc.sbuf_tensor(name, shape, dt))

    def build(self):
        nc, P = self.nc, self.P
        dbg = self.dbg
        self.inp = {}
        for name, shape, dt in [
            ("xp", [P_TOT, D], F32), ("tabK", [128, P_TOT], F32),
            ("tabQ", [128, QL], F32), ("qtab", [1, 128], U32),
            ("w_cq", [D, QR], F32), ("w_qn", [QR, 1024], F32),
            ("w_qr_pad", [QR, 576], F32), ("q_g", [128, 6], F32),
            ("w_ckv", [D, KVR], F32), ("w_kn", [KVR, 1024], F32),
            ("w_v", [KVR, 1024], F32), ("kv_g", [128, 2], F32),
            ("w_kr_pad", [D, 96], F32), ("w_proj", [1024, D], F32),
            ("wg_t", [64, 128, 2048], U8), ("wu_t", [64, 128, 2048], U8),
            ("wd_t", [16, 128, 8192], U8),
            ("gsc", [128, 64], F32), ("usc", [128, 64], F32), ("dsc", [128, 16], F32),
            ("stair", [128, 2048], BF16), ("ident", [128, 128], BF16),
        ]:
            self.inp[name] = nc.dram_tensor(name, shape, dt, kind="ExternalInput").ap()
        inp = self.inp
        self.out = nc.dram_tensor("out", [QL, D], F32, kind="ExternalOutput").ap()
        self.x2_d = nc.dram_tensor(
            "x2_scr", [QL, D], F32,
            kind="ExternalOutput" if "x2" in dbg else "Internal").ap()
        self.ao_d = nc.dram_tensor(
            "ao_scr", [H, 64, QL], BF16,
            kind="ExternalOutput" if "ao" in dbg else "Internal").ap()
        self.dbg_t = {}
        for key, nm, shape, dt, want in [
            ("ckv", "dbg_ckv", [128, 2 * P_TOT], BF16, "ckv"),
            ("k0", "dbg_k0", [128, P_TOT], BF16, "kaug"),
            ("k1", "dbg_k1", [128, P_TOT], BF16, "kaug"),
            ("vv", "dbg_v", [128, NKS * 130], BF16, "kaug"),
            ("normq", "dbg_normq", [128, 6 * QL], BF16, "normq"),
            ("q0", "dbg_q0", [128, QL], BF16, "qcur"),
            ("q1", "dbg_q1", [128, QL], BF16, "qcur"),
            ("den", "dbg_den", [16, QL], F32, "den"),
            ("xlnT", "dbg_xlnT", [128, 16 * 512], BF16, ("p1x", "prx")),
            ("stm", "dbg_stm", [128, 8], F32, "p1x"),
            ("rstdc", "dbg_rstdc", [128, 4], F32, "p1x"),
            ("rstdb", "dbg_rstdb", [128, 512], F32, ("p1x", "prx")),
            ("bm", "dbg_bm", [2, 16], F32, "bm"),
            ("pn", "dbg_pn", [128, 8 * QL], BF16, "pn"),
            ("f1", "dbg_f1", [128, 16 * 512], BF16, "ffx"),
            ("f2", "dbg_f2", [128, 64 * 512], BF16, "ffx"),
            ("f3", "dbg_f3", [128, 16 * 512], BF16, "ffx"),
            ("rcp", "dbg_rcp", [16, QL], F32, "pn"),
        ]:
            wl = want if isinstance(want, tuple) else (want,)
            if any(w in dbg for w in wl):
                self.dbg_t[key] = nc.dram_tensor(nm, shape, dt, kind="ExternalOutput").ap()

        self.ln_at = [None, None]
        self.sq_at = [None, None]
        self.krow_at = 0
        self.st = contextlib.ExitStack()
        with self.st:
            st = self.st
            for nm in ["ini", "x", "tk", "sc", "vc", "ap", "tp", "te", "mm", "ax",
                       "cw", "wc", "qe", "ex", "xv", "bn", "sg", "eg", "mk", "pv",
                       "oe", "od", "mh", "pj", "pq", "fw", "fg", "fh", "fy", "fo",
                       "ou", "x2", "rc", "ma", "fz", "fd"]:
                self.sem(nm)

            self.ones_bf = self.sb(st, "ones_bf", [128, 1], BF16)
            self.ones_row = self.sb(st, "ones_row", [1, 128], F32)
            self.eps_t = self.sb(st, "eps_t", [128, 2], F32)
            self.stair_sb = self.sb(st, "stair_sb", [128, 2048], BF16)
            self.ident_sb = self.sb(st, "ident_sb", [128, 128], BF16)
            self.qtab_sb = self.sb(st, "qtab_sb", [1, 128], U32)
            self.zer_bf = self.sb(st, "zer_bf", [128, 512], BF16)
            self.stm = self.sb(st, "stm", [128, 8], F32)
            self.rstd_c = self.sb(st, "rstd_c", [128, 4], F32)
            self.mneg_bf = self.sb(st, "mneg_bf", [128, 2], F32)
            self.rstd_bf = self.sb(st, "rstd_bf", [128, 4], BF16)
            self.ones_rbf = self.sb(st, "ones_rbf", [1, 128], BF16)
            self.bm2 = self.sb(st, "bm2", [2, 16], F32)
            self.bmo = self.sb(st, "bmo", [1, 16], F32)
            self.bm_scr = self.sb(st, "bm_scr", [2, 1], F32)
            self.krm = self.sb(st, "krm", [1, 2], F32)
            self.lnb = self.sb(st, "lnb", [1, 2], F32)
            self.gsc_sb = self.sb(st, "gsc_sb", [128, 64], F32)
            self.usc_sb = self.sb(st, "usc_sb", [128, 64], F32)
            self.dsc_sb = self.sb(st, "dsc_sb", [128, 16], F32)
            self.kvg_sb = self.sb(st, "kvg_sb", [128, 2], F32)
            self.qg_sb = self.sb(st, "qg_sb", [128, 6], F32)

            for nm, dst in [("stair", self.stair_sb), ("ident", self.ident_sb),
                            ("qtab", self.qtab_sb), ("gsc", self.gsc_sb),
                            ("usc", self.usc_sb), ("dsc", self.dsc_sb),
                            ("kv_g", self.kvg_sb), ("q_g", self.qg_sb)]:
                P.dma("sync", lambda E, nm=nm, dst=dst: E.dma_start(
                    out=dst[:], in_=inp[nm]), self.s_ini)
            P.emit("vector", lambda E: E.memset(self.ones_bf[:], 1.0))
            P.emit("vector", lambda E: E.memset(self.ones_row[:], 1.0))
            P.emit("vector", lambda E: E.memset(self.ones_rbf[:], 1.0))
            P.emit("vector", lambda E: E.memset(self.eps_t[:, 0:1], EPS_LN))
            P.emit("vector", lambda E: E.memset(self.eps_t[:, 1:2], EPS_RMS))
            P.emit("vector", lambda E: E.memset(self.zer_bf[:], 0.0))
            P.emit("vector", lambda E: E.memset(self.bm2[:], 0.0))
            P.emit("vector", lambda E: E.memset(self.krm[:], 0.0), inc=self.s_vc)

            dn_sc = contextlib.ExitStack()
            with dn_sc:
              self.denoms = self.sb(dn_sc, "denoms", [16, QL], F32)
              self.recip = self.sb(dn_sc, "recip", [16, QL], F32)
              ca = contextlib.ExitStack()
              with ca:
                  self.ckv_sb = self.sb(ca, "ckv_sb", [128, 2 * P_TOT], BF16)
                  self.kaug = [self.sb(ca, f"kaug{i}", [128, P_TOT], BF16) for i in range(2)]
                  self.v_sb = self.sb(ca, "v_sb", [128, NKS * 130], BF16)
                  self.normq = self.sb(ca, "normq", [128, 6 * QL], BF16)
                  self.tabq_sb = self.sb(ca, "tabq_sb", [128, QL], F32)

                  P.emit("vector", lambda E: E.memset(self.v_sb[:], 1.0))
                  P.emit("vector", lambda E: E.memset(self.kaug[0][96:128, :], 0.0))
                  P.emit("vector", lambda E: E.memset(self.kaug[0][96:97, :], -1.0))
                  P.emit("vector", lambda E: E.memset(self.kaug[1][0:32, :], 0.0))
                  P.emit("vector", lambda E: E.memset(self.kaug[1][0:1, :], -1.0),
                         inc=self.s_vc)

                  self.phase1()
                  if "p1only" not in dbg:
                      self.prep()
                  if "p1only" not in dbg and "noattn" not in dbg:
                      self.attention()

                  if "ckv" in dbg:
                      P.wait("gpsimd", self.s_cw)
                      P.dma("gpsimd", lambda E: E.dma_start(
                          out=self.dbg_t["ckv"], in_=self.ckv_sb[:]), self.s_ou)
                  if "kaug" in dbg:
                      P.wait("gpsimd", self.s_xv)
                      P.dma("gpsimd", lambda E: E.dma_start(
                          out=self.dbg_t["k0"], in_=self.kaug[0][:]), self.s_ou)
                      P.dma("gpsimd", lambda E: E.dma_start(
                          out=self.dbg_t["k1"], in_=self.kaug[1][:]), self.s_ou)
                      P.dma("gpsimd", lambda E: E.dma_start(
                          out=self.dbg_t["vv"], in_=self.v_sb[:]), self.s_ou)
                  if "normq" in dbg and "p1only" not in dbg:
                      P.wait("gpsimd", self.s_qe)
                      P.dma("gpsimd", lambda E: E.dma_start(
                          out=self.dbg_t["normq"], in_=self.normq[:]), self.s_ou)
                  if "bm" in dbg:
                      P.wait("gpsimd", self.s_bn)
                      P.dma("gpsimd", lambda E: E.dma_start(
                          out=self.dbg_t["bm"], in_=self.bm2[:]), self.s_ou)

              if "den" in dbg:
                  P.wait("gpsimd", self.s_od)
                  P.dma("gpsimd", lambda E: E.dma_start(
                      out=self.dbg_t["den"], in_=self.denoms[:]), self.s_ou)
              if "noftail" not in dbg:
                  self.proj()
            if "noftail" not in dbg:
                self.ffn()

            P.wait("gpsimd", self.s_ou)
            P.wait("sync", self.s_x2)

            with nc.Block() as block:
                P.replay(block)
        return nc

    def emit_ln(self, src, dst, dump, pa=0):
        P = self.P
        stm = self.stm[:, pa * 4:pa * 4 + 4]
        rsc = self.rstd_c[:, pa * 2:pa * 2 + 2]
        if self.ln_at[pa] is not None:
            P.wait("scalar", self.s_ap, self.ln_at[pa])
            P.wait("vector", self.s_ap, self.ln_at[pa])
        P.emit("scalar", lambda E: E.activation(
            dump[:], src, AF.Square, accum_out=stm[:, 1:2]), inc=self.s_sc)
        P.emit("vector", lambda E: E.reduce_sum(stm[:, 0:1], src, axis=AX.X),
               inc=self.s_vc)
        P.wait("scalar", self.s_vc)
        P.emit("scalar", lambda E: E.activation(
            stm[:, 2:3], stm[:, 0:1], AF.Square, bias=0.0, scale=1.0 / D),
            inc=self.s_sc)
        P.wait("scalar", self.s_sc)
        P.emit("scalar", lambda E: E.activation(
            stm[:, 2:3], stm[:, 2:3], AF.Identity, bias=0.0, scale=-1.0),
            inc=self.s_sc)
        P.wait("scalar", self.s_sc)
        P.emit("scalar", lambda E: E.activation(
            stm[:, 3:4], stm[:, 1:2], AF.Identity, bias=stm[:, 2:3], scale=1.0 / D),
            inc=self.s_sc)
        P.wait("scalar", self.s_sc)
        P.emit("scalar", lambda E: E.activation(
            rsc[:, 0:1], stm[:, 3:4], AF.Ln, bias=self.eps_t[:, 0:1]),
            inc=self.s_sc)
        P.wait("scalar", self.s_sc)
        P.emit("scalar", lambda E: E.activation(
            rsc[:, 0:1], rsc[:, 0:1], AF.Exp, bias=0.0, scale=-0.5),
            inc=self.s_sc)
        P.wait("vector", self.s_sc)
        P.emit("vector", lambda E: E.scalar_tensor_tensor(
            rsc[:, 1:2], stm[:, 0:1], -1.0 / D, rsc[:, 0:1],
            ALU.mult, ALU.mult), inc=self.s_vc)
        P.wait("vector", self.s_vc)
        P.emit("vector", lambda E: E.tensor_scalar(
            dst, src, rsc[:, 0:1], rsc[:, 1:2], ALU.mult, ALU.add),
            inc=self.s_ap)
        self.ln_at[pa] = self.s_ap.n

    def emit_center(self, src, dst, dump, pa, rslot=None):
        P = self.P
        stm = self.stm[:, pa * 4:pa * 4 + 4]
        if self.ln_at[pa] is not None:
            P.wait("vector", self.s_ap, self.ln_at[pa])
        P.emit("vector", lambda E: E.reduce_sum(stm[:, 0:1], src, axis=AX.X),
               inc=self.s_vc)
        P.wait("scalar", self.s_vc)
        P.emit("scalar", lambda E: E.activation(
            self.mneg_bf[:, pa:pa + 1], stm[:, 0:1], AF.Identity,
            bias=0.0, scale=-1.0 / D), inc=self.s_sc)
        P.wait("scalar", self.s_sc)
        P.emit("scalar", lambda E: E.activation(
            dst, src, AF.Identity, bias=self.mneg_bf[:, pa:pa + 1], scale=1.0),
            inc=self.s_ap)
        self.ln_at[pa] = self.s_ap.n
        if rslot is not None:
            if self.krow_at:
                P.wait("scalar", self.s_ax, self.krow_at)
            P.emit("scalar", lambda E: E.activation(
                dump[:], dst, AF.Square, accum_out=stm[:, 1:2]), inc=self.s_sc)
            self.sq_at[pa] = self.s_sc.n
            P.wait("scalar", self.s_sc)
            P.emit("scalar", lambda E: E.activation(
                stm[:, 2:3], stm[:, 1:2], AF.Ln, bias=self.eps_t[:, 0:1],
                scale=1.0 / D), inc=self.s_sc)
            P.wait("scalar", self.s_sc)
            P.emit("scalar", lambda E, rslot=rslot: E.activation(
                self.rstd_bf[:, rslot:rslot + 1], stm[:, 2:3], AF.Exp,
                bias=0.0, scale=-0.5), inc=self.s_sc)

    def emit_transpose(self, src_t, xlnT_t, it, tp_ps):
        P = self.P
        for grp in range(4):
            P.wait("tensor", self.s_te, self.s_te.n - 1)
            for j4 in range(4):
                j = grp * 4 + j4
                P.emit("tensor", lambda E, j=j, grp=grp, j4=j4: E.matmul(
                    tp_ps[:, (grp % 2) * 512 + j4 * 128:(grp % 2) * 512 + (j4 + 1) * 128],
                    src_t[:, j * 128:(j + 1) * 128], self.ident_sb[:],
                    start=True, stop=True),
                    inc=(self.s_tp if j4 == 3 else None))
            dst = (xlnT_t[:].rearrange("p (j n) -> p j n", n=512)
                   [:, grp * 4:(grp + 1) * 4, it * 128:(it + 1) * 128])
            srcv = (tp_ps[:, (grp % 2) * 512:((grp % 2) + 1) * 512]
                    .rearrange("p (j n) -> p j n", n=128))
            if grp % 2 == 0 or not SPLIT_EVAC:
                P.wait("vector", self.s_tp)
                P.emit("vector", lambda E, dst=dst, srcv=srcv: E.tensor_copy(
                    dst, srcv), inc=self.s_te)
            else:
                P.wait("scalar", self.s_tp)
                P.emit("scalar", lambda E, dst=dst, srcv=srcv: E.activation(
                    dst, srcv, AF.Copy, bias=0.0, scale=1.0), inc=self.s_te)

    def phase1(self):
        nc, P, inp = self.nc, self.P, self.inp
        p1 = contextlib.ExitStack()
        with p1:
            xst = [self.sb(p1, f"xst{i}", [128, D], BF16) for i in range(3)]
            dump = self.sb(p1, "dump", [128, D], BF16)
            rxrow = [self.sb(p1, f"rxrow{i}", [1, 512], BF16) for i in range(2)]
            xlnbf = [self.sb(p1, f"xlnbf{i}", [128, D], BF16) for i in range(2)]
            xlnT = self.sb(p1, "xlnT", [128, 16 * 512], BF16)
            wckv_sb = self.sb(p1, "wckv_sb", [128, 16 * KVR], BF16)
            wkr_sb = self.sb(p1, "wkr_sb", [128, 16 * 96], BF16)
            tabk_sb = [self.sb(p1, f"tabk_sb{i}", [128, 512], F32) for i in range(2)]
            sqc = self.sb(p1, "sqc", [128, 512], BF16)
            rmsr = self.sb(p1, "rmsr", [1, 512], F32)
            rstdb = self.sb(p1, "rstdb", [128, 512], F32)

            with (nc.psum_tensor("tp_ps", [128, 1024], F32) as tp_ps,
                  nc.psum_tensor("ckv_ps", [128, 1024], F32) as ckv_ps,
                  nc.psum_tensor("kr_ps", [128, 1024], F32) as kr_ps,
                  nc.psum_tensor("ax_ps", [128, 1024], F32) as ax_ps):

                P.dma("gpsimd", lambda E: E.dma_start(
                    out=wckv_sb[:].rearrange("p (j n) -> p j n", n=KVR),
                    in_=inp["w_ckv"].rearrange("(j p) n -> p j n", p=128)), self.s_ini)
                P.dma("gpsimd", lambda E: E.dma_start(
                    out=wkr_sb[:].rearrange("p (j n) -> p j n", n=96),
                    in_=inp["w_kr_pad"].rearrange("(j p) n -> p j n", p=128)), self.s_ini)
                for e in ENGS:
                    P.wait(e, self.s_ini)
                P.wait("tensor", self.s_vc)
                P.wait("scalar", self.s_vc)

                krm_red_done = 0
                rstdb_done = 0
                cache_done = 0
                rope_done = 0
                tabk_mul_done = 0
                rx_done = [0, 0]
                rope_hist = {}

                def tiles(b):
                    P.wait("sync", self.s_cw, rope_hist.get(b - 2, 0))
                    P.dma("sync", lambda E, b=b: E.dma_start(
                        out=tabk_sb[b % 2][:],
                        in_=inp["tabK"][:, b * 512:(b + 1) * 512]), self.s_tk)
                    for it in range(4):
                        g = b * 4 + it
                        buf = g % 3
                        if g >= 3:
                            P.wait("gpsimd", self.s_ap, self.s_ap.n - 2)
                        P.dma("gpsimd", lambda E, g=g, buf=buf: E.dma_start(
                            out=xst[buf][:], in_=inp["xp"][g * 128:(g + 1) * 128, :]),
                            self.s_x)
                        P.wait("vector", self.s_x)
                        P.wait("vector", self.s_tp, self.s_tp.n - 4)
                        self.emit_center(xst[buf][:], xlnbf[g % 2][:], dump, g % 2,
                                         rslot=it)
                        P.wait("tensor", self.s_ap)
                        self.emit_transpose(xlnbf[g % 2], xlnT, it, tp_ps)

                def mms(b):
                    P.wait("tensor", self.s_sc)
                    if b > 0:
                        P.wait("tensor", self.s_sc, rx_done[(b - 1) % 2])
                        P.wait("tensor", self.s_cw, tabk_mul_done)
                    for it2 in range(4):
                        P.emit("tensor", lambda E, it2=it2: E.matmul(
                            kr_ps[64:65, 512 + it2 * 128:512 + (it2 + 1) * 128],
                            self.rstd_bf[:, it2:it2 + 1], self.ident_sb[:],
                            start=True, stop=True, skip_group_check=True),
                            inc=(self.s_ax if it2 == 3 else None))
                    self.krow_at = self.s_ax.n
                    P.wait("scalar", self.s_ax)
                    P.emit("scalar", lambda E, b=b: E.activation(
                        rxrow[b % 2][:], kr_ps[64:65, 512:1024], AF.Copy,
                        bias=0.0, scale=1.0), inc=self.s_sc)
                    rx_done[b % 2] = self.s_sc.n
                    P.wait("tensor", self.s_te)
                    if b > 0:
                        P.wait("tensor", self.s_cw, cache_done)
                    for ot in range(2):
                        for j in range(16):
                            P.emit("tensor", lambda E, ot=ot, j=j: E.matmul(
                                ckv_ps[:, ot * 512:(ot + 1) * 512],
                                wckv_sb[:, j * KVR + ot * 128:j * KVR + (ot + 1) * 128],
                                xlnT[:, j * 512:(j + 1) * 512],
                                start=(j == 0), stop=(j == 15)),
                                inc=(self.s_mm if j == 15 else None))
                    if b > 0:
                        P.wait("tensor", self.s_cw, rope_done)
                    for j in range(16):
                        P.emit("tensor", lambda E, j=j: E.matmul(
                            kr_ps[0:96, 0:512],
                            wkr_sb[:, j * 96:j * 96 + 96],
                            xlnT[:, j * 512:(j + 1) * 512],
                            start=(j == 0), stop=(j == 15)),
                            inc=(self.s_mm if j == 15 else None))
                    P.wait("scalar", self.s_mm, self.s_mm.n - 1)
                    P.emit("scalar", lambda E: E.activation(
                        sqc[:], ckv_ps[:, 0:512], AF.Square), inc=self.s_sc)
                    P.wait("tensor", self.s_sc)
                    P.wait("tensor", self.s_vc, krm_red_done)
                    P.emit("tensor", lambda E: E.matmul(
                        ax_ps[0:1, 0:512], self.ones_bf[:], sqc[:],
                        start=True, stop=False), inc=self.s_ax)
                    P.wait("scalar", self.s_ax)
                    P.emit("scalar", lambda E: E.activation(
                        sqc[:], ckv_ps[:, 512:1024], AF.Square), inc=self.s_sc)
                    P.wait("tensor", self.s_sc)
                    P.emit("tensor", lambda E: E.matmul(
                        ax_ps[0:1, 0:512], self.ones_bf[:], sqc[:],
                        start=False, stop=True), inc=self.s_ax)
                    P.wait("scalar", self.s_ax)
                    P.emit("scalar", lambda E: E.activation(
                        rmsr[:], ax_ps[0:1, 0:512], AF.Ln,
                        bias=self.eps_t[0:1, 1:2], scale=1.0 / KVR), inc=self.s_sc)
                    P.wait("scalar", self.s_sc)
                    P.emit("scalar", lambda E: E.activation(
                        rmsr[:], rmsr[:], AF.Exp, bias=0.0, scale=-0.5), inc=self.s_sc)
                    P.wait("tensor", self.s_sc, rx_done[b % 2])
                    P.emit("tensor", lambda E, b=b: E.matmul(
                        kr_ps[0:96, 512:1024], self.ones_rbf[0:1, 0:96],
                        rxrow[b % 2][:],
                        start=True, stop=True, skip_group_check=True), inc=self.s_ax)
                    P.wait("tensor", self.s_sc)
                    P.wait("tensor", self.s_vc, rstdb_done)
                    P.emit("tensor", lambda E: E.matmul(
                        ax_ps[:, 512:1024], self.ones_row[:], rmsr[:],
                        start=True, stop=True, skip_group_check=True),
                        inc=self.s_ax)

                def block_end(b):
                    nonlocal krm_red_done, rstdb_done, cache_done, rope_done
                    nonlocal tabk_mul_done
                    P.wait("vector", self.s_tk)
                    P.wait("vector", self.s_ax)
                    P.emit("vector", lambda E, b=b: E.tensor_mul(
                        tabk_sb[b % 2][32:64, :], tabk_sb[b % 2][32:64, :],
                        kr_ps[32:64, 512:1024]),
                        inc=self.s_cw)
                    P.emit("vector", lambda E, b=b: E.tensor_mul(
                        tabk_sb[b % 2][64:96, :], tabk_sb[b % 2][64:96, :],
                        kr_ps[64:96, 512:1024]),
                        inc=self.s_cw)
                    tabk_mul_done = self.s_cw.n
                    P.emit("vector", lambda E, b=b: E.tensor_mul(
                        self.kaug[0][64:96, b * 512:(b + 1) * 512],
                        kr_ps[64:96, 0:512],
                        tabk_sb[b % 2][64:96, :]), inc=self.s_cw)
                    P.emit("vector", lambda E, b=b: E.tensor_mul(
                        self.kaug[1][32:64, b * 512:(b + 1) * 512],
                        kr_ps[32:64, 0:512],
                        tabk_sb[b % 2][32:64, :]), inc=self.s_cw)
                    rope_done = self.s_cw.n
                    rope_hist[b] = rope_done
                    P.emit("vector", lambda E: E.tensor_copy(
                        rstdb[:], ax_ps[:, 512:1024]), inc=self.s_vc)
                    rstdb_done = self.s_vc.n
                    for ot in range(2):
                        P.emit("vector", lambda E, b=b, ot=ot: E.scalar_tensor_tensor(
                            self.ckv_sb[:, ot * P_TOT + b * 512:ot * P_TOT + (b + 1) * 512],
                            ckv_ps[:, ot * 512:(ot + 1) * 512],
                            self.kvg_sb[:, ot:ot + 1], rstdb[:], ALU.mult, ALU.mult),
                            inc=self.s_cw)
                    cache_done = self.s_cw.n
                    P.wait("scalar", self.s_cw)
                    P.emit("scalar", lambda E, b=b: E.activation(
                        sqc[64:96, :], self.kaug[0][64:96, b * 512:(b + 1) * 512],
                        AF.Square), inc=self.s_sc)
                    P.wait("tensor", self.s_sc)
                    P.emit("tensor", lambda E: E.matmul(
                        ax_ps[0:1, 0:512], self.ones_bf[64:96, :], sqc[64:96, :],
                        start=True, stop=True), inc=self.s_ax)
                    P.wait("vector", self.s_ax)
                    P.emit("vector", lambda E: E.reduce_max(
                        self.krm[0:1, 1:2], ax_ps[0:1, 0:512], axis=AX.X),
                        inc=self.s_vc)
                    krm_red_done = self.s_vc.n
                    P.wait("vector", self.s_vc)
                    P.emit("vector", lambda E: E.tensor_max(
                        self.krm[0:1, 0:1], self.krm[0:1, 0:1], self.krm[0:1, 1:2]),
                        inc=self.s_cw)
                    return

                pend = None
                for b in range(NB):
                    tiles(b)
                    if pend is not None:
                        block_end(pend)
                    mms(b)
                    pend = b
                block_end(pend)
                if "p1x" in self.dbg:
                    P.wait("gpsimd", self.s_cw)
                    P.wait("gpsimd", self.s_ap)
                    for nm2, src2 in [("xlnT", xlnT), ("stm", self.stm),
                                      ("rstdc", self.rstd_c), ("rstdb", rstdb)]:
                        P.dma("gpsimd", lambda E, nm2=nm2, src2=src2: E.dma_start(
                            out=self.dbg_t[nm2], in_=src2[:]), self.s_ou)

    def prep(self):
        nc, P, inp = self.nc, self.P, self.inp
        pr = contextlib.ExitStack()
        with pr:
            xst = [self.sb(pr, f"pxst{i}", [128, D], F32) for i in range(2)]
            dump = self.sb(pr, "pdump", [128, D], BF16)
            xlnbf = [self.sb(pr, f"pxlnbf{i}", [128, D], BF16) for i in range(2)]
            xlnT = self.sb(pr, "pxlnT", [128, 16 * 512], BF16)
            wcq = [self.sb(pr, f"wcq{i}", [128, QR], BF16) for i in range(3)]
            sq2 = [self.sb(pr, f"sq2{i}", [128, 512], BF16) for i in range(2)]
            rmsr2 = self.sb(pr, "rmsr2", [1, 512], F32)
            rstdb2 = self.sb(pr, "rstdb2", [128, 512], F32)

            with (nc.psum_tensor("cq_ps", [128, 6 * 512], F32) as cq_ps,
                  nc.psum_tensor("pax_ps", [128, 1024], F32) as pax_ps):
                self.phase_barrier(("sync", "gpsimd", "tensor"),
                                   ("sc", "vc", "ap", "cw", "ax", "te", "tk"))
                P.dma("sync", lambda E: E.dma_start(
                    out=self.tabq_sb[:], in_=inp["tabQ"]), self.s_ini)
                P.dma("gpsimd", lambda E: E.dma_start(
                    out=wcq[0][:], in_=inp["w_cq"][0:128, :]), self.s_wc)
                self.cq_at = []
                for qb in range(4):
                    for it in range(4):
                        g = qb * 4 + it
                        buf = g % 2
                        if g >= 2:
                            P.wait("sync", self.s_ap, self.s_ap.n - 1)
                        P.dma("sync", lambda E, g=g, buf=buf: E.dma_start(
                            out=xst[buf][:],
                            in_=inp["xp"][(56 + g) * 128:(57 + g) * 128, :]), self.s_x)
                        P.wait("vector", self.s_x)
                        P.wait("vector", self.s_tp, self.s_tp.n - 4)
                        self.emit_center(xst[buf][:], xlnbf[g % 2][:], dump, g % 2)
                        P.wait("tensor", self.s_ap)
                        self.emit_transpose(xlnbf[g % 2], xlnT, it, pax_ps)
                    P.wait("tensor", self.s_te)
                    P.wait("tensor", self.s_qe)
                    for j in range(16):
                        if not (qb == 3 and j == 15):
                            gj = 16 * qb + j + 1
                            if len(self.cq_at) >= 2:
                                P.wait("gpsimd", self.s_pq, self.cq_at[-2])
                            P.dma("gpsimd", lambda E, jn=(j + 1) % 16, gj=gj: E.dma_start(
                                out=wcq[gj % 3][:],
                                in_=inp["w_cq"][jn * 128:(jn + 1) * 128, :]), self.s_wc)
                        P.wait("tensor", self.s_wc, 16 * (16 * qb + j + 1))
                        for m in range(6):
                            P.emit("tensor", lambda E, j=j, m=m, qb=qb: E.matmul(
                                cq_ps[:, m * 512:(m + 1) * 512],
                                wcq[(16 * qb + j) % 3][:, m * 128:(m + 1) * 128],
                                xlnT[:, j * 512:(j + 1) * 512],
                                start=(j == 0), stop=(j == 15)),
                                inc=(self.s_pq if m == 5 else None))
                        self.cq_at.append(self.s_pq.n)
                    P.wait("scalar", self.s_pq)
                    ax_at = []
                    for m in range(6):
                        if m >= 2:
                            P.wait("scalar", self.s_ax, ax_at[m - 2])
                        P.emit("scalar", lambda E, m=m: E.activation(
                            sq2[m % 2][:], cq_ps[:, m * 512:(m + 1) * 512], AF.Square),
                            inc=self.s_sc)
                        P.wait("tensor", self.s_sc)
                        P.emit("tensor", lambda E, m=m: E.matmul(
                            pax_ps[0:1, 0:512], self.ones_bf[:], sq2[m % 2][:],
                            start=(m == 0), stop=(m == 5)), inc=self.s_ax)
                        ax_at.append(self.s_ax.n)
                    P.wait("scalar", self.s_ax)
                    P.emit("scalar", lambda E: E.activation(
                        rmsr2[:], pax_ps[0:1, 0:512], AF.Ln,
                        bias=self.eps_t[0:1, 1:2], scale=1.0 / QR), inc=self.s_sc)
                    P.wait("scalar", self.s_sc)
                    P.emit("scalar", lambda E: E.activation(
                        rmsr2[:], rmsr2[:], AF.Exp, bias=0.0, scale=-0.5), inc=self.s_sc)
                    P.wait("tensor", self.s_sc)
                    P.emit("tensor", lambda E: E.matmul(
                        pax_ps[:, 512:1024], self.ones_row[:], rmsr2[:],
                        start=True, stop=True), inc=self.s_ax)
                    P.wait("vector", self.s_ax)
                    P.emit("vector", lambda E: E.tensor_copy(
                        rstdb2[:], pax_ps[:, 512:1024]), inc=self.s_vc)
                    for m in range(6):
                        P.emit("vector", lambda E, m=m, qb=qb: E.scalar_tensor_tensor(
                            self.normq[:, m * QL + qb * 512:m * QL + (qb + 1) * 512],
                            cq_ps[:, m * 512:(m + 1) * 512],
                            self.qg_sb[:, m:m + 1], rstdb2[:], ALU.mult, ALU.mult),
                            inc=self.s_qe)
                if "prx" in self.dbg:
                    P.wait("gpsimd", self.s_qe)
                    P.dma("gpsimd", lambda E: E.dma_start(
                        out=self.dbg_t["xlnT"], in_=xlnT[:]), self.s_ou)
                    P.dma("gpsimd", lambda E: E.dma_start(
                        out=self.dbg_t["rstdb"], in_=rstdb2[:]), self.s_ou)

    def attention(self):
        nc, P, inp = self.nc, self.P, self.inp
        p2 = contextlib.ExitStack()
        with p2:
            qcur = [self.sb(p2, f"qcur{i}", [128, QL], BF16) for i in range(2)]
            pp = [self.sb(p2, f"pp{i}", [128, 2048], BF16) for i in range(2)]
            qsq = self.sb(p2, "qsq", [128, QL], BF16)
            dn_tmp = self.sb(p2, "dn_tmp", [128, QL], F32)
            aostg = self.sb(p2, "aostg", [64, QL], BF16)
            sqb = [self.sb(p2, f"sqb{i}", [128, 512], BF16) for i in range(2)]
            wvec = self.sb(p2, "wvec", [128, 2], BF16)
            wkn_sb = self.sb(p2, "wkn_sb", [128, 2 * 1024], BF16)
            wv_sb = self.sb(p2, "wv_sb", [128, 2 * 1024], BF16)
            wqn_sb = self.sb(p2, "wqn_sb", [128, 6 * 1024], BF16)
            wqr_sb = self.sb(p2, "wqr_sb", [128, 6 * 576], BF16)

            self.phase_barrier(("sync", "gpsimd", "tensor", "vector", "scalar"),
                               ("sc", "vc", "ap", "qe", "ax", "te", "pq", "mm"))
            P.emit("vector", lambda E: E.memset(wvec[:], 0.0))
            P.emit("vector", lambda E: E.memset(qcur[0][96:128, :], 0.0))
            P.emit("vector", lambda E: E.memset(qcur[1][0:32, :], 0.0))
            P.emit("vector", lambda E: E.memset(wvec[0:64, 0:1], float(LAM * LAM)))
            P.emit("vector", lambda E: E.memset(wvec[64:128, 1:2], float(LAM * LAM)),
                   inc=self.s_vc)
            for nm, dst, inner in [("w_kn", wkn_sb, 1024), ("w_v", wv_sb, 1024),
                                   ("w_qn", wqn_sb, 1024), ("w_qr_pad", wqr_sb, 576)]:
                P.dma("gpsimd", lambda E, nm=nm, dst=dst, inner=inner: E.dma_start(
                    out=dst[:].rearrange("p (j n) -> p j n", n=inner),
                    in_=inp[nm].rearrange("(j p) n -> p j n", p=128)), self.s_wc)

            with (nc.psum_tensor("s_ps0", [128, 1024], F32) as s0,
                  nc.psum_tensor("s_ps1", [128, 1024], F32) as s1,
                  nc.psum_tensor("o_ps", [65, 2048], F32) as o_ps):
                self.S = [s0, s1]
                P.wait("tensor", self.s_wc)
                P.wait("tensor", self.s_qe)
                P.wait("tensor", self.s_cw)
                P.wait("tensor", self.s_vc)
                self.selreg = nc.alloc_register(mybir.EngineType.PE, name="chsel")
                self.reg_free = [(None, 0)] * 4
                self.pv_done = {}
                self.eg_hist = []
                self.git_base = 0
                self.mh_qsq_at = []
                self.od_at = []

                mrow = dn_tmp[0:1, :]
                for p in range(H // 2):
                    self.emit_pair(p, qcur, sqb, wvec, wkn_sb, wv_sb,
                                   wqn_sb, wqr_sb)
                    self.emit_mhat(2 * p, 0, p, qcur, qsq, mrow)
                    self.emit_mhat(2 * p + 1, 1, p, qcur, qsq, mrow)
                    if "qcur" in self.dbg and p == 0:
                        P.wait("gpsimd", self.s_mh)
                        P.dma("gpsimd", lambda E: E.dma_start(
                            out=self.dbg_t["q0"], in_=qcur[0][:]), self.s_ou)
                        P.dma("gpsimd", lambda E: E.dma_start(
                            out=self.dbg_t["q1"], in_=qcur[1][:]), self.s_ou)
                    self.emit_head(2 * p, 0, qcur, pp, o_ps, dn_tmp, aostg)
                    self.emit_head(2 * p + 1, 1, qcur, pp, o_ps, dn_tmp, aostg)

        P.wait("vector", self.s_od)
        P.emit("vector", lambda E: E.reciprocal(self.recip[:], self.denoms[:]),
               inc=self.s_rc)

    def phase_barrier(self, engines, sem_names):
        for e in engines:
            for nm in sem_names:
                self.P.wait(e, getattr(self, "s_" + nm))

    def sreg(self, r, p0=0, p1=128, c0=0, w=512):
        base = (r % 2) * 512 + c0
        return self.S[r // 2][p0:p1, base:base + w]

    def use_region(self, r):
        sem, tgt = self.reg_free[r]
        if sem is not None:
            self.P.wait("tensor", sem, tgt)

    def release_region(self, r, sem):
        self.reg_free[r] = (sem, sem.n)

    def emit_pair(self, p, qcur, sqb, wvec, wkn_sb, wv_sb, wqn_sb, wqr_sb):
        P = self.P
        kaug, v_sb, ckv, normq = self.kaug, self.v_sb, self.ckv_sb, self.normq
        sq_hist = {}
        bm_hist = {}

        def emit_bm(bb):
            r = bb % 4
            P.wait("tensor", self.s_sc, sq_hist[bb])
            P.emit("tensor", lambda E, r=r, bb=bb: E.matmul(
                self.sreg(r, 0, 2),
                wvec[:], sqb[bb % 2][:], start=True, stop=True,
                skip_group_check=True), inc=self.s_bn)
            P.wait("vector", self.s_bn)
            P.emit("vector", lambda E, r=r: E.reduce_max(
                self.bm_scr[:], self.sreg(r, 0, 2), axis=AX.X),
                inc=self.s_vc)
            P.wait("vector", self.s_vc)
            P.emit("vector", lambda E, p=p: E.tensor_max(
                self.bm2[:, p:p + 1], self.bm2[:, p:p + 1], self.bm_scr[:]),
                inc=self.s_bn)
            bm_hist[bb] = self.s_bn.n
            self.release_region(r, self.s_bn)

        for blk in range(NB):
            r = blk % 4
            self.use_region(r)
            for ot in range(2):
                P.emit("tensor", lambda E, blk=blk, ot=ot, r=r, p=p: E.matmul(
                    self.sreg(r),
                    wkn_sb[:, ot * 1024 + p * 128:ot * 1024 + (p + 1) * 128],
                    ckv[:, ot * P_TOT + blk * 512:ot * P_TOT + (blk + 1) * 512],
                    start=(ot == 0), stop=(ot == 1), skip_group_check=True),
                    inc=(self.s_ex if ot == 1 else None))
            P.wait("vector", self.s_ex)
            P.emit("vector", lambda E, blk=blk, r=r: E.tensor_scalar(
                kaug[0][0:64, blk * 512:(blk + 1) * 512],
                self.sreg(r, 0, 64), LAM, None, ALU.mult))
            P.emit("vector", lambda E, blk=blk, r=r: E.tensor_scalar(
                kaug[1][64:128, blk * 512:(blk + 1) * 512],
                self.sreg(r, 64, 128), LAM, None, ALU.mult),
                inc=self.s_xv)
            P.wait("scalar", self.s_xv)
            if blk >= 2:
                P.wait("scalar", self.s_bn, bm_hist.get(blk - 2, 0))
            P.emit("scalar", lambda E, r=r, blk=blk: E.activation(
                sqb[blk % 2][:], self.sreg(r), AF.Square), inc=self.s_sc)
            sq_hist[blk] = self.s_sc.n
            if blk >= 1:
                emit_bm(blk - 1)
        emit_bm(NB - 1)

        for gk in range(NB):
            r = gk % 4
            self.use_region(r)
            for sub in range(4):
                kt = gk * 4 + sub
                for ot in range(2):
                    P.emit("tensor", lambda E, kt=kt, sub=sub, ot=ot, r=r, p=p: E.matmul(
                        self.sreg(r, c0=sub * 128, w=128),
                        ckv[:, ot * P_TOT + kt * 128:ot * P_TOT + (kt + 1) * 128],
                        wv_sb[:, ot * 1024 + p * 128:ot * 1024 + (p + 1) * 128],
                        start=(ot == 0), stop=(ot == 1), skip_group_check=True),
                        inc=(self.s_ex if (ot == 1 and sub == 3) else None))
            P.wait("vector", self.s_ex)
            for lane in range(2):
                P.emit("vector", lambda E, gk=gk, r=r, lane=lane: E.tensor_copy(
                    v_sb[:].rearrange("p (k c) -> p k c", c=130)
                    [:, gk * 4:(gk + 1) * 4, lane * 65:lane * 65 + 64],
                    self.sreg(r)
                    .rearrange("p (k c) -> p k c", c=128)
                    [:, :, lane * 64:(lane + 1) * 64]),
                    inc=(self.s_xv if lane == 1 else None))
            self.release_region(r, self.s_xv)

        for qc in range(4):
            r = qc
            self.use_region(r)
            for m in range(6):
                P.emit("tensor", lambda E, qc=qc, m=m, r=r, p=p: E.matmul(
                    self.sreg(r),
                    wqn_sb[:, m * 1024 + p * 128:m * 1024 + (p + 1) * 128],
                    normq[:, m * QL + qc * 512:m * QL + (qc + 1) * 512],
                    start=(m == 0), stop=(m == 5), skip_group_check=True),
                    inc=(self.s_ex if m == 5 else None))
            P.wait("vector", self.s_ex)
            P.emit("vector", lambda E, qc=qc, r=r: E.tensor_scalar(
                qcur[0][0:64, qc * 512:(qc + 1) * 512],
                self.sreg(r, 0, 64), LAM, None, ALU.mult))
            P.emit("vector", lambda E, qc=qc, r=r: E.tensor_scalar(
                qcur[1][64:128, qc * 512:(qc + 1) * 512],
                self.sreg(r, 64, 128), LAM, None, ALU.mult),
                inc=self.s_qe)
            self.release_region(r, self.s_qe)
        for par in range(2):
            hh = 2 * p + par
            if par == 0:
                c0, rl, rh, mw = hh * 32, 64, 96, 96
            else:
                c0, rl, rh, mw = 32 + hh * 32, 32, 64, 64
            for qc in range(4):
                r = qc
                self.use_region(r)
                for m in range(6):
                    P.emit("tensor", lambda E, qc=qc, m=m, r=r, c0=c0, mw=mw: E.matmul(
                        self.sreg(r, 0, mw),
                        wqr_sb[:, m * 576 + c0:m * 576 + c0 + mw],
                        normq[:, m * QL + qc * 512:m * QL + (qc + 1) * 512],
                        start=(m == 0), stop=(m == 5), skip_group_check=True),
                        inc=(self.s_ex if m == 5 else None))
                P.wait("vector", self.s_ex)
                P.emit("vector", lambda E, qc=qc, r=r, par=par, rl=rl, rh=rh: E.tensor_mul(
                    qcur[par][rl:rh, qc * 512:(qc + 1) * 512],
                    self.sreg(r, rl, rh),
                    self.tabq_sb[rl:rh, qc * 512:(qc + 1) * 512]), inc=self.s_qe)
                self.release_region(r, self.s_qe)
        P.wait("gpsimd", self.s_bn)
        P.dma("gpsimd", lambda E, p=p: E.dma_start(
            out=self.bmo[0:1, p:p + 1], in_=self.bm2[1:2, p:p + 1]), self.s_od)

    def emit_mhat(self, h, par, p, qcur, qsq, mrow):
        P = self.P
        rl, rh = (0, 96) if par == 0 else (32, 128)
        arow = 96 if par == 0 else 0
        if par == 0:
            bsrc = self.bm2[0:1, p:p + 1]
            P.wait("vector", self.s_bn)
        else:
            P.wait("vector", self.s_od)
            bsrc = self.bmo[0:1, p:p + 1]
        P.emit("vector", lambda E, bsrc=bsrc: E.tensor_add(
            self.lnb[0:1, 0:1], bsrc, self.krm[0:1, 0:1]), inc=self.s_vc)
        P.wait("vector", self.s_qe)
        if len(self.mh_qsq_at) >= 1:
            P.wait("vector", self.s_mh, self.mh_qsq_at[-1])
        qpieces = [(rl, rh)] if rl == 0 else [(32, 64), (64, 128)]
        for pi, (a0, a1) in enumerate(qpieces):
            P.emit("vector", lambda E, par=par, a0=a0, a1=a1: E.tensor_mul(
                qsq[a0:a1, :], qcur[par][a0:a1, :], qcur[par][a0:a1, :]),
                inc=(self.s_vc if pi == len(qpieces) - 1 else None))
        P.wait("scalar", self.s_vc)
        P.emit("scalar", lambda E: E.activation(
            self.lnb[0:1, 1:2], self.lnb[0:1, 0:1], AF.Ln), inc=self.s_sc)
        P.wait("scalar", self.s_sc)
        P.emit("scalar", lambda E: E.activation(
            self.lnb[0:1, 1:2], self.lnb[0:1, 1:2], AF.Copy, bias=0.0, scale=0.5),
            inc=self.s_sc)
        for qc in range(4):
            self.use_region(qc)
            P.wait("tensor", self.s_vc)
            pieces = [(rl, rh)] if rl == 0 else [(32, 64), (64, 128)]
            for pi, (a0, a1) in enumerate(pieces):
                P.emit("tensor", lambda E, qc=qc, a0=a0, a1=a1, pi=pi,
                       np_=len(pieces): E.matmul(
                    self.sreg(qc, 0, 1),
                    self.ones_bf[a0:a1, :], qsq[a0:a1, qc * 512:(qc + 1) * 512],
                    start=(pi == 0), stop=(pi == np_ - 1), skip_group_check=True),
                    inc=(self.s_mh if pi == len(pieces) - 1 else None))
        self.mh_qsq_at.append(self.s_mh.n)
        P.wait("scalar", self.s_mh)
        P.emit("scalar", lambda E: E.activation(
            mrow[0:1, 0:1024], self.S[0][0:1, 0:1024], AF.Ln), inc=self.s_sc)
        P.emit("scalar", lambda E: E.activation(
            mrow[0:1, 1024:2048], self.S[1][0:1, 0:1024], AF.Ln), inc=self.s_sc)
        P.wait("scalar", self.s_sc)
        P.emit("scalar", lambda E, par=par, arow=arow: E.activation(
            qcur[par][arow:arow + 1, :], mrow[:], AF.Exp,
            bias=self.lnb[0:1, 1:2], scale=0.5), inc=self.s_mh)
        for r in range(4):
            self.release_region(r, self.s_mh)

    def emit_head(self, h, par, qcur, pp, o_ps, dn_tmp, aostg):
        P = self.P
        kaug, v_sb = self.kaug, self.v_sb
        gbase = self.git_base
        P.wait("tensor", self.s_oe)
        for r in range(4):
            P.emit("tensor", lambda E, r=r: E.matmul(
                o_ps[0:65, r * 512:(r + 1) * 512],
                self.zer_bf[:, 0:65], self.zer_bf[:, 0:512],
                start=True, stop=False, skip_group_check=True), inc=self.s_pv)
        P.wait("tensor", self.s_mh)
        P.wait("tensor", self.s_qe)
        P.wait("tensor", self.s_xv)

        class Sub:

            def __init__(self, sems):
                self.ops = []
                self.sems = sems

            def emit(self, fn, inc=None, k=1, count=True):
                if inc is None:
                    self.ops.append(fn)
                    return

                def g(E, fn=fn, sem=inc, k=k):
                    fn(E).then_inc(sem.h, k)
                self.ops.append(g)
                if count:
                    inc.n += k

            def wait(self, sem, val):
                if val > 0:
                    self.ops.append(lambda E, hh=sem.h, v=val: E.wait_ge(hh, v))

        def pps(t, hf=None):
            j = t % 4
            buf = pp[j // 2]
            c0 = (j % 2) * 1024
            if hf is None:
                return buf[:, c0:c0 + 1024]
            return buf[:, c0 + hf * 512:c0 + hf * 512 + 512]

        def emit_exp(t, hf=None):
            P.wait("scalar", self.s_sg)
            gi = gbase + t
            if gi - 4 in self.pv_done:
                P.wait("scalar", self.s_pv, self.pv_done[gi - 4])
            if hf is None:
                P.emit("scalar", lambda E, t=t: E.activation(
                    pps(t), self.S[t % 2][:, 0:1024], AF.Exp), inc=self.s_eg)
            else:
                P.emit("scalar", lambda E, t=t, hf=hf: E.activation(
                    pps(t, hf), self.S[t % 2][:, hf * 512:(hf + 1) * 512],
                    AF.Exp), inc=self.s_eg)
            self.eg_hist.append(self.s_eg.n)

        for sg in range(7):
            bodyA = Sub(self)
            bodyB = Sub(self)
            sinks = [bodyA, bodyB]

            def rec_pv(t):
                for s in sinks:
                    s.wait(self.s_eg, self.eg_hist[-1])
                for hf in range(2):
                    for si, s in enumerate(sinks):
                        cqv = si * 2 + hf
                        s.emit(lambda E, t=t, par=par, cqv=cqv, hf=hf: E.matmul(
                            o_ps[0:65, cqv * 512:(cqv + 1) * 512],
                            v_sb[:, t * 130 + par * 65:t * 130 + par * 65 + 65],
                            pps(t, hf),
                            start=False, stop=False, skip_group_check=True),
                            inc=(self.s_pv if hf == 1 else None), count=(si == 0))
                self.pv_done[gbase + t] = self.s_pv.n

            for i in range(8):
                t = sg * 8 + i
                if len(self.eg_hist) >= 2:
                    for s in sinks:
                        s.wait(self.s_eg, self.eg_hist[-2])
                for hf in range(2):
                    for si, s in enumerate(sinks):
                        cqv = si * 2 + hf
                        s.emit(lambda E, t=t, par=par, cqv=cqv, hf=hf: E.matmul(
                            self.sreg((t % 2) * 2 + hf),
                            kaug[par][:, t * 128:(t + 1) * 128],
                            qcur[par][:, cqv * 512:(cqv + 1) * 512],
                            start=True, stop=True, skip_group_check=True),
                            inc=(self.s_sg if hf == 1 else None), count=(si == 0))
                if i > 0:
                    rec_pv(t - 1)
                emit_exp(t)
            rec_pv(sg * 8 + 7)

            def run_branch(E, sg=sg, A=bodyA.ops, B=bodyB.ops):
                E.reg_load(self.selreg, self.qtab_sb[0:1, sg:sg + 1])
                with E.If_eq(self.selreg, 0):
                    for f in A:
                        f(E)
                with E.Else():
                    for f in B:
                        f(E)
            P.q["tensor"].append(run_branch)

        def iter_info(t):
            i = t - NINT
            cq = 0 if i < 8 else 2
            rr = (i % 8) * 128
            if rr < 512:
                return [(0, cq, 1024 - rr), (1, cq + 1, None)]
            return [(1, cq + 1, 1024 - (rr - 512))]

        def rec_pv_diag(prev):
            t, info, (sem, val) = prev
            P.wait("tensor", sem, val)
            for idx, (hf, cqv, moff) in enumerate(info):
                last = idx == len(info) - 1
                P.emit("tensor", lambda E, t=t, par=par, hf=hf, cqv=cqv: E.matmul(
                    o_ps[0:65, cqv * 512:(cqv + 1) * 512],
                    v_sb[:, t * 130 + par * 65:t * 130 + par * 65 + 65],
                    pps(t, hf),
                    start=False, stop=False, skip_group_check=True),
                    inc=(self.s_pv if last else None))
            self.pv_done[gbase + t] = self.s_pv.n

        prev = None
        for t in range(NINT, NKS):
            info = iter_info(t)
            if len(self.eg_hist) >= 2:
                P.wait("tensor", self.s_eg, self.eg_hist[-2])
            for idx, (hf, cqv, moff) in enumerate(info):
                last = idx == len(info) - 1
                P.emit("tensor", lambda E, t=t, par=par, hf=hf, cqv=cqv: E.matmul(
                    self.sreg((t % 2) * 2 + hf),
                    kaug[par][:, t * 128:(t + 1) * 128],
                    qcur[par][:, cqv * 512:(cqv + 1) * 512],
                    start=True, stop=True, skip_group_check=True),
                    inc=(self.s_sg if last else None))
            if prev is not None:
                rec_pv_diag(prev)
            emit_exp(t, hf=(info[0][0] if len(info) == 1 else None))
            ready = (self.s_eg, self.eg_hist[-1])
            nm = 0
            for (hf, cqv, moff) in info:
                if moff is None:
                    continue
                P.wait("vector", self.s_eg)
                P.emit("vector", lambda E, t=t, hf=hf, moff=moff: E.tensor_mul(
                    pps(t, hf), pps(t, hf),
                    self.stair_sb[:, moff:moff + 512]), inc=self.s_mk)
                nm += 1
            if nm:
                ready = (self.s_mk, self.s_mk.n)
            prev = (t, info, ready)
        rec_pv_diag(prev)
        self.git_base += NKS
        for r in range(4):
            self.release_region(r, self.s_eg)

        P.wait("vector", self.s_pv)
        if len(self.od_at) >= 1:
            P.wait("vector", self.s_od, self.od_at[-1])
        P.emit("vector", lambda E, h=h: E.tensor_copy(
            aostg[:], o_ps[0:64, :]), inc=self.s_oe)
        P.emit("vector", lambda E: E.tensor_copy(
            dn_tmp[64:65, :], o_ps[64:65, :]), inc=self.s_oe)
        P.wait("gpsimd", self.s_oe)
        P.dma("gpsimd", lambda E, h=h: E.dma_start(
            out=self.ao_d[h], in_=aostg[:]), self.s_od)
        P.dma("gpsimd", lambda E, h=h: E.dma_start(
            out=self.denoms[h:h + 1, :], in_=dn_tmp[64:65, :]), self.s_od)
        self.od_at.append(self.s_od.n)

    def proj(self):
        nc, P, inp = self.nc, self.P, self.inp
        pj = contextlib.ExitStack()
        with pj:
            attn = self.sb(pj, "attn_all", [128, 8 * QL], BF16)
            wpj = self.sb(pj, "wproj_sb", [128, 8 * D], BF16)
            xown = [self.sb(pj, f"xown{i}", [128, D], F32) for i in range(2)]
            self.rrow = self.sb(pj, "rrow", [1, QL], F32)
            x2st = [self.sb(pj, f"x2st{i}", [128, D], F32) for i in range(2)]
            with (nc.psum_tensor("pj_ps", [128, 2048], F32) as pj_ps,
                  nc.psum_tensor("bc_ps", [128, 1024], F32) as bc_ps):
                self.phase_barrier(("sync", "gpsimd", "tensor"),
                                   ("eg", "pv", "oe", "od", "mh", "xv", "bn",
                                    "mk", "qe", "rc"))
                P.dma("gpsimd", lambda E: E.dma_start(
                    out=wpj[:].rearrange("p (j n) -> p j n", n=D),
                    in_=inp["w_proj"].rearrange("(j p) n -> p j n", p=128)), self.s_wc)
                P.wait("gpsimd", self.s_od)
                for j in range(8):
                    for sub in range(2):
                        P.dma("gpsimd", lambda E, j=j, sub=sub: E.dma_start(
                            out=attn[sub * 64:(sub + 1) * 64, j * QL:(j + 1) * QL],
                            in_=self.ao_d[2 * j + sub]), self.s_pj)
                P.wait("gpsimd", self.s_rc)
                ma_at = []
                for hh in range(H):
                    P.wait("gpsimd", self.s_pq)
                    P.dma("gpsimd", lambda E, hh=hh: E.dma_start(
                        out=self.rrow[:], in_=self.recip[hh:hh + 1, :]), self.s_pj)
                    P.wait("tensor", self.s_pj)
                    for k4 in range(4):
                        reg = k4 % 2
                        if len(ma_at) >= 2:
                            P.wait("tensor", self.s_ma, ma_at[-2])
                        P.emit("tensor", lambda E, k4=k4, reg=reg: E.matmul(
                            bc_ps[:, reg * 512:(reg + 1) * 512],
                            self.ones_row[:],
                            self.rrow[0:1, k4 * 512:(k4 + 1) * 512],
                            start=True, stop=True, skip_group_check=True),
                            inc=self.s_pq)
                        P.wait("vector", self.s_pq)
                        r0 = (hh % 2) * 64
                        P.emit("vector", lambda E, hh=hh, k4=k4, reg=reg, r0=r0:
                               E.tensor_mul(
                            attn[r0:r0 + 64,
                                 (hh // 2) * QL + k4 * 512:(hh // 2) * QL + (k4 + 1) * 512],
                            attn[r0:r0 + 64,
                                 (hh // 2) * QL + k4 * 512:(hh // 2) * QL + (k4 + 1) * 512],
                            bc_ps[r0:r0 + 64, reg * 512:(reg + 1) * 512]),
                            inc=self.s_ma)
                        ma_at.append(self.s_ma.n)
                if "pn" in self.dbg:
                    P.wait("gpsimd", self.s_ma)
                    P.dma("gpsimd", lambda E: E.dma_start(
                        out=self.dbg_t["pn"], in_=attn[:]), self.s_ou)
                    P.dma("gpsimd", lambda E: E.dma_start(
                        out=self.dbg_t["rcp"], in_=self.recip[:]), self.s_ou)
                P.wait("tensor", self.s_ma)
                P.wait("tensor", self.s_wc)
                adds_at = []
                for tt in range(16):
                    if tt >= 2:
                        P.wait("sync", self.s_vc, adds_at[tt - 2])
                    P.dma("sync", lambda E, tt=tt: E.dma_start(
                        out=xown[tt % 2][:],
                        in_=inp["xp"][(56 + tt) * 128:(57 + tt) * 128, :]), self.s_x)
                    P.wait("tensor", self.s_vc)
                    for j in range(8):
                        for dc in range(4):
                            P.emit("tensor", lambda E, tt=tt, j=j, dc=dc: E.matmul(
                                pj_ps[:, dc * 512:(dc + 1) * 512],
                                attn[:, j * QL + tt * 128:j * QL + (tt + 1) * 128],
                                wpj[:, j * D + dc * 512:j * D + (dc + 1) * 512],
                                start=(j == 0), stop=(j == 7)),
                                inc=(self.s_mm if j == 7 and dc == 3 else None))
                    P.wait("vector", self.s_mm)
                    P.wait("vector", self.s_x)
                    if tt >= 2:
                        P.wait("vector", self.s_x2, 16 * (tt - 1))
                    P.emit("vector", lambda E, tt=tt: E.tensor_add(
                        x2st[tt % 2][:], pj_ps[:], xown[tt % 2][:]), inc=self.s_vc)
                    adds_at.append(self.s_vc.n)
                    P.wait("sync", self.s_vc)
                    P.dma("sync", lambda E, tt=tt: E.dma_start(
                        out=self.x2_d[tt * 128:(tt + 1) * 128, :],
                        in_=x2st[tt % 2][:]), self.s_x2)

    def ffn(self):
        nc, P, inp = self.nc, self.P, self.inp
        ff = contextlib.ExitStack()
        with ff:
            x2t = [self.sb(ff, f"fx2t{i}", [128, D], F32) for i in range(2)]
            dump = self.sb(ff, "fdump", [128, D], BF16)
            xlnbf = [self.sb(ff, f"fxlnbf{i}", [128, D], BF16) for i in range(2)]
            xlnT = self.sb(ff, "fxlnT", [128, 16 * 512], BF16)
            hbuf = self.sb(ff, "hbuf", [128, 64 * 512], BF16)
            wg8 = [self.sb(ff, f"wg8{i}", [128, 2048], U8) for i in range(2)]
            wu8 = [self.sb(ff, f"wu8{i}", [128, 2048], U8) for i in range(2)]
            wgb = [self.sb(ff, f"wgb{i}", [128, 2048], BF16) for i in range(2)]
            wub = [self.sb(ff, f"wub{i}", [128, 2048], BF16) for i in range(2)]
            wd8 = self.sb(ff, "wd8", [128, 8192], U8)
            wdb = [self.sb(ff, f"wdb{i}", [128, 4096], BF16) for i in range(2)]
            sil = self.sb(ff, "sil", [128, 512], F32)
            ytmp = self.sb(ff, "ytmp", [128, 16 * 512], BF16)
            x3st = [self.sb(ff, f"x3st{i}", [128, D], F32) for i in range(2)]

            with (nc.psum_tensor("big_ps", [128, 2048], F32) as big_ps,
                  nc.psum_tensor("g_ps", [128, 512], F32) as g_ps,
                  nc.psum_tensor("u_ps", [128, 512], F32) as u_ps,
                  nc.psum_tensor("y_ps", [128, 512], F32) as y_ps):
                self.phase_barrier(("sync", "gpsimd", "tensor"),
                                    ("x2", "vc", "mm", "ma", "pq", "x"))
                fo_at = []
                fw0 = self.s_fw.n
                fz0 = self.s_fz.n
                te_mark = [0]
                self.fd_marks = []

                def stage_a(pa):
                    for it in range(4):
                        g = pa * 4 + it
                        buf = g % 2
                        if g >= 2:
                            P.wait("gpsimd", self.s_ap, self.s_ap.n - 1)
                            if fo_at:
                                P.wait("gpsimd", self.s_fo, fo_at[-1])
                        P.dma("gpsimd", lambda E, g=g, buf=buf: E.dma_start(
                            out=x2t[buf][:],
                            in_=self.x2_d[g * 128:(g + 1) * 128, :]), self.s_x)
                        P.wait("scalar", self.s_x)
                        P.wait("vector", self.s_x)
                        P.wait("vector", self.s_tp, self.s_tp.n - 4)
                        self.emit_ln(x2t[buf][:], xlnbf[g % 2][:], dump, g % 2)
                        P.wait("tensor", self.s_ap)
                        if fo_at:
                            P.wait("tensor", self.s_fo, fo_at[-1])
                        self.emit_transpose(xlnbf[g % 2], xlnT, it, big_ps)
                    te_mark[0] = self.s_te.n

                def stage_b(pa):
                    P.wait("tensor", self.s_te)
                    P.dma("sync", lambda E: E.dma_start(
                        out=wg8[0][:], in_=inp["wg_t"][0]), self.s_fw)
                    P.dma("sync", lambda E: E.dma_start(
                        out=wu8[0][:], in_=inp["wu_t"][0]), self.s_fw)
                    for i in range(64):
                        if i + 1 < 64:
                            P.wait("sync", self.s_sc, self.cast_at[-1]
                                   if getattr(self, "cast_at", None) else 0)
                            P.dma("sync", lambda E, i=i: E.dma_start(
                                out=wg8[(i + 1) % 2][:], in_=inp["wg_t"][i + 1]),
                                self.s_fw)
                            P.dma("sync", lambda E, i=i: E.dma_start(
                                out=wu8[(i + 1) % 2][:], in_=inp["wu_t"][i + 1]),
                                self.s_fw)
                        P.wait("scalar", self.s_fw, fw0 + 32 * (64 * pa + i + 1))
                        P.wait("scalar", self.s_fg, self.s_fg.n - 2)
                        P.emit("scalar", lambda E, i=i: E.activation(
                            wgb[i % 2][:], wg8[i % 2][:], AF.Copy, bias=-128.0,
                            scale=1.0))
                        P.emit("scalar", lambda E, i=i: E.activation(
                            wub[i % 2][:], wu8[i % 2][:], AF.Copy, bias=-128.0,
                            scale=1.0), inc=self.s_sc)
                        if not hasattr(self, "cast_at"):
                            self.cast_at = []
                        self.cast_at.append(self.s_sc.n)
                        P.wait("tensor", self.s_sc)
                        P.wait("tensor", self.s_fh)
                        for j in range(16):
                            P.emit("tensor", lambda E, i=i, j=j: E.matmul(
                                g_ps[:], wgb[i % 2][:, j * 128:(j + 1) * 128],
                                xlnT[:, j * 512:(j + 1) * 512],
                                start=(j == 0), stop=(j == 15)),
                                inc=(self.s_fg if j == 15 else None))
                        for j in range(16):
                            P.emit("tensor", lambda E, i=i, j=j: E.matmul(
                                u_ps[:], wub[i % 2][:, j * 128:(j + 1) * 128],
                                xlnT[:, j * 512:(j + 1) * 512],
                                start=(j == 0), stop=(j == 15)),
                                inc=(self.s_fg if j == 15 else None))
                        P.wait("scalar", self.s_fg, self.s_fg.n - 1)
                        P.emit("scalar", lambda E, i=i: E.activation(
                            sil[:], g_ps[:], AF.Silu, bias=0.0,
                            scale=self.gsc_sb[:, i:i + 1]), inc=self.s_sc)
                        P.wait("vector", self.s_sc)
                        P.wait("vector", self.s_fg)
                        P.emit("vector", lambda E, i=i: E.scalar_tensor_tensor(
                            hbuf[:, i * 512:(i + 1) * 512], u_ps[:],
                            self.usc_sb[:, i:i + 1], sil[:], ALU.mult, ALU.mult),
                            inc=self.s_fh)

                def stage_c(pa):
                    P.dma("sync", lambda E: E.dma_start(
                        out=wd8[:], in_=inp["wd_t"][0]), self.s_fz)
                    dcast_at = []
                    for ht in range(16):
                        P.wait("scalar", self.s_fz, fz0 + 16 * (16 * pa + ht + 1))
                        for half in range(2):
                            if self.fd_marks:
                                sem, val = self.fd_marks[-1][half]
                                P.wait("scalar", sem, val)
                            P.emit("scalar", lambda E, ht=ht, half=half: E.activation(
                                wdb[half][:], wd8[:, half * 4096:(half + 1) * 4096],
                                AF.Copy, bias=-128.0, scale=1.0), inc=self.s_sc)
                            dcast_at.append(self.s_sc.n)
                        if ht + 1 < 16:
                            P.wait("sync", self.s_sc, dcast_at[-1])
                            P.dma("sync", lambda E, ht=ht: E.dma_start(
                                out=wd8[:], in_=inp["wd_t"][ht + 1]), self.s_fz)
                        P.wait("tensor", self.s_sc, dcast_at[-2])
                        P.wait("tensor", self.s_fh)
                        P.wait("tensor", self.s_fy)
                        marks = [None, None]
                        for j in range(64):
                            if j == 32:
                                P.wait("tensor", self.s_sc, dcast_at[-1])
                            P.emit("tensor", lambda E, ht=ht, j=j: E.matmul(
                                y_ps[:], wdb[j // 32][:, (j % 32) * 128:(j % 32 + 1) * 128],
                                hbuf[:, j * 512:(j + 1) * 512],
                                start=(j == 0), stop=(j == 63)),
                                inc=(self.s_fd if j == 31 else
                                     (self.s_fy if j == 63 else None)))
                            if j == 31:
                                marks[0] = (self.s_fd, self.s_fd.n)
                            if j == 63:
                                marks[1] = (self.s_fy, self.s_fy.n)
                        self.fd_marks.append(marks)
                        P.wait("vector", self.s_fy)
                        P.emit("vector", lambda E, ht=ht: E.tensor_scalar(
                            ytmp[:, ht * 512:(ht + 1) * 512], y_ps[:],
                            self.dsc_sb[:, ht:ht + 1], None, ALU.mult), inc=self.s_fy)
                    if "ffx" in self.dbg and pa == 0:
                        P.wait("gpsimd", self.s_fy)
                        P.wait("gpsimd", self.s_te)
                        P.dma("gpsimd", lambda E: E.dma_start(
                            out=self.dbg_t["f1"], in_=xlnT[:]), self.s_ou)
                        P.dma("gpsimd", lambda E: E.dma_start(
                            out=self.dbg_t["f2"], in_=hbuf[:]), self.s_ou)
                        P.dma("gpsimd", lambda E: E.dma_start(
                            out=self.dbg_t["f3"], in_=ytmp[:]), self.s_ou)

                def stage_d(pa):
                    for it in range(4):
                        g = pa * 4 + it
                        P.wait("tensor", self.s_fy)
                        P.wait("tensor", self.s_te, te_mark[0])
                        if fo_at:
                            P.wait("tensor", self.s_fo, fo_at[-1])
                        for ht in range(16):
                            P.emit("tensor", lambda E, it=it, ht=ht: E.matmul(
                                big_ps[:, ht * 128:(ht + 1) * 128],
                                ytmp[:, ht * 512 + it * 128:ht * 512 + (it + 1) * 128],
                                self.ident_sb[:], start=True, stop=True),
                                inc=self.s_tp)
                        P.wait("gpsimd", self.s_ap)
                        P.wait("gpsimd", self.s_fo)
                        P.dma("gpsimd", lambda E, g=g: E.dma_start(
                            out=x2t[g % 2][:],
                            in_=self.x2_d[g * 128:(g + 1) * 128, :]), self.s_x)
                        P.wait("vector", self.s_tp)
                        P.wait("vector", self.s_x)
                        if len(fo_at) >= 2:
                            P.wait("vector", self.s_ou, 16 * (len(fo_at) - 1))
                        P.emit("vector", lambda E, g=g: E.tensor_add(
                            x3st[g % 2][:], big_ps[:], x2t[g % 2][:]), inc=self.s_fo)
                        fo_at.append(self.s_fo.n)
                        P.wait("gpsimd", self.s_fo)
                        P.dma("gpsimd", lambda E, g=g: E.dma_start(
                            out=self.out[g * 128:(g + 1) * 128, :],
                            in_=x3st[g % 2][:]), self.s_ou)

                stage_a(0)
                for pa in range(4):
                    stage_b(pa)
                    stage_c(pa)
                    if pa + 1 < 4:
                        stage_a(pa + 1)
                    stage_d(pa)


def kernel(**inputs):
    mk = MK(DBG)
    nc = mk.build()
    maps = [core_inputs(c, inputs) for c in range(8)]
    res = run_bass_kernel_spmd(nc, maps, core_ids=list(range(8)))
    outp = np.zeros((2, 8192, D), np.float32)
    for c in range(8):
        o = res.results[c]["out"]
        outp[0, 1024 * c:1024 * (c + 1)] = o[0:1024]
        outp[1, 1024 * (7 - c):1024 * (8 - c)] = o[1024:2048]
    return outp


# version 10
# speedup vs baseline: 1.1261x; 1.0251x over previous
import contextlib
import numpy as np
import ml_dtypes
import concourse.bass as bass
import concourse.mybir as mybir
from concourse.bass_utils import run_bass_kernel_spmd
from concourse.expressions_rust import RuntimeValue

BFNP = ml_dtypes.bfloat16
F32 = mybir.dt.float32
BF16 = mybir.dt.bfloat16
U8 = mybir.dt.uint8
U32 = mybir.dt.uint32
AF = mybir.ActivationFunctionType
ALU = mybir.AluOpType
AX = mybir.AxisListType

D = 2048
H = 16
QR, KVR = 768, 256
P_TOT = 9216
NKS = 72
NINT = 56
QL = 2048
NB = 18
EPS_LN = 1e-5
EPS_RMS = 1e-6
LAM = float(96.0 ** -0.25)
SKR = float((16.0 / np.sqrt(96.0)) ** 0.5)
SQRP = SKR / 16.0
DBG = ()
SPLIT_EVAC = False

ENGS = ("tensor", "vector", "scalar", "sync", "gpsimd")


class Sem:
    __slots__ = ("h", "n")

    def __init__(self, h):
        self.h = h
        self.n = 0


class Prog:
    def __init__(self):
        self.q = {e: [] for e in ENGS}

    def emit(self, e, fn, inc=None, k=1):
        if inc is None:
            self.q[e].append(fn)
            return

        def g(E, fn=fn, sem=inc, k=k):
            fn(E).then_inc(sem.h, k)
        self.q[e].append(g)
        inc.n += k

    def dma(self, e, fn, sem):
        self.emit(e, fn, inc=sem, k=16)

    def wait(self, e, sem, val=None):
        v = sem.n if val is None else val
        if v <= 0:
            return
        self.q[e].append(lambda E, h=sem.h, v=v: E.wait_ge(h, v))

    def replay(self, block):
        for e in ENGS:
            ops = self.q[e]
            getattr(block, e)(lambda E, ops=ops: [f(E) for f in ops])


def _rope_tab(positions):
    f = np.arange(32) // 2
    invf = 10000.0 ** (-(f / 16.0))
    return np.cos(positions[None, :].astype(np.float64) * invf[:, None])


def core_inputs(c, a):
    QA0 = 1024 * c
    QB0 = 1024 * (7 - c)
    PA = QA0 + 1024
    PB = QB0 + 1024
    x = np.asarray(a["L0_attn_x"])
    d = {}
    d["xp"] = np.ascontiguousarray(
        np.concatenate([x[0, :QA0], x[1, :QB0], x[0, QA0:PA], x[1, QB0:PB]], axis=0))
    pos_k = np.concatenate([np.arange(QA0), np.arange(QB0),
                            np.arange(QA0, PA), np.arange(QB0, PB)])
    pos_q = np.concatenate([np.arange(QA0, PA), np.arange(QB0, PB)])
    ck = _rope_tab(pos_k)
    cq = _rope_tab(pos_q)
    tabK = np.zeros((128, P_TOT), np.float32)
    tabK[32:64] = tabK[64:96] = (np.sqrt(2.0) * (SKR / 16.0) * ck).astype(np.float32)
    tabQ = np.zeros((128, QL), np.float32)
    tabQ[32:64] = tabQ[64:96] = (np.sqrt(2.0) * SQRP * cq).astype(np.float32)
    d["tabK"] = tabK
    d["tabQ"] = tabQ
    qt = np.zeros((1, 128), np.uint32)
    for g in range(NINT // 8):
        qt[0, g] = 0 if g < c else 1
    d["qtab"] = qt
    d["w_cq"] = np.asarray(a["L0_attn_w_cq"])
    d["w_qn"] = np.asarray(a["L0_attn_w_q_nope"])
    d["w_qr_pad"] = np.ascontiguousarray(np.concatenate(
        [np.zeros((QR, 64), np.float32), np.asarray(a["L0_attn_w_q_rope"])], axis=1))
    d["q_g"] = np.ascontiguousarray(np.asarray(a["L0_attn_q_g"]).reshape(6, 128).T)
    d["w_ckv"] = np.asarray(a["L0_attn_w_ckv"])
    d["w_kn"] = np.asarray(a["L0_attn_w_k_nope"])
    d["w_v"] = np.asarray(a["L0_attn_w_v"])
    d["kv_g"] = np.ascontiguousarray(np.asarray(a["L0_attn_kv_g"]).reshape(2, 128).T)
    wkr = np.asarray(a["L0_attn_w_k_rope"])
    d["w_kr_pad"] = np.ascontiguousarray(np.concatenate(
        [np.zeros((D, 32), np.float32), wkr, wkr], axis=1))
    d["w_proj"] = np.asarray(a["L0_attn_w_proj"])
    wgT = np.asarray(a["L0_ffn_gate_w"]).astype(np.uint8).T
    wuT = np.asarray(a["L0_ffn_up_w"]).astype(np.uint8).T
    wdT = np.asarray(a["L0_ffn_down_w"]).astype(np.uint8).T
    d["wg_t"] = np.ascontiguousarray(
        wgT.reshape(16, 128, 64, 128).transpose(2, 1, 0, 3).reshape(64, 128, 2048))
    d["wu_t"] = np.ascontiguousarray(
        wuT.reshape(16, 128, 64, 128).transpose(2, 1, 0, 3).reshape(64, 128, 2048))
    d["wd_t"] = np.ascontiguousarray(
        wdT.reshape(64, 128, 16, 128).transpose(2, 1, 0, 3).reshape(16, 128, 8192))
    d["gsc"] = np.ascontiguousarray(np.asarray(a["L0_ffn_gate_scale"]).reshape(64, 128).T)
    d["usc"] = np.ascontiguousarray(np.asarray(a["L0_ffn_up_scale"]).reshape(64, 128).T)
    d["dsc"] = np.ascontiguousarray(np.asarray(a["L0_ffn_down_scale"]).reshape(16, 128).T)
    stair = np.zeros((128, 2048), np.float32)
    for i in range(128):
        stair[i, 1024 + i:] = 1.0
    d["stair"] = stair.astype(BFNP)
    d["ident"] = np.eye(128, dtype=BFNP)
    return d


class MK:
    def __init__(self, dbg=()):
        self.dbg = dbg
        self.nc = bass.Bass("TRN2", target_bir_lowering=False)
        self.P = Prog()

    def sem(self, name):
        s = Sem(self.st.enter_context(self.nc.semaphore(name)))
        setattr(self, "s_" + name, s)
        return s

    def sb(self, stack, name, shape, dt):
        return stack.enter_context(self.nc.sbuf_tensor(name, shape, dt))

    def build(self):
        nc, P = self.nc, self.P
        dbg = self.dbg
        self.inp = {}
        for name, shape, dt in [
            ("xp", [P_TOT, D], F32), ("tabK", [128, P_TOT], F32),
            ("tabQ", [128, QL], F32), ("qtab", [1, 128], U32),
            ("w_cq", [D, QR], F32), ("w_qn", [QR, 1024], F32),
            ("w_qr_pad", [QR, 576], F32), ("q_g", [128, 6], F32),
            ("w_ckv", [D, KVR], F32), ("w_kn", [KVR, 1024], F32),
            ("w_v", [KVR, 1024], F32), ("kv_g", [128, 2], F32),
            ("w_kr_pad", [D, 96], F32), ("w_proj", [1024, D], F32),
            ("wg_t", [64, 128, 2048], U8), ("wu_t", [64, 128, 2048], U8),
            ("wd_t", [16, 128, 8192], U8),
            ("gsc", [128, 64], F32), ("usc", [128, 64], F32), ("dsc", [128, 16], F32),
            ("stair", [128, 2048], BF16), ("ident", [128, 128], BF16),
        ]:
            self.inp[name] = nc.dram_tensor(name, shape, dt, kind="ExternalInput").ap()
        inp = self.inp
        self.out = nc.dram_tensor("out", [QL, D], F32, kind="ExternalOutput").ap()
        self.x2_d = nc.dram_tensor(
            "x2_scr", [QL, D], F32,
            kind="ExternalOutput" if "x2" in dbg else "Internal").ap()
        self.ao_d = nc.dram_tensor(
            "ao_scr", [H, 64, QL], BF16,
            kind="ExternalOutput" if "ao" in dbg else "Internal").ap()
        self.dbg_t = {}
        for key, nm, shape, dt, want in [
            ("ckv", "dbg_ckv", [128, 2 * P_TOT], BF16, "ckv"),
            ("k0", "dbg_k0", [128, P_TOT], BF16, "kaug"),
            ("k1", "dbg_k1", [128, P_TOT], BF16, "kaug"),
            ("vv", "dbg_v", [128, NKS * 130], BF16, "kaug"),
            ("normq", "dbg_normq", [128, 6 * QL], BF16, "normq"),
            ("q0", "dbg_q0", [128, QL], BF16, "qcur"),
            ("q1", "dbg_q1", [128, QL], BF16, "qcur"),
            ("den", "dbg_den", [16, QL], F32, "den"),
            ("xlnT", "dbg_xlnT", [128, 16 * 512], BF16, ("p1x", "prx")),
            ("stm", "dbg_stm", [128, 8], F32, "p1x"),
            ("rstdc", "dbg_rstdc", [128, 4], F32, "p1x"),
            ("rstdb", "dbg_rstdb", [128, 512], F32, ("p1x", "prx")),
            ("bm", "dbg_bm", [2, 16], F32, "bm"),
            ("pn", "dbg_pn", [128, 8 * QL], BF16, "pn"),
            ("f1", "dbg_f1", [128, 16 * 512], BF16, "ffx"),
            ("f2", "dbg_f2", [128, 64 * 512], BF16, "ffx"),
            ("f3", "dbg_f3", [128, 16 * 512], BF16, "ffx"),
            ("rcp", "dbg_rcp", [16, QL], F32, "pn"),
        ]:
            wl = want if isinstance(want, tuple) else (want,)
            if any(w in dbg for w in wl):
                self.dbg_t[key] = nc.dram_tensor(nm, shape, dt, kind="ExternalOutput").ap()

        self.ln_at = [None, None]
        self.sq_at = [None, None]
        self.krow_at = 0
        self.st = contextlib.ExitStack()
        with self.st:
            st = self.st
            for nm in ["ini", "x", "tk", "sc", "vc", "ap", "tp", "te", "mm", "ax",
                       "cw", "wc", "qe", "ex", "xv", "bn", "sg", "eg", "mk", "pv",
                       "oe", "od", "mh", "pj", "pq", "fw", "fg", "fh", "fy", "fo",
                       "ou", "x2", "rc", "ma", "fz", "fd"]:
                self.sem(nm)

            self.ones_bf = self.sb(st, "ones_bf", [128, 1], BF16)
            self.ones_row = self.sb(st, "ones_row", [1, 128], F32)
            self.eps_t = self.sb(st, "eps_t", [128, 2], F32)
            self.stair_sb = self.sb(st, "stair_sb", [128, 2048], BF16)
            self.ident_sb = self.sb(st, "ident_sb", [128, 128], BF16)
            self.qtab_sb = self.sb(st, "qtab_sb", [1, 128], U32)
            self.zer_bf = self.sb(st, "zer_bf", [128, 512], BF16)
            self.stm = self.sb(st, "stm", [128, 8], F32)
            self.rstd_c = self.sb(st, "rstd_c", [128, 4], F32)
            self.mneg_bf = self.sb(st, "mneg_bf", [128, 2], F32)
            self.rstd_bf = self.sb(st, "rstd_bf", [128, 4], BF16)
            self.ones_rbf = self.sb(st, "ones_rbf", [1, 128], BF16)
            self.bm2 = self.sb(st, "bm2", [2, 16], F32)
            self.bmo = self.sb(st, "bmo", [1, 16], F32)
            self.bm_scr = self.sb(st, "bm_scr", [2, 1], F32)
            self.krm = self.sb(st, "krm", [1, 2], F32)
            self.lnb = self.sb(st, "lnb", [1, 2], F32)
            self.gsc_sb = self.sb(st, "gsc_sb", [128, 64], F32)
            self.usc_sb = self.sb(st, "usc_sb", [128, 64], F32)
            self.dsc_sb = self.sb(st, "dsc_sb", [128, 16], F32)
            self.kvg_sb = self.sb(st, "kvg_sb", [128, 2], F32)
            self.qg_sb = self.sb(st, "qg_sb", [128, 6], F32)

            for nm, dst in [("stair", self.stair_sb), ("ident", self.ident_sb),
                            ("qtab", self.qtab_sb), ("gsc", self.gsc_sb),
                            ("usc", self.usc_sb), ("dsc", self.dsc_sb),
                            ("kv_g", self.kvg_sb), ("q_g", self.qg_sb)]:
                P.dma("sync", lambda E, nm=nm, dst=dst: E.dma_start(
                    out=dst[:], in_=inp[nm]), self.s_ini)
            P.emit("vector", lambda E: E.memset(self.ones_bf[:], 1.0))
            P.emit("vector", lambda E: E.memset(self.ones_row[:], 1.0))
            P.emit("vector", lambda E: E.memset(self.ones_rbf[:], 1.0))
            P.emit("vector", lambda E: E.memset(self.eps_t[:, 0:1], EPS_LN))
            P.emit("vector", lambda E: E.memset(self.eps_t[:, 1:2], EPS_RMS))
            P.emit("vector", lambda E: E.memset(self.zer_bf[:], 0.0))
            P.emit("vector", lambda E: E.memset(self.bm2[:], 0.0))
            P.emit("vector", lambda E: E.memset(self.krm[:], 0.0), inc=self.s_vc)

            dn_sc = contextlib.ExitStack()
            with dn_sc:
              self.denoms = self.sb(dn_sc, "denoms", [16, QL], F32)
              self.recip = self.sb(dn_sc, "recip", [16, QL], F32)
              ca = contextlib.ExitStack()
              with ca:
                  self.ckv_sb = self.sb(ca, "ckv_sb", [128, 2 * P_TOT], BF16)
                  self.kaug = [self.sb(ca, f"kaug{i}", [128, P_TOT], BF16) for i in range(2)]
                  self.v_sb = self.sb(ca, "v_sb", [128, NKS * 130], BF16)
                  self.normq = self.sb(ca, "normq", [128, 6 * QL], BF16)
                  self.tabq_sb = self.sb(ca, "tabq_sb", [128, QL], F32)

                  P.emit("vector", lambda E: E.memset(self.v_sb[:], 1.0))
                  P.emit("vector", lambda E: E.memset(self.kaug[0][96:128, :], 0.0))
                  P.emit("vector", lambda E: E.memset(self.kaug[0][96:97, :], -1.0))
                  P.emit("vector", lambda E: E.memset(self.kaug[1][0:32, :], 0.0))
                  P.emit("vector", lambda E: E.memset(self.kaug[1][0:1, :], -1.0),
                         inc=self.s_vc)

                  self.phase1()
                  if "p1only" not in dbg:
                      self.prep()
                  if "p1only" not in dbg and "noattn" not in dbg:
                      self.attention()

                  if "ckv" in dbg:
                      P.wait("gpsimd", self.s_cw)
                      P.dma("gpsimd", lambda E: E.dma_start(
                          out=self.dbg_t["ckv"], in_=self.ckv_sb[:]), self.s_ou)
                  if "kaug" in dbg:
                      P.wait("gpsimd", self.s_xv)
                      P.dma("gpsimd", lambda E: E.dma_start(
                          out=self.dbg_t["k0"], in_=self.kaug[0][:]), self.s_ou)
                      P.dma("gpsimd", lambda E: E.dma_start(
                          out=self.dbg_t["k1"], in_=self.kaug[1][:]), self.s_ou)
                      P.dma("gpsimd", lambda E: E.dma_start(
                          out=self.dbg_t["vv"], in_=self.v_sb[:]), self.s_ou)
                  if "normq" in dbg and "p1only" not in dbg:
                      P.wait("gpsimd", self.s_qe)
                      P.dma("gpsimd", lambda E: E.dma_start(
                          out=self.dbg_t["normq"], in_=self.normq[:]), self.s_ou)
                  if "bm" in dbg:
                      P.wait("gpsimd", self.s_bn)
                      P.dma("gpsimd", lambda E: E.dma_start(
                          out=self.dbg_t["bm"], in_=self.bm2[:]), self.s_ou)

              if "den" in dbg:
                  P.wait("gpsimd", self.s_od)
                  P.dma("gpsimd", lambda E: E.dma_start(
                      out=self.dbg_t["den"], in_=self.denoms[:]), self.s_ou)
              if "noftail" not in dbg:
                  self.proj()
            if "noftail" not in dbg:
                self.ffn()

            P.wait("gpsimd", self.s_ou)
            P.wait("sync", self.s_x2)

            with nc.Block() as block:
                P.replay(block)
        return nc

    def emit_ln(self, src, dst, dump, pa=0):
        P = self.P
        stm = self.stm[:, pa * 4:pa * 4 + 4]
        rsc = self.rstd_c[:, pa * 2:pa * 2 + 2]
        if self.ln_at[pa] is not None:
            P.wait("scalar", self.s_ap, self.ln_at[pa])
            P.wait("vector", self.s_ap, self.ln_at[pa])
        P.emit("scalar", lambda E: E.activation(
            dump[:], src, AF.Square, accum_out=stm[:, 1:2]), inc=self.s_sc)
        P.emit("vector", lambda E: E.reduce_sum(stm[:, 0:1], src, axis=AX.X),
               inc=self.s_vc)
        P.wait("scalar", self.s_vc)
        P.emit("scalar", lambda E: E.activation(
            stm[:, 2:3], stm[:, 0:1], AF.Square, bias=0.0, scale=1.0 / D),
            inc=self.s_sc)
        P.wait("scalar", self.s_sc)
        P.emit("scalar", lambda E: E.activation(
            stm[:, 2:3], stm[:, 2:3], AF.Identity, bias=0.0, scale=-1.0),
            inc=self.s_sc)
        P.wait("scalar", self.s_sc)
        P.emit("scalar", lambda E: E.activation(
            stm[:, 3:4], stm[:, 1:2], AF.Identity, bias=stm[:, 2:3], scale=1.0 / D),
            inc=self.s_sc)
        P.wait("scalar", self.s_sc)
        P.emit("scalar", lambda E: E.activation(
            rsc[:, 0:1], stm[:, 3:4], AF.Ln, bias=self.eps_t[:, 0:1]),
            inc=self.s_sc)
        P.wait("scalar", self.s_sc)
        P.emit("scalar", lambda E: E.activation(
            rsc[:, 0:1], rsc[:, 0:1], AF.Exp, bias=0.0, scale=-0.5),
            inc=self.s_sc)
        P.wait("vector", self.s_sc)
        P.emit("vector", lambda E: E.scalar_tensor_tensor(
            rsc[:, 1:2], stm[:, 0:1], -1.0 / D, rsc[:, 0:1],
            ALU.mult, ALU.mult), inc=self.s_vc)
        P.wait("vector", self.s_vc)
        P.emit("vector", lambda E: E.tensor_scalar(
            dst, src, rsc[:, 0:1], rsc[:, 1:2], ALU.mult, ALU.add),
            inc=self.s_ap)
        self.ln_at[pa] = self.s_ap.n

    def emit_center(self, src, dst, dump, pa, rslot=None):
        P = self.P
        stm = self.stm[:, pa * 4:pa * 4 + 4]
        if self.ln_at[pa] is not None:
            P.wait("vector", self.s_ap, self.ln_at[pa])
        P.emit("vector", lambda E: E.reduce_sum(stm[:, 0:1], src, axis=AX.X),
               inc=self.s_vc)
        P.wait("scalar", self.s_vc)
        P.emit("scalar", lambda E: E.activation(
            self.mneg_bf[:, pa:pa + 1], stm[:, 0:1], AF.Identity,
            bias=0.0, scale=-1.0 / D), inc=self.s_sc)
        P.wait("scalar", self.s_sc)
        P.wait("scalar", self.s_tp, self.s_tp.n)
        P.emit("scalar", lambda E: E.activation(
            dst, src, AF.Identity, bias=self.mneg_bf[:, pa:pa + 1], scale=1.0),
            inc=self.s_ap)
        self.ln_at[pa] = self.s_ap.n
        return self.s_ap.n

    def emit_rstd(self, dst, dump, pa, rslot):
        P = self.P
        stm = self.stm[:, pa * 4:pa * 4 + 4]
        if self.krow_at:
            P.wait("scalar", self.s_ax, self.krow_at)
        P.emit("scalar", lambda E: E.activation(
            dump[:], dst, AF.Square, accum_out=stm[:, 1:2]), inc=self.s_sc)
        self.sq_at[pa] = self.s_sc.n
        P.wait("scalar", self.s_sc)
        P.emit("scalar", lambda E: E.activation(
            stm[:, 2:3], stm[:, 1:2], AF.Ln, bias=self.eps_t[:, 0:1],
            scale=1.0 / D), inc=self.s_sc)
        P.wait("scalar", self.s_sc)
        P.emit("scalar", lambda E, rslot=rslot: E.activation(
            self.rstd_bf[:, rslot:rslot + 1], stm[:, 2:3], AF.Exp,
            bias=0.0, scale=-0.5), inc=self.s_sc)

    def emit_transpose(self, src_t, xlnT_t, it, tp_ps):
        P = self.P
        for grp in range(4):
            P.wait("tensor", self.s_te, self.s_te.n - 1)
            for j4 in range(4):
                j = grp * 4 + j4
                P.emit("tensor", lambda E, j=j, grp=grp, j4=j4: E.matmul(
                    tp_ps[:, (grp % 2) * 512 + j4 * 128:(grp % 2) * 512 + (j4 + 1) * 128],
                    src_t[:, j * 128:(j + 1) * 128], self.ident_sb[:],
                    start=True, stop=True),
                    inc=(self.s_tp if j4 == 3 else None))
            dst = (xlnT_t[:].rearrange("p (j n) -> p j n", n=512)
                   [:, grp * 4:(grp + 1) * 4, it * 128:(it + 1) * 128])
            srcv = (tp_ps[:, (grp % 2) * 512:((grp % 2) + 1) * 512]
                    .rearrange("p (j n) -> p j n", n=128))
            if grp % 2 == 0 or not SPLIT_EVAC:
                P.wait("vector", self.s_tp)
                P.emit("vector", lambda E, dst=dst, srcv=srcv: E.tensor_copy(
                    dst, srcv), inc=self.s_te)
            else:
                P.wait("scalar", self.s_tp)
                P.emit("scalar", lambda E, dst=dst, srcv=srcv: E.activation(
                    dst, srcv, AF.Copy, bias=0.0, scale=1.0), inc=self.s_te)

    def phase1(self):
        nc, P, inp = self.nc, self.P, self.inp
        p1 = contextlib.ExitStack()
        with p1:
            xst = [self.sb(p1, f"xst{i}", [128, D], BF16) for i in range(2)]
            dump = self.sb(p1, "dump", [128, D], BF16)
            rxrow = [self.sb(p1, f"rxrow{i}", [1, 512], BF16) for i in range(2)]
            xlnbf = [self.sb(p1, f"xlnbf{i}", [128, D], BF16) for i in range(3)]
            xlnT = self.sb(p1, "xlnT", [128, 16 * 512], BF16)
            wckv_sb = self.sb(p1, "wckv_sb", [128, 16 * KVR], BF16)
            wkr_sb = self.sb(p1, "wkr_sb", [128, 16 * 96], BF16)
            tabk_sb = [self.sb(p1, f"tabk_sb{i}", [128, 512], F32) for i in range(2)]
            sqc = self.sb(p1, "sqc", [128, 512], BF16)
            rmsr = self.sb(p1, "rmsr", [1, 512], F32)
            rstdb = self.sb(p1, "rstdb", [128, 512], F32)

            with (nc.psum_tensor("tp_ps", [128, 1024], F32) as tp_ps,
                  nc.psum_tensor("ckv_ps", [128, 1024], F32) as ckv_ps,
                  nc.psum_tensor("kr_ps", [128, 1024], F32) as kr_ps,
                  nc.psum_tensor("ax_ps", [128, 1024], F32) as ax_ps):

                P.dma("gpsimd", lambda E: E.dma_start(
                    out=wckv_sb[:].rearrange("p (j n) -> p j n", n=KVR),
                    in_=inp["w_ckv"].rearrange("(j p) n -> p j n", p=128)), self.s_ini)
                P.dma("gpsimd", lambda E: E.dma_start(
                    out=wkr_sb[:].rearrange("p (j n) -> p j n", n=96),
                    in_=inp["w_kr_pad"].rearrange("(j p) n -> p j n", p=128)), self.s_ini)
                for e in ENGS:
                    P.wait(e, self.s_ini)
                P.wait("tensor", self.s_vc)
                P.wait("scalar", self.s_vc)

                krm_red_done = 0
                rstdb_done = 0
                cache_done = 0
                rope_done = 0
                tabk_mul_done = 0
                rx_done = [0, 0]
                rope_hist = {}

                xc_at = {}

                def center_tile(g):
                    buf = g % 2
                    if g >= 2:
                        P.wait("gpsimd", self.s_ap, self.s_ap.n - 1)
                    P.dma("gpsimd", lambda E, g=g, buf=buf: E.dma_start(
                        out=xst[buf][:], in_=inp["xp"][g * 128:(g + 1) * 128, :]),
                        self.s_x)
                    P.wait("vector", self.s_x)
                    xc_at[g] = self.emit_center(xst[buf][:], xlnbf[g % 3][:],
                                                dump, g % 2)

                def tiles(b):
                    P.wait("sync", self.s_cw, rope_hist.get(b - 2, 0))
                    P.dma("sync", lambda E, b=b: E.dma_start(
                        out=tabk_sb[b % 2][:],
                        in_=inp["tabK"][:, b * 512:(b + 1) * 512]), self.s_tk)
                    for it in range(3):
                        center_tile(b * 4 + it)
                    for it in range(4):
                        g = b * 4 + it
                        P.wait("tensor", self.s_ap, xc_at[g])
                        self.emit_transpose(xlnbf[g % 3], xlnT, it, tp_ps)
                        self.emit_rstd(xlnbf[g % 3][:], dump, g % 2, it)
                        if it == 0:
                            center_tile(b * 4 + 3)

                def mms(b):
                    P.wait("tensor", self.s_sc)
                    if b > 0:
                        P.wait("tensor", self.s_sc, rx_done[(b - 1) % 2])
                        P.wait("tensor", self.s_cw, tabk_mul_done)
                    for it2 in range(4):
                        P.emit("tensor", lambda E, it2=it2: E.matmul(
                            kr_ps[64:65, 512 + it2 * 128:512 + (it2 + 1) * 128],
                            self.rstd_bf[:, it2:it2 + 1], self.ident_sb[:],
                            start=True, stop=True, skip_group_check=True),
                            inc=(self.s_ax if it2 == 3 else None))
                    self.krow_at = self.s_ax.n
                    P.wait("scalar", self.s_ax)
                    P.emit("scalar", lambda E, b=b: E.activation(
                        rxrow[b % 2][:], kr_ps[64:65, 512:1024], AF.Copy,
                        bias=0.0, scale=1.0), inc=self.s_sc)
                    rx_done[b % 2] = self.s_sc.n
                    P.wait("tensor", self.s_te)
                    if b > 0:
                        P.wait("tensor", self.s_cw, cache_done)
                    for ot in range(2):
                        for j in range(16):
                            P.emit("tensor", lambda E, ot=ot, j=j: E.matmul(
                                ckv_ps[:, ot * 512:(ot + 1) * 512],
                                wckv_sb[:, j * KVR + ot * 128:j * KVR + (ot + 1) * 128],
                                xlnT[:, j * 512:(j + 1) * 512],
                                start=(j == 0), stop=(j == 15)),
                                inc=(self.s_mm if j == 15 else None))
                    P.wait("scalar", self.s_mm, self.s_mm.n - 1)
                    P.emit("scalar", lambda E: E.activation(
                        sqc[:], ckv_ps[:, 0:512], AF.Square), inc=self.s_sc)
                    sqa_done = self.s_sc.n
                    if b > 0:
                        P.wait("tensor", self.s_cw, rope_done)
                    for j in range(16):
                        P.emit("tensor", lambda E, j=j: E.matmul(
                            kr_ps[0:96, 0:512],
                            wkr_sb[:, j * 96:j * 96 + 96],
                            xlnT[:, j * 512:(j + 1) * 512],
                            start=(j == 0), stop=(j == 15)),
                            inc=(self.s_mm if j == 15 else None))
                        if j == 2:
                            P.wait("tensor", self.s_sc, sqa_done)
                            P.wait("tensor", self.s_vc, krm_red_done)
                            P.emit("tensor", lambda E: E.matmul(
                                ax_ps[0:1, 0:512], self.ones_bf[:], sqc[:],
                                start=True, stop=False,
                                skip_group_check=True), inc=self.s_ax)
                            P.wait("scalar", self.s_ax)
                            P.emit("scalar", lambda E: E.activation(
                                sqc[:], ckv_ps[:, 512:1024], AF.Square),
                                inc=self.s_sc)
                            sqb_done = self.s_sc.n
                        if j == 8:
                            P.wait("tensor", self.s_sc, sqb_done)
                            P.emit("tensor", lambda E: E.matmul(
                                ax_ps[0:1, 0:512], self.ones_bf[:], sqc[:],
                                start=False, stop=True,
                                skip_group_check=True), inc=self.s_ax)
                            P.wait("scalar", self.s_ax)
                            P.emit("scalar", lambda E: E.activation(
                                rmsr[:], ax_ps[0:1, 0:512], AF.Ln,
                                bias=self.eps_t[0:1, 1:2], scale=1.0 / KVR),
                                inc=self.s_sc)
                            P.wait("scalar", self.s_sc)
                            P.emit("scalar", lambda E: E.activation(
                                rmsr[:], rmsr[:], AF.Exp, bias=0.0, scale=-0.5),
                                inc=self.s_sc)
                    P.wait("tensor", self.s_sc, rx_done[b % 2])
                    P.emit("tensor", lambda E, b=b: E.matmul(
                        kr_ps[0:96, 512:1024], self.ones_rbf[0:1, 0:96],
                        rxrow[b % 2][:],
                        start=True, stop=True, skip_group_check=True), inc=self.s_ax)
                    P.wait("tensor", self.s_sc)
                    P.wait("tensor", self.s_vc, rstdb_done)
                    P.emit("tensor", lambda E: E.matmul(
                        ax_ps[:, 512:1024], self.ones_row[:], rmsr[:],
                        start=True, stop=True, skip_group_check=True),
                        inc=self.s_ax)

                def block_end(b):
                    nonlocal krm_red_done, rstdb_done, cache_done, rope_done
                    nonlocal tabk_mul_done
                    P.wait("vector", self.s_tk)
                    P.wait("vector", self.s_ax)
                    P.emit("vector", lambda E, b=b: E.tensor_mul(
                        tabk_sb[b % 2][32:64, :], tabk_sb[b % 2][32:64, :],
                        kr_ps[32:64, 512:1024]),
                        inc=self.s_cw)
                    P.emit("vector", lambda E, b=b: E.tensor_mul(
                        tabk_sb[b % 2][64:96, :], tabk_sb[b % 2][64:96, :],
                        kr_ps[64:96, 512:1024]),
                        inc=self.s_cw)
                    tabk_mul_done = self.s_cw.n
                    P.emit("vector", lambda E, b=b: E.tensor_mul(
                        self.kaug[0][64:96, b * 512:(b + 1) * 512],
                        kr_ps[64:96, 0:512],
                        tabk_sb[b % 2][64:96, :]), inc=self.s_cw)
                    P.emit("vector", lambda E, b=b: E.tensor_mul(
                        self.kaug[1][32:64, b * 512:(b + 1) * 512],
                        kr_ps[32:64, 0:512],
                        tabk_sb[b % 2][32:64, :]), inc=self.s_cw)
                    rope_done = self.s_cw.n
                    rope_hist[b] = rope_done
                    P.emit("vector", lambda E: E.tensor_copy(
                        rstdb[:], ax_ps[:, 512:1024]), inc=self.s_vc)
                    rstdb_done = self.s_vc.n
                    for ot in range(2):
                        P.emit("vector", lambda E, b=b, ot=ot: E.scalar_tensor_tensor(
                            self.ckv_sb[:, ot * P_TOT + b * 512:ot * P_TOT + (b + 1) * 512],
                            ckv_ps[:, ot * 512:(ot + 1) * 512],
                            self.kvg_sb[:, ot:ot + 1], rstdb[:], ALU.mult, ALU.mult),
                            inc=self.s_cw)
                    cache_done = self.s_cw.n
                    P.wait("scalar", self.s_cw)
                    P.emit("scalar", lambda E, b=b: E.activation(
                        sqc[64:96, :], self.kaug[0][64:96, b * 512:(b + 1) * 512],
                        AF.Square), inc=self.s_sc)
                    P.wait("tensor", self.s_sc)
                    P.emit("tensor", lambda E: E.matmul(
                        ax_ps[0:1, 0:512], self.ones_bf[64:96, :], sqc[64:96, :],
                        start=True, stop=True), inc=self.s_ax)
                    P.wait("vector", self.s_ax)
                    P.emit("vector", lambda E: E.reduce_max(
                        self.krm[0:1, 1:2], ax_ps[0:1, 0:512], axis=AX.X),
                        inc=self.s_vc)
                    krm_red_done = self.s_vc.n
                    P.wait("vector", self.s_vc)
                    P.emit("vector", lambda E: E.tensor_max(
                        self.krm[0:1, 0:1], self.krm[0:1, 0:1], self.krm[0:1, 1:2]),
                        inc=self.s_cw)
                    return

                pend = None
                for b in range(NB):
                    tiles(b)
                    if pend is not None:
                        block_end(pend)
                    mms(b)
                    pend = b
                block_end(pend)
                if "p1x" in self.dbg:
                    P.wait("gpsimd", self.s_cw)
                    P.wait("gpsimd", self.s_ap)
                    for nm2, src2 in [("xlnT", xlnT), ("stm", self.stm),
                                      ("rstdc", self.rstd_c), ("rstdb", rstdb)]:
                        P.dma("gpsimd", lambda E, nm2=nm2, src2=src2: E.dma_start(
                            out=self.dbg_t[nm2], in_=src2[:]), self.s_ou)

    def prep(self):
        nc, P, inp = self.nc, self.P, self.inp
        pr = contextlib.ExitStack()
        with pr:
            xst = [self.sb(pr, f"pxst{i}", [128, D], F32) for i in range(2)]
            dump = self.sb(pr, "pdump", [128, D], BF16)
            xlnbf = [self.sb(pr, f"pxlnbf{i}", [128, D], BF16) for i in range(2)]
            xlnT = self.sb(pr, "pxlnT", [128, 16 * 512], BF16)
            wcq = [self.sb(pr, f"wcq{i}", [128, QR], BF16) for i in range(3)]
            sq2 = [self.sb(pr, f"sq2{i}", [128, 512], BF16) for i in range(2)]
            rmsr2 = self.sb(pr, "rmsr2", [1, 512], F32)
            rstdb2 = self.sb(pr, "rstdb2", [128, 512], F32)

            with (nc.psum_tensor("cq_ps", [128, 6 * 512], F32) as cq_ps,
                  nc.psum_tensor("pax_ps", [128, 1024], F32) as pax_ps):
                self.phase_barrier(("sync", "gpsimd", "tensor"),
                                   ("sc", "vc", "ap", "cw", "ax", "te", "tk"))
                P.dma("sync", lambda E: E.dma_start(
                    out=self.tabq_sb[:], in_=inp["tabQ"]), self.s_ini)
                P.dma("gpsimd", lambda E: E.dma_start(
                    out=wcq[0][:], in_=inp["w_cq"][0:128, :]), self.s_wc)
                self.cq_at = []
                for qb in range(4):
                    for it in range(4):
                        g = qb * 4 + it
                        buf = g % 2
                        if g >= 2:
                            P.wait("sync", self.s_ap, self.s_ap.n - 1)
                        P.dma("sync", lambda E, g=g, buf=buf: E.dma_start(
                            out=xst[buf][:],
                            in_=inp["xp"][(56 + g) * 128:(57 + g) * 128, :]), self.s_x)
                        P.wait("vector", self.s_x)
                        P.wait("vector", self.s_tp, self.s_tp.n - 4)
                        self.emit_center(xst[buf][:], xlnbf[g % 2][:], dump, g % 2)
                        P.wait("tensor", self.s_ap)
                        self.emit_transpose(xlnbf[g % 2], xlnT, it, pax_ps)
                    P.wait("tensor", self.s_te)
                    P.wait("tensor", self.s_qe)
                    for j in range(16):
                        if not (qb == 3 and j == 15):
                            gj = 16 * qb + j + 1
                            if len(self.cq_at) >= 2:
                                P.wait("gpsimd", self.s_pq, self.cq_at[-2])
                            P.dma("gpsimd", lambda E, jn=(j + 1) % 16, gj=gj: E.dma_start(
                                out=wcq[gj % 3][:],
                                in_=inp["w_cq"][jn * 128:(jn + 1) * 128, :]), self.s_wc)
                        P.wait("tensor", self.s_wc, 16 * (16 * qb + j + 1))
                        for m in range(6):
                            P.emit("tensor", lambda E, j=j, m=m, qb=qb: E.matmul(
                                cq_ps[:, m * 512:(m + 1) * 512],
                                wcq[(16 * qb + j) % 3][:, m * 128:(m + 1) * 128],
                                xlnT[:, j * 512:(j + 1) * 512],
                                start=(j == 0), stop=(j == 15)),
                                inc=(self.s_pq if m == 5 else None))
                        self.cq_at.append(self.s_pq.n)
                    P.wait("scalar", self.s_pq)
                    ax_at = []
                    for m in range(6):
                        if m >= 2:
                            P.wait("scalar", self.s_ax, ax_at[m - 2])
                        P.emit("scalar", lambda E, m=m: E.activation(
                            sq2[m % 2][:], cq_ps[:, m * 512:(m + 1) * 512], AF.Square),
                            inc=self.s_sc)
                        P.wait("tensor", self.s_sc)
                        P.emit("tensor", lambda E, m=m: E.matmul(
                            pax_ps[0:1, 0:512], self.ones_bf[:], sq2[m % 2][:],
                            start=(m == 0), stop=(m == 5)), inc=self.s_ax)
                        ax_at.append(self.s_ax.n)
                    P.wait("scalar", self.s_ax)
                    P.emit("scalar", lambda E: E.activation(
                        rmsr2[:], pax_ps[0:1, 0:512], AF.Ln,
                        bias=self.eps_t[0:1, 1:2], scale=1.0 / QR), inc=self.s_sc)
                    P.wait("scalar", self.s_sc)
                    P.emit("scalar", lambda E: E.activation(
                        rmsr2[:], rmsr2[:], AF.Exp, bias=0.0, scale=-0.5), inc=self.s_sc)
                    P.wait("tensor", self.s_sc)
                    P.emit("tensor", lambda E: E.matmul(
                        pax_ps[:, 512:1024], self.ones_row[:], rmsr2[:],
                        start=True, stop=True), inc=self.s_ax)
                    P.wait("vector", self.s_ax)
                    P.emit("vector", lambda E: E.tensor_copy(
                        rstdb2[:], pax_ps[:, 512:1024]), inc=self.s_vc)
                    for m in range(6):
                        P.emit("vector", lambda E, m=m, qb=qb: E.scalar_tensor_tensor(
                            self.normq[:, m * QL + qb * 512:m * QL + (qb + 1) * 512],
                            cq_ps[:, m * 512:(m + 1) * 512],
                            self.qg_sb[:, m:m + 1], rstdb2[:], ALU.mult, ALU.mult),
                            inc=self.s_qe)
                if "prx" in self.dbg:
                    P.wait("gpsimd", self.s_qe)
                    P.dma("gpsimd", lambda E: E.dma_start(
                        out=self.dbg_t["xlnT"], in_=xlnT[:]), self.s_ou)
                    P.dma("gpsimd", lambda E: E.dma_start(
                        out=self.dbg_t["rstdb"], in_=rstdb2[:]), self.s_ou)

    def attention(self):
        nc, P, inp = self.nc, self.P, self.inp
        p2 = contextlib.ExitStack()
        with p2:
            qcur = [self.sb(p2, f"qcur{i}", [128, QL], BF16) for i in range(2)]
            pp = [self.sb(p2, f"pp{i}", [128, 2048], BF16) for i in range(2)]
            qsq = self.sb(p2, "qsq", [128, QL], BF16)
            dn_tmp = self.sb(p2, "dn_tmp", [128, QL], F32)
            aostg = self.sb(p2, "aostg", [64, QL], BF16)
            sqb = [self.sb(p2, f"sqb{i}", [128, 512], BF16) for i in range(2)]
            wvec = self.sb(p2, "wvec", [128, 2], BF16)
            wkn_sb = self.sb(p2, "wkn_sb", [128, 2 * 1024], BF16)
            wv_sb = self.sb(p2, "wv_sb", [128, 2 * 1024], BF16)
            wqn_sb = self.sb(p2, "wqn_sb", [128, 6 * 1024], BF16)
            wqr_sb = self.sb(p2, "wqr_sb", [128, 6 * 576], BF16)

            self.phase_barrier(("sync", "gpsimd", "tensor", "vector", "scalar"),
                               ("sc", "vc", "ap", "qe", "ax", "te", "pq", "mm"))
            P.emit("vector", lambda E: E.memset(wvec[:], 0.0))
            P.emit("vector", lambda E: E.memset(qcur[0][96:128, :], 0.0))
            P.emit("vector", lambda E: E.memset(qcur[1][0:32, :], 0.0))
            P.emit("vector", lambda E: E.memset(wvec[0:64, 0:1], float(LAM * LAM)))
            P.emit("vector", lambda E: E.memset(wvec[64:128, 1:2], float(LAM * LAM)),
                   inc=self.s_vc)
            for nm, dst, inner in [("w_kn", wkn_sb, 1024), ("w_v", wv_sb, 1024),
                                   ("w_qn", wqn_sb, 1024), ("w_qr_pad", wqr_sb, 576)]:
                P.dma("gpsimd", lambda E, nm=nm, dst=dst, inner=inner: E.dma_start(
                    out=dst[:].rearrange("p (j n) -> p j n", n=inner),
                    in_=inp[nm].rearrange("(j p) n -> p j n", p=128)), self.s_wc)

            with (nc.psum_tensor("s_ps0", [128, 1024], F32) as s0,
                  nc.psum_tensor("s_ps1", [128, 1024], F32) as s1,
                  nc.psum_tensor("o_ps", [65, 2048], F32) as o_ps):
                self.S = [s0, s1]
                P.wait("tensor", self.s_wc)
                P.wait("tensor", self.s_qe)
                P.wait("tensor", self.s_cw)
                P.wait("tensor", self.s_vc)
                self.selreg = nc.alloc_register(mybir.EngineType.PE, name="chsel")
                self.reg_free = [(None, 0)] * 4
                self.pv_done = {}
                self.eg_hist = []
                self.git_base = 0
                self.mh_qsq_at = []
                self.od_at = []

                mrow = dn_tmp[0:1, :]
                for p in range(H // 2):
                    self.emit_pair(p, qcur, sqb, wvec, wkn_sb, wv_sb,
                                   wqn_sb, wqr_sb)
                    self.emit_mhat(2 * p, 0, p, qcur, qsq, mrow)
                    self.emit_mhat(2 * p + 1, 1, p, qcur, qsq, mrow)
                    if "qcur" in self.dbg and p == 0:
                        P.wait("gpsimd", self.s_mh)
                        P.dma("gpsimd", lambda E: E.dma_start(
                            out=self.dbg_t["q0"], in_=qcur[0][:]), self.s_ou)
                        P.dma("gpsimd", lambda E: E.dma_start(
                            out=self.dbg_t["q1"], in_=qcur[1][:]), self.s_ou)
                    self.emit_head(2 * p, 0, qcur, pp, o_ps, dn_tmp, aostg)
                    self.emit_head(2 * p + 1, 1, qcur, pp, o_ps, dn_tmp, aostg)

        P.wait("vector", self.s_od)
        P.emit("vector", lambda E: E.reciprocal(self.recip[:], self.denoms[:]),
               inc=self.s_rc)

    def phase_barrier(self, engines, sem_names):
        for e in engines:
            for nm in sem_names:
                self.P.wait(e, getattr(self, "s_" + nm))

    def sreg(self, r, p0=0, p1=128, c0=0, w=512):
        base = (r % 2) * 512 + c0
        return self.S[r // 2][p0:p1, base:base + w]

    def use_region(self, r):
        sem, tgt = self.reg_free[r]
        if sem is not None:
            self.P.wait("tensor", sem, tgt)

    def release_region(self, r, sem):
        self.reg_free[r] = (sem, sem.n)

    def emit_pair(self, p, qcur, sqb, wvec, wkn_sb, wv_sb, wqn_sb, wqr_sb):
        P = self.P
        kaug, v_sb, ckv, normq = self.kaug, self.v_sb, self.ckv_sb, self.normq
        sq_hist = {}
        bm_hist = {}

        def emit_bm(bb):
            r = bb % 4
            P.wait("tensor", self.s_sc, sq_hist[bb])
            P.emit("tensor", lambda E, r=r, bb=bb: E.matmul(
                self.sreg(r, 0, 2),
                wvec[:], sqb[bb % 2][:], start=True, stop=True,
                skip_group_check=True), inc=self.s_bn)
            P.wait("vector", self.s_bn)
            P.emit("vector", lambda E, r=r: E.reduce_max(
                self.bm_scr[:], self.sreg(r, 0, 2), axis=AX.X),
                inc=self.s_vc)
            P.wait("vector", self.s_vc)
            P.emit("vector", lambda E, p=p: E.tensor_max(
                self.bm2[:, p:p + 1], self.bm2[:, p:p + 1], self.bm_scr[:]),
                inc=self.s_bn)
            bm_hist[bb] = self.s_bn.n
            self.release_region(r, self.s_bn)

        for blk in range(NB):
            r = blk % 4
            self.use_region(r)
            for ot in range(2):
                P.emit("tensor", lambda E, blk=blk, ot=ot, r=r, p=p: E.matmul(
                    self.sreg(r),
                    wkn_sb[:, ot * 1024 + p * 128:ot * 1024 + (p + 1) * 128],
                    ckv[:, ot * P_TOT + blk * 512:ot * P_TOT + (blk + 1) * 512],
                    start=(ot == 0), stop=(ot == 1), skip_group_check=True),
                    inc=(self.s_ex if ot == 1 else None))
            P.wait("vector", self.s_ex)
            P.emit("vector", lambda E, blk=blk, r=r: E.tensor_scalar(
                kaug[0][0:64, blk * 512:(blk + 1) * 512],
                self.sreg(r, 0, 64), LAM, None, ALU.mult))
            P.emit("vector", lambda E, blk=blk, r=r: E.tensor_scalar(
                kaug[1][64:128, blk * 512:(blk + 1) * 512],
                self.sreg(r, 64, 128), LAM, None, ALU.mult),
                inc=self.s_xv)
            P.wait("scalar", self.s_xv)
            if blk >= 2:
                P.wait("scalar", self.s_bn, bm_hist.get(blk - 2, 0))
            P.emit("scalar", lambda E, r=r, blk=blk: E.activation(
                sqb[blk % 2][:], self.sreg(r), AF.Square), inc=self.s_sc)
            sq_hist[blk] = self.s_sc.n
            if blk >= 1:
                emit_bm(blk - 1)
        emit_bm(NB - 1)

        for gk in range(NB):
            r = gk % 4
            self.use_region(r)
            for sub in range(4):
                kt = gk * 4 + sub
                for ot in range(2):
                    P.emit("tensor", lambda E, kt=kt, sub=sub, ot=ot, r=r, p=p: E.matmul(
                        self.sreg(r, c0=sub * 128, w=128),
                        ckv[:, ot * P_TOT + kt * 128:ot * P_TOT + (kt + 1) * 128],
                        wv_sb[:, ot * 1024 + p * 128:ot * 1024 + (p + 1) * 128],
                        start=(ot == 0), stop=(ot == 1), skip_group_check=True),
                        inc=(self.s_ex if (ot == 1 and sub == 3) else None))
            P.wait("vector", self.s_ex)
            for lane in range(2):
                P.emit("vector", lambda E, gk=gk, r=r, lane=lane: E.tensor_copy(
                    v_sb[:].rearrange("p (k c) -> p k c", c=130)
                    [:, gk * 4:(gk + 1) * 4, lane * 65:lane * 65 + 64],
                    self.sreg(r)
                    .rearrange("p (k c) -> p k c", c=128)
                    [:, :, lane * 64:(lane + 1) * 64]),
                    inc=(self.s_xv if lane == 1 else None))
            self.release_region(r, self.s_xv)

        for qc in range(4):
            r = qc
            self.use_region(r)
            for m in range(6):
                P.emit("tensor", lambda E, qc=qc, m=m, r=r, p=p: E.matmul(
                    self.sreg(r),
                    wqn_sb[:, m * 1024 + p * 128:m * 1024 + (p + 1) * 128],
                    normq[:, m * QL + qc * 512:m * QL + (qc + 1) * 512],
                    start=(m == 0), stop=(m == 5), skip_group_check=True),
                    inc=(self.s_ex if m == 5 else None))
            P.wait("vector", self.s_ex)
            P.emit("vector", lambda E, qc=qc, r=r: E.tensor_scalar(
                qcur[0][0:64, qc * 512:(qc + 1) * 512],
                self.sreg(r, 0, 64), LAM, None, ALU.mult))
            P.emit("vector", lambda E, qc=qc, r=r: E.tensor_scalar(
                qcur[1][64:128, qc * 512:(qc + 1) * 512],
                self.sreg(r, 64, 128), LAM, None, ALU.mult),
                inc=self.s_qe)
            self.release_region(r, self.s_qe)
        for par in range(2):
            hh = 2 * p + par
            if par == 0:
                c0, rl, rh, mw = hh * 32, 64, 96, 96
            else:
                c0, rl, rh, mw = 32 + hh * 32, 32, 64, 64
            for qc in range(4):
                r = qc
                self.use_region(r)
                for m in range(6):
                    P.emit("tensor", lambda E, qc=qc, m=m, r=r, c0=c0, mw=mw: E.matmul(
                        self.sreg(r, 0, mw),
                        wqr_sb[:, m * 576 + c0:m * 576 + c0 + mw],
                        normq[:, m * QL + qc * 512:m * QL + (qc + 1) * 512],
                        start=(m == 0), stop=(m == 5), skip_group_check=True),
                        inc=(self.s_ex if m == 5 else None))
                P.wait("vector", self.s_ex)
                P.emit("vector", lambda E, qc=qc, r=r, par=par, rl=rl, rh=rh: E.tensor_mul(
                    qcur[par][rl:rh, qc * 512:(qc + 1) * 512],
                    self.sreg(r, rl, rh),
                    self.tabq_sb[rl:rh, qc * 512:(qc + 1) * 512]), inc=self.s_qe)
                self.release_region(r, self.s_qe)
        P.wait("gpsimd", self.s_bn)
        P.dma("gpsimd", lambda E, p=p: E.dma_start(
            out=self.bmo[0:1, p:p + 1], in_=self.bm2[1:2, p:p + 1]), self.s_od)

    def emit_mhat(self, h, par, p, qcur, qsq, mrow):
        P = self.P
        rl, rh = (0, 96) if par == 0 else (32, 128)
        arow = 96 if par == 0 else 0
        if par == 0:
            bsrc = self.bm2[0:1, p:p + 1]
            P.wait("vector", self.s_bn)
        else:
            P.wait("vector", self.s_od)
            bsrc = self.bmo[0:1, p:p + 1]
        P.emit("vector", lambda E, bsrc=bsrc: E.tensor_add(
            self.lnb[0:1, 0:1], bsrc, self.krm[0:1, 0:1]), inc=self.s_vc)
        P.wait("vector", self.s_qe)
        if len(self.mh_qsq_at) >= 1:
            P.wait("vector", self.s_mh, self.mh_qsq_at[-1])
        qpieces = [(rl, rh)] if rl == 0 else [(32, 64), (64, 128)]
        for pi, (a0, a1) in enumerate(qpieces):
            P.emit("vector", lambda E, par=par, a0=a0, a1=a1: E.tensor_mul(
                qsq[a0:a1, :], qcur[par][a0:a1, :], qcur[par][a0:a1, :]),
                inc=(self.s_vc if pi == len(qpieces) - 1 else None))
        P.wait("scalar", self.s_vc)
        P.emit("scalar", lambda E: E.activation(
            self.lnb[0:1, 1:2], self.lnb[0:1, 0:1], AF.Ln), inc=self.s_sc)
        P.wait("scalar", self.s_sc)
        P.emit("scalar", lambda E: E.activation(
            self.lnb[0:1, 1:2], self.lnb[0:1, 1:2], AF.Copy, bias=0.0, scale=0.5),
            inc=self.s_sc)
        for qc in range(4):
            self.use_region(qc)
            P.wait("tensor", self.s_vc)
            pieces = [(rl, rh)] if rl == 0 else [(32, 64), (64, 128)]
            for pi, (a0, a1) in enumerate(pieces):
                P.emit("tensor", lambda E, qc=qc, a0=a0, a1=a1, pi=pi,
                       np_=len(pieces): E.matmul(
                    self.sreg(qc, 0, 1),
                    self.ones_bf[a0:a1, :], qsq[a0:a1, qc * 512:(qc + 1) * 512],
                    start=(pi == 0), stop=(pi == np_ - 1), skip_group_check=True),
                    inc=(self.s_mh if pi == len(pieces) - 1 else None))
        self.mh_qsq_at.append(self.s_mh.n)
        P.wait("scalar", self.s_mh)
        P.emit("scalar", lambda E: E.activation(
            mrow[0:1, 0:1024], self.S[0][0:1, 0:1024], AF.Ln), inc=self.s_sc)
        P.emit("scalar", lambda E: E.activation(
            mrow[0:1, 1024:2048], self.S[1][0:1, 0:1024], AF.Ln), inc=self.s_sc)
        P.wait("scalar", self.s_sc)
        P.emit("scalar", lambda E, par=par, arow=arow: E.activation(
            qcur[par][arow:arow + 1, :], mrow[:], AF.Exp,
            bias=self.lnb[0:1, 1:2], scale=0.5), inc=self.s_mh)
        for r in range(4):
            self.release_region(r, self.s_mh)

    def emit_head(self, h, par, qcur, pp, o_ps, dn_tmp, aostg):
        P = self.P
        kaug, v_sb = self.kaug, self.v_sb
        gbase = self.git_base
        P.wait("tensor", self.s_oe)
        for r in range(4):
            P.emit("tensor", lambda E, r=r: E.matmul(
                o_ps[0:65, r * 512:(r + 1) * 512],
                self.zer_bf[:, 0:65], self.zer_bf[:, 0:512],
                start=True, stop=False, skip_group_check=True), inc=self.s_pv)
        P.wait("tensor", self.s_mh)
        P.wait("tensor", self.s_qe)
        P.wait("tensor", self.s_xv)

        class Sub:

            def __init__(self, sems):
                self.ops = []
                self.sems = sems

            def emit(self, fn, inc=None, k=1, count=True):
                if inc is None:
                    self.ops.append(fn)
                    return

                def g(E, fn=fn, sem=inc, k=k):
                    fn(E).then_inc(sem.h, k)
                self.ops.append(g)
                if count:
                    inc.n += k

            def wait(self, sem, val):
                if val > 0:
                    self.ops.append(lambda E, hh=sem.h, v=val: E.wait_ge(hh, v))

        def pps(t, hf=None):
            j = t % 4
            buf = pp[j // 2]
            c0 = (j % 2) * 1024
            if hf is None:
                return buf[:, c0:c0 + 1024]
            return buf[:, c0 + hf * 512:c0 + hf * 512 + 512]

        def emit_exp(t, hf=None):
            P.wait("scalar", self.s_sg)
            gi = gbase + t
            if gi - 4 in self.pv_done:
                P.wait("scalar", self.s_pv, self.pv_done[gi - 4])
            if hf is None:
                P.emit("scalar", lambda E, t=t: E.activation(
                    pps(t), self.S[t % 2][:, 0:1024], AF.Exp), inc=self.s_eg)
            else:
                P.emit("scalar", lambda E, t=t, hf=hf: E.activation(
                    pps(t, hf), self.S[t % 2][:, hf * 512:(hf + 1) * 512],
                    AF.Exp), inc=self.s_eg)
            self.eg_hist.append(self.s_eg.n)

        for sg in range(7):
            bodyA = Sub(self)
            bodyB = Sub(self)
            sinks = [bodyA, bodyB]

            def rec_pv(t):
                for s in sinks:
                    s.wait(self.s_eg, self.eg_hist[-1])
                for hf in range(2):
                    for si, s in enumerate(sinks):
                        cqv = si * 2 + hf
                        s.emit(lambda E, t=t, par=par, cqv=cqv, hf=hf: E.matmul(
                            o_ps[0:65, cqv * 512:(cqv + 1) * 512],
                            v_sb[:, t * 130 + par * 65:t * 130 + par * 65 + 65],
                            pps(t, hf),
                            start=False, stop=False, skip_group_check=True),
                            inc=(self.s_pv if hf == 1 else None), count=(si == 0))
                self.pv_done[gbase + t] = self.s_pv.n

            for i in range(8):
                t = sg * 8 + i
                if len(self.eg_hist) >= 2:
                    for s in sinks:
                        s.wait(self.s_eg, self.eg_hist[-2])
                for hf in range(2):
                    for si, s in enumerate(sinks):
                        cqv = si * 2 + hf
                        s.emit(lambda E, t=t, par=par, cqv=cqv, hf=hf: E.matmul(
                            self.sreg((t % 2) * 2 + hf),
                            kaug[par][:, t * 128:(t + 1) * 128],
                            qcur[par][:, cqv * 512:(cqv + 1) * 512],
                            start=True, stop=True, skip_group_check=True),
                            inc=(self.s_sg if hf == 1 else None), count=(si == 0))
                if i > 0:
                    rec_pv(t - 1)
                emit_exp(t)
            rec_pv(sg * 8 + 7)

            def run_branch(E, sg=sg, A=bodyA.ops, B=bodyB.ops):
                E.reg_load(self.selreg, self.qtab_sb[0:1, sg:sg + 1])
                with E.If_eq(self.selreg, 0):
                    for f in A:
                        f(E)
                with E.Else():
                    for f in B:
                        f(E)
            P.q["tensor"].append(run_branch)

        def iter_info(t):
            i = t - NINT
            cq = 0 if i < 8 else 2
            rr = (i % 8) * 128
            if rr < 512:
                return [(0, cq, 1024 - rr), (1, cq + 1, None)]
            return [(1, cq + 1, 1024 - (rr - 512))]

        def rec_pv_diag(prev):
            t, info, (sem, val) = prev
            P.wait("tensor", sem, val)
            for idx, (hf, cqv, moff) in enumerate(info):
                last = idx == len(info) - 1
                P.emit("tensor", lambda E, t=t, par=par, hf=hf, cqv=cqv: E.matmul(
                    o_ps[0:65, cqv * 512:(cqv + 1) * 512],
                    v_sb[:, t * 130 + par * 65:t * 130 + par * 65 + 65],
                    pps(t, hf),
                    start=False, stop=False, skip_group_check=True),
                    inc=(self.s_pv if last else None))
            self.pv_done[gbase + t] = self.s_pv.n

        prev = None
        for t in range(NINT, NKS):
            info = iter_info(t)
            if len(self.eg_hist) >= 2:
                P.wait("tensor", self.s_eg, self.eg_hist[-2])
            for idx, (hf, cqv, moff) in enumerate(info):
                last = idx == len(info) - 1
                P.emit("tensor", lambda E, t=t, par=par, hf=hf, cqv=cqv: E.matmul(
                    self.sreg((t % 2) * 2 + hf),
                    kaug[par][:, t * 128:(t + 1) * 128],
                    qcur[par][:, cqv * 512:(cqv + 1) * 512],
                    start=True, stop=True, skip_group_check=True),
                    inc=(self.s_sg if last else None))
            if prev is not None:
                rec_pv_diag(prev)
            emit_exp(t, hf=(info[0][0] if len(info) == 1 else None))
            ready = (self.s_eg, self.eg_hist[-1])
            nm = 0
            for (hf, cqv, moff) in info:
                if moff is None:
                    continue
                P.wait("vector", self.s_eg)
                P.emit("vector", lambda E, t=t, hf=hf, moff=moff: E.tensor_mul(
                    pps(t, hf), pps(t, hf),
                    self.stair_sb[:, moff:moff + 512]), inc=self.s_mk)
                nm += 1
            if nm:
                ready = (self.s_mk, self.s_mk.n)
            prev = (t, info, ready)
        rec_pv_diag(prev)
        self.git_base += NKS
        for r in range(4):
            self.release_region(r, self.s_eg)

        P.wait("vector", self.s_pv)
        if len(self.od_at) >= 1:
            P.wait("vector", self.s_od, self.od_at[-1])
        P.emit("vector", lambda E, h=h: E.tensor_copy(
            aostg[:], o_ps[0:64, :]), inc=self.s_oe)
        P.emit("vector", lambda E: E.tensor_copy(
            dn_tmp[64:65, :], o_ps[64:65, :]), inc=self.s_oe)
        P.wait("gpsimd", self.s_oe)
        P.dma("gpsimd", lambda E, h=h: E.dma_start(
            out=self.ao_d[h], in_=aostg[:]), self.s_od)
        P.dma("gpsimd", lambda E, h=h: E.dma_start(
            out=self.denoms[h:h + 1, :], in_=dn_tmp[64:65, :]), self.s_od)
        self.od_at.append(self.s_od.n)

    def proj(self):
        nc, P, inp = self.nc, self.P, self.inp
        pj = contextlib.ExitStack()
        with pj:
            attn = self.sb(pj, "attn_all", [128, 8 * QL], BF16)
            wpj = self.sb(pj, "wproj_sb", [128, 8 * D], BF16)
            xown = [self.sb(pj, f"xown{i}", [128, D], F32) for i in range(2)]
            self.rrow = self.sb(pj, "rrow", [1, QL], F32)
            x2st = [self.sb(pj, f"x2st{i}", [128, D], F32) for i in range(2)]
            with (nc.psum_tensor("pj_ps", [128, 2048], F32) as pj_ps,
                  nc.psum_tensor("bc_ps", [128, 1024], F32) as bc_ps):
                self.phase_barrier(("sync", "gpsimd", "tensor"),
                                   ("eg", "pv", "oe", "od", "mh", "xv", "bn",
                                    "mk", "qe", "rc"))
                P.dma("gpsimd", lambda E: E.dma_start(
                    out=wpj[:].rearrange("p (j n) -> p j n", n=D),
                    in_=inp["w_proj"].rearrange("(j p) n -> p j n", p=128)), self.s_wc)
                P.wait("gpsimd", self.s_od)
                for j in range(8):
                    for sub in range(2):
                        P.dma("gpsimd", lambda E, j=j, sub=sub: E.dma_start(
                            out=attn[sub * 64:(sub + 1) * 64, j * QL:(j + 1) * QL],
                            in_=self.ao_d[2 * j + sub]), self.s_pj)
                P.wait("gpsimd", self.s_rc)
                ma_at = []
                for hh in range(H):
                    P.wait("gpsimd", self.s_pq)
                    P.dma("gpsimd", lambda E, hh=hh: E.dma_start(
                        out=self.rrow[:], in_=self.recip[hh:hh + 1, :]), self.s_pj)
                    P.wait("tensor", self.s_pj)
                    for k4 in range(4):
                        reg = k4 % 2
                        if len(ma_at) >= 2:
                            P.wait("tensor", self.s_ma, ma_at[-2])
                        P.emit("tensor", lambda E, k4=k4, reg=reg: E.matmul(
                            bc_ps[:, reg * 512:(reg + 1) * 512],
                            self.ones_row[:],
                            self.rrow[0:1, k4 * 512:(k4 + 1) * 512],
                            start=True, stop=True, skip_group_check=True),
                            inc=self.s_pq)
                        P.wait("vector", self.s_pq)
                        r0 = (hh % 2) * 64
                        P.emit("vector", lambda E, hh=hh, k4=k4, reg=reg, r0=r0:
                               E.tensor_mul(
                            attn[r0:r0 + 64,
                                 (hh // 2) * QL + k4 * 512:(hh // 2) * QL + (k4 + 1) * 512],
                            attn[r0:r0 + 64,
                                 (hh // 2) * QL + k4 * 512:(hh // 2) * QL + (k4 + 1) * 512],
                            bc_ps[r0:r0 + 64, reg * 512:(reg + 1) * 512]),
                            inc=self.s_ma)
                        ma_at.append(self.s_ma.n)
                if "pn" in self.dbg:
                    P.wait("gpsimd", self.s_ma)
                    P.dma("gpsimd", lambda E: E.dma_start(
                        out=self.dbg_t["pn"], in_=attn[:]), self.s_ou)
                    P.dma("gpsimd", lambda E: E.dma_start(
                        out=self.dbg_t["rcp"], in_=self.recip[:]), self.s_ou)
                P.wait("tensor", self.s_ma)
                P.wait("tensor", self.s_wc)
                adds_at = []
                for tt in range(16):
                    if tt >= 2:
                        P.wait("sync", self.s_vc, adds_at[tt - 2])
                    P.dma("sync", lambda E, tt=tt: E.dma_start(
                        out=xown[tt % 2][:],
                        in_=inp["xp"][(56 + tt) * 128:(57 + tt) * 128, :]), self.s_x)
                    P.wait("tensor", self.s_vc)
                    for j in range(8):
                        for dc in range(4):
                            P.emit("tensor", lambda E, tt=tt, j=j, dc=dc: E.matmul(
                                pj_ps[:, dc * 512:(dc + 1) * 512],
                                attn[:, j * QL + tt * 128:j * QL + (tt + 1) * 128],
                                wpj[:, j * D + dc * 512:j * D + (dc + 1) * 512],
                                start=(j == 0), stop=(j == 7)),
                                inc=(self.s_mm if j == 7 and dc == 3 else None))
                    P.wait("vector", self.s_mm)
                    P.wait("vector", self.s_x)
                    if tt >= 2:
                        P.wait("vector", self.s_x2, 16 * (tt - 1))
                    P.emit("vector", lambda E, tt=tt: E.tensor_add(
                        x2st[tt % 2][:], pj_ps[:], xown[tt % 2][:]), inc=self.s_vc)
                    adds_at.append(self.s_vc.n)
                    P.wait("sync", self.s_vc)
                    P.dma("sync", lambda E, tt=tt: E.dma_start(
                        out=self.x2_d[tt * 128:(tt + 1) * 128, :],
                        in_=x2st[tt % 2][:]), self.s_x2)

    def ffn(self):
        nc, P, inp = self.nc, self.P, self.inp
        ff = contextlib.ExitStack()
        with ff:
            x2t = [self.sb(ff, f"fx2t{i}", [128, D], F32) for i in range(2)]
            dump = self.sb(ff, "fdump", [128, D], BF16)
            xlnbf = [self.sb(ff, f"fxlnbf{i}", [128, D], BF16) for i in range(2)]
            xlnT = self.sb(ff, "fxlnT", [128, 16 * 512], BF16)
            hbuf = self.sb(ff, "hbuf", [128, 64 * 512], BF16)
            wg8 = [self.sb(ff, f"wg8{i}", [128, 2048], U8) for i in range(2)]
            wu8 = [self.sb(ff, f"wu8{i}", [128, 2048], U8) for i in range(2)]
            wgb = [self.sb(ff, f"wgb{i}", [128, 2048], BF16) for i in range(2)]
            wub = [self.sb(ff, f"wub{i}", [128, 2048], BF16) for i in range(2)]
            wd8 = self.sb(ff, "wd8", [128, 8192], U8)
            wdb = [self.sb(ff, f"wdb{i}", [128, 4096], BF16) for i in range(2)]
            sil = self.sb(ff, "sil", [128, 512], F32)
            ytmp = self.sb(ff, "ytmp", [128, 16 * 512], BF16)
            x3st = [self.sb(ff, f"x3st{i}", [128, D], F32) for i in range(2)]

            with (nc.psum_tensor("big_ps", [128, 2048], F32) as big_ps,
                  nc.psum_tensor("g_ps", [128, 512], F32) as g_ps,
                  nc.psum_tensor("u_ps", [128, 512], F32) as u_ps,
                  nc.psum_tensor("y_ps", [128, 512], F32) as y_ps):
                self.phase_barrier(("sync", "gpsimd", "tensor"),
                                    ("x2", "vc", "mm", "ma", "pq", "x"))
                fo_at = []
                fw0 = self.s_fw.n
                fz0 = self.s_fz.n
                te_mark = [0]
                self.fd_marks = []

                def stage_a(pa):
                    for it in range(4):
                        g = pa * 4 + it
                        buf = g % 2
                        if g >= 2:
                            P.wait("gpsimd", self.s_ap, self.s_ap.n - 1)
                            if fo_at:
                                P.wait("gpsimd", self.s_fo, fo_at[-1])
                        P.dma("gpsimd", lambda E, g=g, buf=buf: E.dma_start(
                            out=x2t[buf][:],
                            in_=self.x2_d[g * 128:(g + 1) * 128, :]), self.s_x)
                        P.wait("scalar", self.s_x)
                        P.wait("vector", self.s_x)
                        P.wait("vector", self.s_tp, self.s_tp.n - 4)
                        self.emit_ln(x2t[buf][:], xlnbf[g % 2][:], dump, g % 2)
                        P.wait("tensor", self.s_ap)
                        if fo_at:
                            P.wait("tensor", self.s_fo, fo_at[-1])
                        self.emit_transpose(xlnbf[g % 2], xlnT, it, big_ps)
                    te_mark[0] = self.s_te.n

                def stage_b(pa):
                    P.wait("tensor", self.s_te)
                    P.dma("sync", lambda E: E.dma_start(
                        out=wg8[0][:], in_=inp["wg_t"][0]), self.s_fw)
                    P.dma("sync", lambda E: E.dma_start(
                        out=wu8[0][:], in_=inp["wu_t"][0]), self.s_fw)
                    for i in range(64):
                        if i + 1 < 64:
                            P.wait("sync", self.s_sc, self.cast_at[-1]
                                   if getattr(self, "cast_at", None) else 0)
                            P.dma("sync", lambda E, i=i: E.dma_start(
                                out=wg8[(i + 1) % 2][:], in_=inp["wg_t"][i + 1]),
                                self.s_fw)
                            P.dma("sync", lambda E, i=i: E.dma_start(
                                out=wu8[(i + 1) % 2][:], in_=inp["wu_t"][i + 1]),
                                self.s_fw)
                        P.wait("scalar", self.s_fw, fw0 + 32 * (64 * pa + i + 1))
                        P.wait("scalar", self.s_fg, self.s_fg.n - 2)
                        P.emit("scalar", lambda E, i=i: E.activation(
                            wgb[i % 2][:], wg8[i % 2][:], AF.Copy, bias=-128.0,
                            scale=1.0))
                        P.emit("scalar", lambda E, i=i: E.activation(
                            wub[i % 2][:], wu8[i % 2][:], AF.Copy, bias=-128.0,
                            scale=1.0), inc=self.s_sc)
                        if not hasattr(self, "cast_at"):
                            self.cast_at = []
                        self.cast_at.append(self.s_sc.n)
                        P.wait("tensor", self.s_sc)
                        P.wait("tensor", self.s_fh)
                        for j in range(16):
                            P.emit("tensor", lambda E, i=i, j=j: E.matmul(
                                g_ps[:], wgb[i % 2][:, j * 128:(j + 1) * 128],
                                xlnT[:, j * 512:(j + 1) * 512],
                                start=(j == 0), stop=(j == 15)),
                                inc=(self.s_fg if j == 15 else None))
                        for j in range(16):
                            P.emit("tensor", lambda E, i=i, j=j: E.matmul(
                                u_ps[:], wub[i % 2][:, j * 128:(j + 1) * 128],
                                xlnT[:, j * 512:(j + 1) * 512],
                                start=(j == 0), stop=(j == 15)),
                                inc=(self.s_fg if j == 15 else None))
                        P.wait("scalar", self.s_fg, self.s_fg.n - 1)
                        P.emit("scalar", lambda E, i=i: E.activation(
                            sil[:], g_ps[:], AF.Silu, bias=0.0,
                            scale=self.gsc_sb[:, i:i + 1]), inc=self.s_sc)
                        P.wait("vector", self.s_sc)
                        P.wait("vector", self.s_fg)
                        P.emit("vector", lambda E, i=i: E.scalar_tensor_tensor(
                            hbuf[:, i * 512:(i + 1) * 512], u_ps[:],
                            self.usc_sb[:, i:i + 1], sil[:], ALU.mult, ALU.mult),
                            inc=self.s_fh)

                def stage_c(pa):
                    P.dma("sync", lambda E: E.dma_start(
                        out=wd8[:], in_=inp["wd_t"][0]), self.s_fz)
                    dcast_at = []
                    for ht in range(16):
                        P.wait("scalar", self.s_fz, fz0 + 16 * (16 * pa + ht + 1))
                        for half in range(2):
                            if self.fd_marks:
                                sem, val = self.fd_marks[-1][half]
                                P.wait("scalar", sem, val)
                            P.emit("scalar", lambda E, ht=ht, half=half: E.activation(
                                wdb[half][:], wd8[:, half * 4096:(half + 1) * 4096],
                                AF.Copy, bias=-128.0, scale=1.0), inc=self.s_sc)
                            dcast_at.append(self.s_sc.n)
                        if ht + 1 < 16:
                            P.wait("sync", self.s_sc, dcast_at[-1])
                            P.dma("sync", lambda E, ht=ht: E.dma_start(
                                out=wd8[:], in_=inp["wd_t"][ht + 1]), self.s_fz)
                        P.wait("tensor", self.s_sc, dcast_at[-2])
                        P.wait("tensor", self.s_fh)
                        P.wait("tensor", self.s_fy)
                        marks = [None, None]
                        for j in range(64):
                            if j == 32:
                                P.wait("tensor", self.s_sc, dcast_at[-1])
                            P.emit("tensor", lambda E, ht=ht, j=j: E.matmul(
                                y_ps[:], wdb[j // 32][:, (j % 32) * 128:(j % 32 + 1) * 128],
                                hbuf[:, j * 512:(j + 1) * 512],
                                start=(j == 0), stop=(j == 63)),
                                inc=(self.s_fd if j == 31 else
                                     (self.s_fy if j == 63 else None)))
                            if j == 31:
                                marks[0] = (self.s_fd, self.s_fd.n)
                            if j == 63:
                                marks[1] = (self.s_fy, self.s_fy.n)
                        self.fd_marks.append(marks)
                        P.wait("vector", self.s_fy)
                        P.emit("vector", lambda E, ht=ht: E.tensor_scalar(
                            ytmp[:, ht * 512:(ht + 1) * 512], y_ps[:],
                            self.dsc_sb[:, ht:ht + 1], None, ALU.mult), inc=self.s_fy)
                    if "ffx" in self.dbg and pa == 0:
                        P.wait("gpsimd", self.s_fy)
                        P.wait("gpsimd", self.s_te)
                        P.dma("gpsimd", lambda E: E.dma_start(
                            out=self.dbg_t["f1"], in_=xlnT[:]), self.s_ou)
                        P.dma("gpsimd", lambda E: E.dma_start(
                            out=self.dbg_t["f2"], in_=hbuf[:]), self.s_ou)
                        P.dma("gpsimd", lambda E: E.dma_start(
                            out=self.dbg_t["f3"], in_=ytmp[:]), self.s_ou)

                def stage_d(pa):
                    for it in range(4):
                        g = pa * 4 + it
                        P.wait("tensor", self.s_fy)
                        P.wait("tensor", self.s_te, te_mark[0])
                        if fo_at:
                            P.wait("tensor", self.s_fo, fo_at[-1])
                        for ht in range(16):
                            P.emit("tensor", lambda E, it=it, ht=ht: E.matmul(
                                big_ps[:, ht * 128:(ht + 1) * 128],
                                ytmp[:, ht * 512 + it * 128:ht * 512 + (it + 1) * 128],
                                self.ident_sb[:], start=True, stop=True),
                                inc=self.s_tp)
                        P.wait("gpsimd", self.s_ap)
                        P.wait("gpsimd", self.s_fo)
                        P.dma("gpsimd", lambda E, g=g: E.dma_start(
                            out=x2t[g % 2][:],
                            in_=self.x2_d[g * 128:(g + 1) * 128, :]), self.s_x)
                        P.wait("vector", self.s_tp)
                        P.wait("vector", self.s_x)
                        if len(fo_at) >= 2:
                            P.wait("vector", self.s_ou, 16 * (len(fo_at) - 1))
                        P.emit("vector", lambda E, g=g: E.tensor_add(
                            x3st[g % 2][:], big_ps[:], x2t[g % 2][:]), inc=self.s_fo)
                        fo_at.append(self.s_fo.n)
                        P.wait("gpsimd", self.s_fo)
                        P.dma("gpsimd", lambda E, g=g: E.dma_start(
                            out=self.out[g * 128:(g + 1) * 128, :],
                            in_=x3st[g % 2][:]), self.s_ou)

                stage_a(0)
                for pa in range(4):
                    stage_b(pa)
                    stage_c(pa)
                    if pa + 1 < 4:
                        stage_a(pa + 1)
                    stage_d(pa)


def kernel(**inputs):
    mk = MK(DBG)
    nc = mk.build()
    maps = [core_inputs(c, inputs) for c in range(8)]
    res = run_bass_kernel_spmd(nc, maps, core_ids=list(range(8)))
    outp = np.zeros((2, 8192, D), np.float32)
    for c in range(8):
        o = res.results[c]["out"]
        outp[0, 1024 * c:1024 * (c + 1)] = o[0:1024]
        outp[1, 1024 * (7 - c):1024 * (8 - c)] = o[1024:2048]
    return outp


# version 11
# speedup vs baseline: 1.1346x; 1.0075x over previous
import contextlib
import numpy as np
import ml_dtypes
import concourse.bass as bass
import concourse.mybir as mybir
from concourse.bass_utils import run_bass_kernel_spmd
from concourse.expressions_rust import RuntimeValue

BFNP = ml_dtypes.bfloat16
F32 = mybir.dt.float32
BF16 = mybir.dt.bfloat16
U8 = mybir.dt.uint8
U32 = mybir.dt.uint32
AF = mybir.ActivationFunctionType
ALU = mybir.AluOpType
AX = mybir.AxisListType

D = 2048
H = 16
QR, KVR = 768, 256
P_TOT = 9216
NKS = 72
NINT = 56
QL = 2048
NB = 18
EPS_LN = 1e-5
EPS_RMS = 1e-6
LAM = float(96.0 ** -0.25)
SKR = float((16.0 / np.sqrt(96.0)) ** 0.5)
SQRP = SKR / 16.0
DBG = ()
SPLIT_EVAC = False

ENGS = ("tensor", "vector", "scalar", "sync", "gpsimd")


class Sem:
    __slots__ = ("h", "n")

    def __init__(self, h):
        self.h = h
        self.n = 0


class Prog:
    def __init__(self):
        self.q = {e: [] for e in ENGS}

    def emit(self, e, fn, inc=None, k=1):
        if inc is None:
            self.q[e].append(fn)
            return

        def g(E, fn=fn, sem=inc, k=k):
            fn(E).then_inc(sem.h, k)
        self.q[e].append(g)
        inc.n += k

    def dma(self, e, fn, sem):
        self.emit(e, fn, inc=sem, k=16)

    def wait(self, e, sem, val=None):
        v = sem.n if val is None else val
        if v <= 0:
            return
        self.q[e].append(lambda E, h=sem.h, v=v: E.wait_ge(h, v))

    def replay(self, block):
        for e in ENGS:
            ops = self.q[e]
            getattr(block, e)(lambda E, ops=ops: [f(E) for f in ops])


def _rope_tab(positions):
    f = np.arange(32) // 2
    invf = 10000.0 ** (-(f / 16.0))
    return np.cos(positions[None, :].astype(np.float64) * invf[:, None])


def core_inputs(c, a):
    QA0 = 1024 * c
    QB0 = 1024 * (7 - c)
    PA = QA0 + 1024
    PB = QB0 + 1024
    x = np.asarray(a["L0_attn_x"])
    d = {}
    d["xp"] = np.ascontiguousarray(
        np.concatenate([x[0, :QA0], x[1, :QB0], x[0, QA0:PA], x[1, QB0:PB]], axis=0))
    pos_k = np.concatenate([np.arange(QA0), np.arange(QB0),
                            np.arange(QA0, PA), np.arange(QB0, PB)])
    pos_q = np.concatenate([np.arange(QA0, PA), np.arange(QB0, PB)])
    ck = _rope_tab(pos_k)
    cq = _rope_tab(pos_q)
    tabK = np.zeros((128, P_TOT), np.float32)
    tabK[32:64] = tabK[64:96] = (np.sqrt(2.0) * (SKR / 16.0) * ck).astype(np.float32)
    tabQ = np.zeros((128, QL), np.float32)
    tabQ[32:64] = tabQ[64:96] = (np.sqrt(2.0) * SQRP * cq).astype(np.float32)
    d["tabK"] = tabK
    d["tabQ"] = tabQ
    qt = np.zeros((1, 128), np.uint32)
    for g in range(NINT // 8):
        qt[0, g] = 0 if g < c else 1
    d["qtab"] = qt
    d["w_cq"] = np.asarray(a["L0_attn_w_cq"])
    d["w_qn"] = np.asarray(a["L0_attn_w_q_nope"])
    d["w_qr_pad"] = np.ascontiguousarray(np.concatenate(
        [np.zeros((QR, 64), np.float32), np.asarray(a["L0_attn_w_q_rope"])], axis=1))
    d["q_g"] = np.ascontiguousarray(np.asarray(a["L0_attn_q_g"]).reshape(6, 128).T)
    d["w_ckv"] = np.asarray(a["L0_attn_w_ckv"])
    d["w_kn"] = np.asarray(a["L0_attn_w_k_nope"])
    d["w_v"] = np.asarray(a["L0_attn_w_v"])
    d["kv_g"] = np.ascontiguousarray(np.asarray(a["L0_attn_kv_g"]).reshape(2, 128).T)
    wkr = np.asarray(a["L0_attn_w_k_rope"])
    d["w_kr_pad"] = np.ascontiguousarray(np.concatenate(
        [np.zeros((D, 32), np.float32), wkr, wkr], axis=1))
    d["w_proj"] = np.asarray(a["L0_attn_w_proj"])
    wgT = np.asarray(a["L0_ffn_gate_w"]).astype(np.uint8).T
    wuT = np.asarray(a["L0_ffn_up_w"]).astype(np.uint8).T
    wdT = np.asarray(a["L0_ffn_down_w"]).astype(np.uint8).T
    d["wg_t"] = np.ascontiguousarray(
        wgT.reshape(16, 128, 64, 128).transpose(2, 1, 0, 3).reshape(64, 128, 2048))
    d["wu_t"] = np.ascontiguousarray(
        wuT.reshape(16, 128, 64, 128).transpose(2, 1, 0, 3).reshape(64, 128, 2048))
    d["wd_t"] = np.ascontiguousarray(
        wdT.reshape(64, 128, 16, 128).transpose(2, 1, 0, 3).reshape(16, 128, 8192))
    d["gsc"] = np.ascontiguousarray(np.asarray(a["L0_ffn_gate_scale"]).reshape(64, 128).T)
    d["usc"] = np.ascontiguousarray(np.asarray(a["L0_ffn_up_scale"]).reshape(64, 128).T)
    d["dsc"] = np.ascontiguousarray(np.asarray(a["L0_ffn_down_scale"]).reshape(16, 128).T)
    stair = np.zeros((128, 2048), np.float32)
    for i in range(128):
        stair[i, 1024 + i:] = 1.0
    d["stair"] = stair.astype(BFNP)
    d["ident"] = np.eye(128, dtype=BFNP)
    return d


class MK:
    def __init__(self, dbg=()):
        self.dbg = dbg
        self.nc = bass.Bass("TRN2", target_bir_lowering=False)
        self.P = Prog()

    def sem(self, name):
        s = Sem(self.st.enter_context(self.nc.semaphore(name)))
        setattr(self, "s_" + name, s)
        return s

    def sb(self, stack, name, shape, dt):
        return stack.enter_context(self.nc.sbuf_tensor(name, shape, dt))

    def build(self):
        nc, P = self.nc, self.P
        dbg = self.dbg
        self.inp = {}
        for name, shape, dt in [
            ("xp", [P_TOT, D], F32), ("tabK", [128, P_TOT], F32),
            ("tabQ", [128, QL], F32), ("qtab", [1, 128], U32),
            ("w_cq", [D, QR], F32), ("w_qn", [QR, 1024], F32),
            ("w_qr_pad", [QR, 576], F32), ("q_g", [128, 6], F32),
            ("w_ckv", [D, KVR], F32), ("w_kn", [KVR, 1024], F32),
            ("w_v", [KVR, 1024], F32), ("kv_g", [128, 2], F32),
            ("w_kr_pad", [D, 96], F32), ("w_proj", [1024, D], F32),
            ("wg_t", [64, 128, 2048], U8), ("wu_t", [64, 128, 2048], U8),
            ("wd_t", [16, 128, 8192], U8),
            ("gsc", [128, 64], F32), ("usc", [128, 64], F32), ("dsc", [128, 16], F32),
            ("stair", [128, 2048], BF16), ("ident", [128, 128], BF16),
        ]:
            self.inp[name] = nc.dram_tensor(name, shape, dt, kind="ExternalInput").ap()
        inp = self.inp
        self.out = nc.dram_tensor("out", [QL, D], F32, kind="ExternalOutput").ap()
        self.x2_d = nc.dram_tensor(
            "x2_scr", [QL, D], F32,
            kind="ExternalOutput" if "x2" in dbg else "Internal").ap()
        self.ao_d = nc.dram_tensor(
            "ao_scr", [H, 64, QL], BF16,
            kind="ExternalOutput" if "ao" in dbg else "Internal").ap()
        self.dbg_t = {}
        for key, nm, shape, dt, want in [
            ("ckv", "dbg_ckv", [128, 2 * P_TOT], BF16, "ckv"),
            ("k0", "dbg_k0", [128, P_TOT], BF16, "kaug"),
            ("k1", "dbg_k1", [128, P_TOT], BF16, "kaug"),
            ("vv", "dbg_v", [128, NKS * 130], BF16, "kaug"),
            ("normq", "dbg_normq", [128, 6 * QL], BF16, "normq"),
            ("q0", "dbg_q0", [128, QL], BF16, "qcur"),
            ("q1", "dbg_q1", [128, QL], BF16, "qcur"),
            ("den", "dbg_den", [16, QL], F32, "den"),
            ("xlnT", "dbg_xlnT", [128, 16 * 512], BF16, ("p1x", "prx")),
            ("stm", "dbg_stm", [128, 8], F32, "p1x"),
            ("rstdc", "dbg_rstdc", [128, 4], F32, "p1x"),
            ("rstdb", "dbg_rstdb", [128, 512], F32, ("p1x", "prx")),
            ("bm", "dbg_bm", [2, 16], F32, "bm"),
            ("pn", "dbg_pn", [128, 8 * QL], BF16, "pn"),
            ("f1", "dbg_f1", [128, 16 * 512], BF16, "ffx"),
            ("f2", "dbg_f2", [128, 64 * 512], BF16, "ffx"),
            ("f3", "dbg_f3", [128, 16 * 512], BF16, "ffx"),
            ("rcp", "dbg_rcp", [16, QL], F32, "pn"),
        ]:
            wl = want if isinstance(want, tuple) else (want,)
            if any(w in dbg for w in wl):
                self.dbg_t[key] = nc.dram_tensor(nm, shape, dt, kind="ExternalOutput").ap()

        self.ln_at = [None, None]
        self.sq_at = [None, None]
        self.krow_at = 0
        self.st = contextlib.ExitStack()
        with self.st:
            st = self.st
            for nm in ["ini", "x", "tk", "sc", "vc", "ap", "tp", "te", "mm", "ax",
                       "cw", "wc", "qe", "ex", "xv", "bn", "sg", "eg", "mk", "pv",
                       "oe", "od", "mh", "pj", "pq", "fw", "fg", "fh", "fy", "fo",
                       "ou", "x2", "rc", "ma", "fz", "fd"]:
                self.sem(nm)

            self.ones_bf = self.sb(st, "ones_bf", [128, 1], BF16)
            self.ones_row = self.sb(st, "ones_row", [1, 128], F32)
            self.eps_t = self.sb(st, "eps_t", [128, 2], F32)
            self.stair_sb = self.sb(st, "stair_sb", [128, 2048], BF16)
            self.ident_sb = self.sb(st, "ident_sb", [128, 128], BF16)
            self.qtab_sb = self.sb(st, "qtab_sb", [1, 128], U32)
            self.zer_bf = self.sb(st, "zer_bf", [128, 512], BF16)
            self.stm = self.sb(st, "stm", [128, 8], F32)
            self.rstd_c = self.sb(st, "rstd_c", [128, 4], F32)
            self.mneg_bf = self.sb(st, "mneg_bf", [128, 2], F32)
            self.rstd_bf = self.sb(st, "rstd_bf", [128, 4], BF16)
            self.ones_rbf = self.sb(st, "ones_rbf", [1, 128], BF16)
            self.bm2 = self.sb(st, "bm2", [2, 16], F32)
            self.bmo = self.sb(st, "bmo", [1, 16], F32)
            self.bm_scr = self.sb(st, "bm_scr", [2, 1], F32)
            self.krm = self.sb(st, "krm", [1, 2], F32)
            self.lnb = self.sb(st, "lnb", [1, 2], F32)
            self.gsc_sb = self.sb(st, "gsc_sb", [128, 64], F32)
            self.usc_sb = self.sb(st, "usc_sb", [128, 64], F32)
            self.dsc_sb = self.sb(st, "dsc_sb", [128, 16], F32)
            self.kvg_sb = self.sb(st, "kvg_sb", [128, 2], F32)
            self.qg_sb = self.sb(st, "qg_sb", [128, 6], F32)

            for nm, dst in [("stair", self.stair_sb), ("ident", self.ident_sb),
                            ("qtab", self.qtab_sb), ("gsc", self.gsc_sb),
                            ("usc", self.usc_sb), ("dsc", self.dsc_sb),
                            ("kv_g", self.kvg_sb), ("q_g", self.qg_sb)]:
                P.dma("sync", lambda E, nm=nm, dst=dst: E.dma_start(
                    out=dst[:], in_=inp[nm]), self.s_ini)
            P.emit("vector", lambda E: E.memset(self.ones_bf[:], 1.0))
            P.emit("vector", lambda E: E.memset(self.ones_row[:], 1.0))
            P.emit("vector", lambda E: E.memset(self.ones_rbf[:], 1.0))
            P.emit("vector", lambda E: E.memset(self.eps_t[:, 0:1], EPS_LN))
            P.emit("vector", lambda E: E.memset(self.eps_t[:, 1:2], EPS_RMS))
            P.emit("vector", lambda E: E.memset(self.zer_bf[:], 0.0))
            P.emit("vector", lambda E: E.memset(self.bm2[:], 0.0))
            P.emit("vector", lambda E: E.memset(self.krm[:], 0.0), inc=self.s_vc)

            dn_sc = contextlib.ExitStack()
            with dn_sc:
              self.denoms = self.sb(dn_sc, "denoms", [16, QL], F32)
              self.recip = self.sb(dn_sc, "recip", [16, QL], F32)
              ca = contextlib.ExitStack()
              with ca:
                  self.ckv_sb = self.sb(ca, "ckv_sb", [128, 2 * P_TOT], BF16)
                  self.kaug = [self.sb(ca, f"kaug{i}", [128, P_TOT], BF16) for i in range(2)]
                  self.v_sb = self.sb(ca, "v_sb", [128, NKS * 130], BF16)
                  self.normq = self.sb(ca, "normq", [128, 6 * QL], BF16)
                  self.tabq_sb = self.sb(ca, "tabq_sb", [128, QL], F32)

                  P.emit("vector", lambda E: E.memset(self.v_sb[:], 1.0))
                  P.emit("vector", lambda E: E.memset(self.kaug[0][96:128, :], 0.0))
                  P.emit("vector", lambda E: E.memset(self.kaug[0][96:97, :], -1.0))
                  P.emit("vector", lambda E: E.memset(self.kaug[1][0:32, :], 0.0))
                  P.emit("vector", lambda E: E.memset(self.kaug[1][0:1, :], -1.0),
                         inc=self.s_vc)

                  self.phase1()
                  if "p1only" not in dbg:
                      self.prep()
                  if "p1only" not in dbg and "noattn" not in dbg:
                      self.attention()

                  if "ckv" in dbg:
                      P.wait("gpsimd", self.s_cw)
                      P.dma("gpsimd", lambda E: E.dma_start(
                          out=self.dbg_t["ckv"], in_=self.ckv_sb[:]), self.s_ou)
                  if "kaug" in dbg:
                      P.wait("gpsimd", self.s_xv)
                      P.dma("gpsimd", lambda E: E.dma_start(
                          out=self.dbg_t["k0"], in_=self.kaug[0][:]), self.s_ou)
                      P.dma("gpsimd", lambda E: E.dma_start(
                          out=self.dbg_t["k1"], in_=self.kaug[1][:]), self.s_ou)
                      P.dma("gpsimd", lambda E: E.dma_start(
                          out=self.dbg_t["vv"], in_=self.v_sb[:]), self.s_ou)
                  if "normq" in dbg and "p1only" not in dbg:
                      P.wait("gpsimd", self.s_qe)
                      P.dma("gpsimd", lambda E: E.dma_start(
                          out=self.dbg_t["normq"], in_=self.normq[:]), self.s_ou)
                  if "bm" in dbg:
                      P.wait("gpsimd", self.s_bn)
                      P.dma("gpsimd", lambda E: E.dma_start(
                          out=self.dbg_t["bm"], in_=self.bm2[:]), self.s_ou)

              if "den" in dbg:
                  P.wait("gpsimd", self.s_od)
                  P.dma("gpsimd", lambda E: E.dma_start(
                      out=self.dbg_t["den"], in_=self.denoms[:]), self.s_ou)
              if "noftail" not in dbg:
                  self.proj()
            if "noftail" not in dbg:
                self.ffn()

            P.wait("gpsimd", self.s_ou)
            P.wait("sync", self.s_x2)

            with nc.Block() as block:
                P.replay(block)
        return nc

    def emit_ln(self, src, dst, dump, pa=0):
        P = self.P
        stm = self.stm[:, pa * 4:pa * 4 + 4]
        rsc = self.rstd_c[:, pa * 2:pa * 2 + 2]
        if self.ln_at[pa] is not None:
            P.wait("scalar", self.s_ap, self.ln_at[pa])
            P.wait("vector", self.s_ap, self.ln_at[pa])
        P.emit("scalar", lambda E: E.activation(
            dump[:], src, AF.Square, accum_out=stm[:, 1:2]), inc=self.s_sc)
        P.emit("vector", lambda E: E.reduce_sum(stm[:, 0:1], src, axis=AX.X),
               inc=self.s_vc)
        P.wait("scalar", self.s_vc)
        P.emit("scalar", lambda E: E.activation(
            stm[:, 2:3], stm[:, 0:1], AF.Square, bias=0.0, scale=1.0 / D),
            inc=self.s_sc)
        P.wait("scalar", self.s_sc)
        P.emit("scalar", lambda E: E.activation(
            stm[:, 2:3], stm[:, 2:3], AF.Identity, bias=0.0, scale=-1.0),
            inc=self.s_sc)
        P.wait("scalar", self.s_sc)
        P.emit("scalar", lambda E: E.activation(
            stm[:, 3:4], stm[:, 1:2], AF.Identity, bias=stm[:, 2:3], scale=1.0 / D),
            inc=self.s_sc)
        P.wait("scalar", self.s_sc)
        P.emit("scalar", lambda E: E.activation(
            rsc[:, 0:1], stm[:, 3:4], AF.Ln, bias=self.eps_t[:, 0:1]),
            inc=self.s_sc)
        P.wait("scalar", self.s_sc)
        P.emit("scalar", lambda E: E.activation(
            rsc[:, 0:1], rsc[:, 0:1], AF.Exp, bias=0.0, scale=-0.5),
            inc=self.s_sc)
        P.wait("vector", self.s_sc)
        P.emit("vector", lambda E: E.scalar_tensor_tensor(
            rsc[:, 1:2], stm[:, 0:1], -1.0 / D, rsc[:, 0:1],
            ALU.mult, ALU.mult), inc=self.s_vc)
        P.wait("vector", self.s_vc)
        P.emit("vector", lambda E: E.tensor_scalar(
            dst, src, rsc[:, 0:1], rsc[:, 1:2], ALU.mult, ALU.add),
            inc=self.s_ap)
        self.ln_at[pa] = self.s_ap.n

    def emit_center(self, src, dst, dump, pa, rslot=None):
        P = self.P
        stm = self.stm[:, pa * 4:pa * 4 + 4]
        if self.ln_at[pa] is not None:
            P.wait("vector", self.s_ap, self.ln_at[pa])
        P.emit("vector", lambda E: E.reduce_sum(stm[:, 0:1], src, axis=AX.X),
               inc=self.s_vc)
        P.wait("scalar", self.s_vc)
        P.emit("scalar", lambda E: E.activation(
            self.mneg_bf[:, pa:pa + 1], stm[:, 0:1], AF.Identity,
            bias=0.0, scale=-1.0 / D), inc=self.s_sc)
        P.wait("scalar", self.s_sc)
        P.wait("scalar", self.s_tp, self.s_tp.n)
        P.emit("scalar", lambda E: E.activation(
            dst, src, AF.Identity, bias=self.mneg_bf[:, pa:pa + 1], scale=1.0),
            inc=self.s_ap)
        self.ln_at[pa] = self.s_ap.n
        return self.s_ap.n

    def emit_rstd(self, dst, dump, pa, rslot):
        P = self.P
        stm = self.stm[:, pa * 4:pa * 4 + 4]
        if self.krow_at:
            P.wait("scalar", self.s_ax, self.krow_at)
        P.emit("scalar", lambda E: E.activation(
            dump[:], dst, AF.Square, accum_out=stm[:, 1:2]), inc=self.s_sc)
        self.sq_at[pa] = self.s_sc.n
        P.wait("scalar", self.s_sc)
        P.emit("scalar", lambda E: E.activation(
            stm[:, 2:3], stm[:, 1:2], AF.Ln, bias=self.eps_t[:, 0:1],
            scale=1.0 / D), inc=self.s_sc)
        P.wait("scalar", self.s_sc)
        P.emit("scalar", lambda E, rslot=rslot: E.activation(
            self.rstd_bf[:, rslot:rslot + 1], stm[:, 2:3], AF.Exp,
            bias=0.0, scale=-0.5), inc=self.s_sc)

    def emit_transpose(self, src_t, xlnT_t, it, tp_ps):
        P = self.P
        for grp in range(4):
            P.wait("tensor", self.s_te, self.s_te.n - 1)
            for j4 in range(4):
                j = grp * 4 + j4
                P.emit("tensor", lambda E, j=j, grp=grp, j4=j4: E.matmul(
                    tp_ps[:, (grp % 2) * 512 + j4 * 128:(grp % 2) * 512 + (j4 + 1) * 128],
                    src_t[:, j * 128:(j + 1) * 128], self.ident_sb[:],
                    start=True, stop=True),
                    inc=(self.s_tp if j4 == 3 else None))
            dst = (xlnT_t[:].rearrange("p (j n) -> p j n", n=512)
                   [:, grp * 4:(grp + 1) * 4, it * 128:(it + 1) * 128])
            srcv = (tp_ps[:, (grp % 2) * 512:((grp % 2) + 1) * 512]
                    .rearrange("p (j n) -> p j n", n=128))
            if grp % 2 == 0 or not SPLIT_EVAC:
                P.wait("vector", self.s_tp)
                P.emit("vector", lambda E, dst=dst, srcv=srcv: E.tensor_copy(
                    dst, srcv), inc=self.s_te)
            else:
                P.wait("scalar", self.s_tp)
                P.emit("scalar", lambda E, dst=dst, srcv=srcv: E.activation(
                    dst, srcv, AF.Copy, bias=0.0, scale=1.0), inc=self.s_te)

    def phase1(self):
        nc, P, inp = self.nc, self.P, self.inp
        p1 = contextlib.ExitStack()
        with p1:
            xst = [self.sb(p1, f"xst{i}", [128, D], BF16) for i in range(2)]
            dump = self.sb(p1, "dump", [128, D], BF16)
            rxrow = [self.sb(p1, f"rxrow{i}", [1, 512], BF16) for i in range(2)]
            xlnbf = [self.sb(p1, f"xlnbf{i}", [128, D], BF16) for i in range(3)]
            xlnT = self.sb(p1, "xlnT", [128, 16 * 512], BF16)
            wckv_sb = self.sb(p1, "wckv_sb", [128, 16 * KVR], BF16)
            wkr_sb = self.sb(p1, "wkr_sb", [128, 16 * 96], BF16)
            tabk_sb = [self.sb(p1, f"tabk_sb{i}", [128, 512], F32) for i in range(2)]
            sqc = self.sb(p1, "sqc", [128, 512], BF16)
            rmsr = self.sb(p1, "rmsr", [1, 512], F32)
            rstdb = self.sb(p1, "rstdb", [128, 512], F32)

            with (nc.psum_tensor("tp_ps", [128, 1024], F32) as tp_ps,
                  nc.psum_tensor("ckv_ps", [128, 1024], F32) as ckv_ps,
                  nc.psum_tensor("kr_ps", [128, 1024], F32) as kr_ps,
                  nc.psum_tensor("ax_ps", [128, 1024], F32) as ax_ps):

                P.dma("gpsimd", lambda E: E.dma_start(
                    out=wckv_sb[:].rearrange("p (j n) -> p j n", n=KVR),
                    in_=inp["w_ckv"].rearrange("(j p) n -> p j n", p=128)), self.s_ini)
                P.dma("gpsimd", lambda E: E.dma_start(
                    out=wkr_sb[:].rearrange("p (j n) -> p j n", n=96),
                    in_=inp["w_kr_pad"].rearrange("(j p) n -> p j n", p=128)), self.s_ini)
                for e in ENGS:
                    P.wait(e, self.s_ini)
                P.wait("tensor", self.s_vc)
                P.wait("scalar", self.s_vc)

                krm_red_done = 0
                rstdb_done = 0
                cache_done = 0
                rope_done = 0
                tabk_mul_done = 0
                rx_done = [0, 0]
                rope_hist = {}

                xc_at = {}

                def center_tile(g):
                    buf = g % 2
                    if g >= 2:
                        P.wait("gpsimd", self.s_ap, self.s_ap.n - 1)
                    P.dma("gpsimd", lambda E, g=g, buf=buf: E.dma_start(
                        out=xst[buf][:], in_=inp["xp"][g * 128:(g + 1) * 128, :]),
                        self.s_x)
                    P.wait("vector", self.s_x)
                    xc_at[g] = self.emit_center(xst[buf][:], xlnbf[g % 3][:],
                                                dump, g % 2)

                def tiles(b):
                    P.wait("sync", self.s_cw, rope_hist.get(b - 2, 0))
                    P.dma("sync", lambda E, b=b: E.dma_start(
                        out=tabk_sb[b % 2][:],
                        in_=inp["tabK"][:, b * 512:(b + 1) * 512]), self.s_tk)
                    for it in range(3):
                        center_tile(b * 4 + it)
                    for it in range(4):
                        g = b * 4 + it
                        P.wait("tensor", self.s_ap, xc_at[g])
                        self.emit_transpose(xlnbf[g % 3], xlnT, it, tp_ps)
                        self.emit_rstd(xlnbf[g % 3][:], dump, g % 2, it)
                        if it == 0:
                            center_tile(b * 4 + 3)

                def mms(b):
                    P.wait("tensor", self.s_sc)
                    if b > 0:
                        P.wait("tensor", self.s_sc, rx_done[(b - 1) % 2])
                        P.wait("tensor", self.s_cw, tabk_mul_done)
                    for it2 in range(4):
                        P.emit("tensor", lambda E, it2=it2: E.matmul(
                            kr_ps[64:65, 512 + it2 * 128:512 + (it2 + 1) * 128],
                            self.rstd_bf[:, it2:it2 + 1], self.ident_sb[:],
                            start=True, stop=True, skip_group_check=True),
                            inc=(self.s_ax if it2 == 3 else None))
                    self.krow_at = self.s_ax.n
                    P.wait("scalar", self.s_ax)
                    P.emit("scalar", lambda E, b=b: E.activation(
                        rxrow[b % 2][:], kr_ps[64:65, 512:1024], AF.Copy,
                        bias=0.0, scale=1.0), inc=self.s_sc)
                    rx_done[b % 2] = self.s_sc.n
                    P.wait("tensor", self.s_te)
                    if b > 0:
                        P.wait("tensor", self.s_cw, cache_done)
                    for ot in range(2):
                        for j in range(16):
                            P.emit("tensor", lambda E, ot=ot, j=j: E.matmul(
                                ckv_ps[:, ot * 512:(ot + 1) * 512],
                                wckv_sb[:, j * KVR + ot * 128:j * KVR + (ot + 1) * 128],
                                xlnT[:, j * 512:(j + 1) * 512],
                                start=(j == 0), stop=(j == 15)),
                                inc=(self.s_mm if j == 15 else None))
                    P.wait("scalar", self.s_mm, self.s_mm.n - 1)
                    P.emit("scalar", lambda E: E.activation(
                        sqc[:], ckv_ps[:, 0:512], AF.Square), inc=self.s_sc)
                    sqa_done = self.s_sc.n
                    if b > 0:
                        P.wait("tensor", self.s_cw, rope_done)
                    for j in range(16):
                        P.emit("tensor", lambda E, j=j: E.matmul(
                            kr_ps[0:96, 0:512],
                            wkr_sb[:, j * 96:j * 96 + 96],
                            xlnT[:, j * 512:(j + 1) * 512],
                            start=(j == 0), stop=(j == 15)),
                            inc=(self.s_mm if j == 15 else None))
                        if j == 2:
                            P.wait("tensor", self.s_sc, sqa_done)
                            P.wait("tensor", self.s_vc, krm_red_done)
                            P.emit("tensor", lambda E: E.matmul(
                                ax_ps[0:1, 0:512], self.ones_bf[:], sqc[:],
                                start=True, stop=False,
                                skip_group_check=True), inc=self.s_ax)
                            P.wait("scalar", self.s_ax)
                            P.emit("scalar", lambda E: E.activation(
                                sqc[:], ckv_ps[:, 512:1024], AF.Square),
                                inc=self.s_sc)
                            sqb_done = self.s_sc.n
                        if j == 8:
                            P.wait("tensor", self.s_sc, sqb_done)
                            P.emit("tensor", lambda E: E.matmul(
                                ax_ps[0:1, 0:512], self.ones_bf[:], sqc[:],
                                start=False, stop=True,
                                skip_group_check=True), inc=self.s_ax)
                            P.wait("scalar", self.s_ax)
                            P.emit("scalar", lambda E: E.activation(
                                rmsr[:], ax_ps[0:1, 0:512], AF.Ln,
                                bias=self.eps_t[0:1, 1:2], scale=1.0 / KVR),
                                inc=self.s_sc)
                            P.wait("scalar", self.s_sc)
                            P.emit("scalar", lambda E: E.activation(
                                rmsr[:], rmsr[:], AF.Exp, bias=0.0, scale=-0.5),
                                inc=self.s_sc)
                    P.wait("tensor", self.s_sc, rx_done[b % 2])
                    P.emit("tensor", lambda E, b=b: E.matmul(
                        kr_ps[0:96, 512:1024], self.ones_rbf[0:1, 0:96],
                        rxrow[b % 2][:],
                        start=True, stop=True, skip_group_check=True), inc=self.s_ax)
                    P.wait("tensor", self.s_sc)
                    P.wait("tensor", self.s_vc, rstdb_done)
                    P.emit("tensor", lambda E: E.matmul(
                        ax_ps[:, 512:1024], self.ones_row[:], rmsr[:],
                        start=True, stop=True, skip_group_check=True),
                        inc=self.s_ax)

                def block_end(b):
                    nonlocal krm_red_done, rstdb_done, cache_done, rope_done
                    nonlocal tabk_mul_done
                    P.wait("vector", self.s_tk)
                    P.wait("vector", self.s_ax)
                    P.emit("vector", lambda E, b=b: E.tensor_mul(
                        tabk_sb[b % 2][32:64, :], tabk_sb[b % 2][32:64, :],
                        kr_ps[32:64, 512:1024]),
                        inc=self.s_cw)
                    P.emit("vector", lambda E, b=b: E.tensor_mul(
                        tabk_sb[b % 2][64:96, :], tabk_sb[b % 2][64:96, :],
                        kr_ps[64:96, 512:1024]),
                        inc=self.s_cw)
                    tabk_mul_done = self.s_cw.n
                    P.emit("vector", lambda E, b=b: E.tensor_mul(
                        self.kaug[0][64:96, b * 512:(b + 1) * 512],
                        kr_ps[64:96, 0:512],
                        tabk_sb[b % 2][64:96, :]), inc=self.s_cw)
                    P.emit("vector", lambda E, b=b: E.tensor_mul(
                        self.kaug[1][32:64, b * 512:(b + 1) * 512],
                        kr_ps[32:64, 0:512],
                        tabk_sb[b % 2][32:64, :]), inc=self.s_cw)
                    rope_done = self.s_cw.n
                    rope_hist[b] = rope_done
                    P.emit("vector", lambda E: E.tensor_copy(
                        rstdb[:], ax_ps[:, 512:1024]), inc=self.s_vc)
                    rstdb_done = self.s_vc.n
                    for ot in range(2):
                        P.emit("vector", lambda E, b=b, ot=ot: E.scalar_tensor_tensor(
                            self.ckv_sb[:, ot * P_TOT + b * 512:ot * P_TOT + (b + 1) * 512],
                            ckv_ps[:, ot * 512:(ot + 1) * 512],
                            self.kvg_sb[:, ot:ot + 1], rstdb[:], ALU.mult, ALU.mult),
                            inc=self.s_cw)
                    cache_done = self.s_cw.n
                    P.wait("scalar", self.s_cw)
                    P.emit("scalar", lambda E, b=b: E.activation(
                        sqc[64:96, :], self.kaug[0][64:96, b * 512:(b + 1) * 512],
                        AF.Square), inc=self.s_sc)
                    P.wait("tensor", self.s_sc)
                    P.emit("tensor", lambda E: E.matmul(
                        ax_ps[0:1, 0:512], self.ones_bf[64:96, :], sqc[64:96, :],
                        start=True, stop=True), inc=self.s_ax)
                    P.wait("vector", self.s_ax)
                    P.emit("vector", lambda E: E.reduce_max(
                        self.krm[0:1, 1:2], ax_ps[0:1, 0:512], axis=AX.X),
                        inc=self.s_vc)
                    krm_red_done = self.s_vc.n
                    P.wait("vector", self.s_vc)
                    P.emit("vector", lambda E: E.tensor_max(
                        self.krm[0:1, 0:1], self.krm[0:1, 0:1], self.krm[0:1, 1:2]),
                        inc=self.s_cw)
                    return

                pend = None
                for b in range(NB):
                    tiles(b)
                    if pend is not None:
                        block_end(pend)
                    mms(b)
                    pend = b
                block_end(pend)
                if "p1x" in self.dbg:
                    P.wait("gpsimd", self.s_cw)
                    P.wait("gpsimd", self.s_ap)
                    for nm2, src2 in [("xlnT", xlnT), ("stm", self.stm),
                                      ("rstdc", self.rstd_c), ("rstdb", rstdb)]:
                        P.dma("gpsimd", lambda E, nm2=nm2, src2=src2: E.dma_start(
                            out=self.dbg_t[nm2], in_=src2[:]), self.s_ou)

    def prep(self):
        nc, P, inp = self.nc, self.P, self.inp
        pr = contextlib.ExitStack()
        with pr:
            xst = [self.sb(pr, f"pxst{i}", [128, D], F32) for i in range(2)]
            dump = self.sb(pr, "pdump", [128, D], BF16)
            xlnbf = [self.sb(pr, f"pxlnbf{i}", [128, D], BF16) for i in range(2)]
            xlnT = self.sb(pr, "pxlnT", [128, 16 * 512], BF16)
            wcq = [self.sb(pr, f"wcq{i}", [128, QR], BF16) for i in range(3)]
            sq2 = [self.sb(pr, f"sq2{i}", [128, 512], BF16) for i in range(2)]
            rmsr2 = self.sb(pr, "rmsr2", [1, 512], F32)
            rstdb2 = self.sb(pr, "rstdb2", [128, 512], F32)

            with (nc.psum_tensor("cq_ps", [128, 6 * 512], F32) as cq_ps,
                  nc.psum_tensor("pax_ps", [128, 1024], F32) as pax_ps):
                self.phase_barrier(("sync", "gpsimd", "tensor"),
                                   ("sc", "vc", "ap", "cw", "ax", "te", "tk"))
                P.dma("sync", lambda E: E.dma_start(
                    out=self.tabq_sb[:], in_=inp["tabQ"]), self.s_ini)
                P.dma("gpsimd", lambda E: E.dma_start(
                    out=wcq[0][:], in_=inp["w_cq"][0:128, :]), self.s_wc)
                self.cq_at = []
                for qb in range(4):
                    for it in range(4):
                        g = qb * 4 + it
                        buf = g % 2
                        if g >= 2:
                            P.wait("sync", self.s_ap, self.s_ap.n - 1)
                        P.dma("sync", lambda E, g=g, buf=buf: E.dma_start(
                            out=xst[buf][:],
                            in_=inp["xp"][(56 + g) * 128:(57 + g) * 128, :]), self.s_x)
                        P.wait("vector", self.s_x)
                        P.wait("vector", self.s_tp, self.s_tp.n - 4)
                        self.emit_center(xst[buf][:], xlnbf[g % 2][:], dump, g % 2)
                        P.wait("tensor", self.s_ap)
                        self.emit_transpose(xlnbf[g % 2], xlnT, it, pax_ps)
                    P.wait("tensor", self.s_te)
                    P.wait("tensor", self.s_qe)
                    for j in range(16):
                        if not (qb == 3 and j == 15):
                            gj = 16 * qb + j + 1
                            if len(self.cq_at) >= 2:
                                P.wait("gpsimd", self.s_pq, self.cq_at[-2])
                            P.dma("gpsimd", lambda E, jn=(j + 1) % 16, gj=gj: E.dma_start(
                                out=wcq[gj % 3][:],
                                in_=inp["w_cq"][jn * 128:(jn + 1) * 128, :]), self.s_wc)
                        P.wait("tensor", self.s_wc, 16 * (16 * qb + j + 1))
                        for m in range(6):
                            P.emit("tensor", lambda E, j=j, m=m, qb=qb: E.matmul(
                                cq_ps[:, m * 512:(m + 1) * 512],
                                wcq[(16 * qb + j) % 3][:, m * 128:(m + 1) * 128],
                                xlnT[:, j * 512:(j + 1) * 512],
                                start=(j == 0), stop=(j == 15)),
                                inc=(self.s_pq if m == 5 else None))
                        self.cq_at.append(self.s_pq.n)
                    P.wait("scalar", self.s_pq)
                    ax_at = []
                    for m in range(6):
                        if m >= 2:
                            P.wait("scalar", self.s_ax, ax_at[m - 2])
                        P.emit("scalar", lambda E, m=m: E.activation(
                            sq2[m % 2][:], cq_ps[:, m * 512:(m + 1) * 512], AF.Square),
                            inc=self.s_sc)
                        P.wait("tensor", self.s_sc)
                        P.emit("tensor", lambda E, m=m: E.matmul(
                            pax_ps[0:1, 0:512], self.ones_bf[:], sq2[m % 2][:],
                            start=(m == 0), stop=(m == 5)), inc=self.s_ax)
                        ax_at.append(self.s_ax.n)
                    P.wait("scalar", self.s_ax)
                    P.emit("scalar", lambda E: E.activation(
                        rmsr2[:], pax_ps[0:1, 0:512], AF.Ln,
                        bias=self.eps_t[0:1, 1:2], scale=1.0 / QR), inc=self.s_sc)
                    P.wait("scalar", self.s_sc)
                    P.emit("scalar", lambda E: E.activation(
                        rmsr2[:], rmsr2[:], AF.Exp, bias=0.0, scale=-0.5), inc=self.s_sc)
                    P.wait("tensor", self.s_sc)
                    P.emit("tensor", lambda E: E.matmul(
                        pax_ps[:, 512:1024], self.ones_row[:], rmsr2[:],
                        start=True, stop=True), inc=self.s_ax)
                    P.wait("vector", self.s_ax)
                    P.emit("vector", lambda E: E.tensor_copy(
                        rstdb2[:], pax_ps[:, 512:1024]), inc=self.s_vc)
                    for m in range(6):
                        P.emit("vector", lambda E, m=m, qb=qb: E.scalar_tensor_tensor(
                            self.normq[:, m * QL + qb * 512:m * QL + (qb + 1) * 512],
                            cq_ps[:, m * 512:(m + 1) * 512],
                            self.qg_sb[:, m:m + 1], rstdb2[:], ALU.mult, ALU.mult),
                            inc=self.s_qe)
                if "prx" in self.dbg:
                    P.wait("gpsimd", self.s_qe)
                    P.dma("gpsimd", lambda E: E.dma_start(
                        out=self.dbg_t["xlnT"], in_=xlnT[:]), self.s_ou)
                    P.dma("gpsimd", lambda E: E.dma_start(
                        out=self.dbg_t["rstdb"], in_=rstdb2[:]), self.s_ou)

    def attention(self):
        nc, P, inp = self.nc, self.P, self.inp
        p2 = contextlib.ExitStack()
        with p2:
            qcur = [self.sb(p2, f"qcur{i}", [128, QL], BF16) for i in range(2)]
            pp = [self.sb(p2, f"pp{i}", [128, 2048], BF16) for i in range(2)]
            qsq = self.sb(p2, "qsq", [128, QL], BF16)
            dn_tmp = self.sb(p2, "dn_tmp", [128, QL], F32)
            aostg = self.sb(p2, "aostg", [64, QL], BF16)
            sqb = [self.sb(p2, f"sqb{i}", [128, 512], BF16) for i in range(2)]
            wvec = self.sb(p2, "wvec", [128, 2], BF16)
            wkn_sb = self.sb(p2, "wkn_sb", [128, 2 * 1024], BF16)
            wv_sb = self.sb(p2, "wv_sb", [128, 2 * 1024], BF16)
            wqn_sb = self.sb(p2, "wqn_sb", [128, 6 * 1024], BF16)
            wqr_sb = self.sb(p2, "wqr_sb", [128, 6 * 576], BF16)

            self.phase_barrier(("sync", "gpsimd", "tensor", "vector", "scalar"),
                               ("sc", "vc", "ap", "qe", "ax", "te", "pq", "mm"))
            P.emit("vector", lambda E: E.memset(wvec[:], 0.0))
            P.emit("vector", lambda E: E.memset(qcur[0][96:128, :], 0.0))
            P.emit("vector", lambda E: E.memset(qcur[1][0:32, :], 0.0))
            P.emit("vector", lambda E: E.memset(wvec[0:64, 0:1], float(LAM * LAM)))
            P.emit("vector", lambda E: E.memset(wvec[64:128, 1:2], float(LAM * LAM)),
                   inc=self.s_vc)
            for nm, dst, inner in [("w_kn", wkn_sb, 1024), ("w_v", wv_sb, 1024),
                                   ("w_qn", wqn_sb, 1024), ("w_qr_pad", wqr_sb, 576)]:
                P.dma("gpsimd", lambda E, nm=nm, dst=dst, inner=inner: E.dma_start(
                    out=dst[:].rearrange("p (j n) -> p j n", n=inner),
                    in_=inp[nm].rearrange("(j p) n -> p j n", p=128)), self.s_wc)

            with (nc.psum_tensor("s_ps0", [128, 1024], F32) as s0,
                  nc.psum_tensor("s_ps1", [128, 1024], F32) as s1,
                  nc.psum_tensor("o_ps", [65, 2048], F32) as o_ps):
                self.S = [s0, s1]
                P.wait("tensor", self.s_wc)
                P.wait("tensor", self.s_qe)
                P.wait("tensor", self.s_cw)
                P.wait("tensor", self.s_vc)
                self.selreg = nc.alloc_register(mybir.EngineType.PE, name="chsel")
                self.reg_free = [(None, 0)] * 4
                self.pv_done = {}
                self.eg_hist = []
                self.git_base = 0
                self.mh_qsq_at = []
                self.od_at = []

                mrow = dn_tmp[0:1, :]
                for p in range(H // 2):
                    self.emit_pair(p, qcur, sqb, wvec, wkn_sb, wv_sb,
                                   wqn_sb, wqr_sb)
                    self.emit_mhat(2 * p, 0, p, qcur, qsq, mrow)
                    self.emit_mhat(2 * p + 1, 1, p, qcur, qsq, mrow)
                    if "qcur" in self.dbg and p == 0:
                        P.wait("gpsimd", self.s_mh)
                        P.dma("gpsimd", lambda E: E.dma_start(
                            out=self.dbg_t["q0"], in_=qcur[0][:]), self.s_ou)
                        P.dma("gpsimd", lambda E: E.dma_start(
                            out=self.dbg_t["q1"], in_=qcur[1][:]), self.s_ou)
                    self.emit_head(2 * p, 0, qcur, pp, o_ps, dn_tmp, aostg)
                    self.emit_head(2 * p + 1, 1, qcur, pp, o_ps, dn_tmp, aostg)

        P.wait("vector", self.s_od)
        P.emit("vector", lambda E: E.reciprocal(self.recip[:], self.denoms[:]),
               inc=self.s_rc)

    def phase_barrier(self, engines, sem_names):
        for e in engines:
            for nm in sem_names:
                self.P.wait(e, getattr(self, "s_" + nm))

    def sreg(self, r, p0=0, p1=128, c0=0, w=512):
        base = (r % 2) * 512 + c0
        return self.S[r // 2][p0:p1, base:base + w]

    def use_region(self, r):
        sem, tgt = self.reg_free[r]
        if sem is not None:
            self.P.wait("tensor", sem, tgt)

    def release_region(self, r, sem):
        self.reg_free[r] = (sem, sem.n)

    def emit_pair(self, p, qcur, sqb, wvec, wkn_sb, wv_sb, wqn_sb, wqr_sb):
        P = self.P
        kaug, v_sb, ckv, normq = self.kaug, self.v_sb, self.ckv_sb, self.normq
        sq_hist = {}
        bm_hist = {}

        def emit_bm(bb):
            r = bb % 4
            P.wait("tensor", self.s_sc, sq_hist[bb])
            P.emit("tensor", lambda E, r=r, bb=bb: E.matmul(
                self.sreg(r, 0, 2),
                wvec[:], sqb[bb % 2][:], start=True, stop=True,
                skip_group_check=True), inc=self.s_bn)
            P.wait("vector", self.s_bn)
            P.emit("vector", lambda E, r=r: E.reduce_max(
                self.bm_scr[:], self.sreg(r, 0, 2), axis=AX.X),
                inc=self.s_vc)
            P.wait("vector", self.s_vc)
            P.emit("vector", lambda E, p=p: E.tensor_max(
                self.bm2[:, p:p + 1], self.bm2[:, p:p + 1], self.bm_scr[:]),
                inc=self.s_bn)
            bm_hist[bb] = self.s_bn.n
            self.release_region(r, self.s_bn)

        for blk in range(NB):
            r = blk % 4
            self.use_region(r)
            for ot in range(2):
                P.emit("tensor", lambda E, blk=blk, ot=ot, r=r, p=p: E.matmul(
                    self.sreg(r),
                    wkn_sb[:, ot * 1024 + p * 128:ot * 1024 + (p + 1) * 128],
                    ckv[:, ot * P_TOT + blk * 512:ot * P_TOT + (blk + 1) * 512],
                    start=(ot == 0), stop=(ot == 1), skip_group_check=True),
                    inc=(self.s_ex if ot == 1 else None))
            P.wait("vector", self.s_ex)
            P.emit("vector", lambda E, blk=blk, r=r: E.tensor_scalar(
                kaug[0][0:64, blk * 512:(blk + 1) * 512],
                self.sreg(r, 0, 64), LAM, None, ALU.mult))
            P.emit("vector", lambda E, blk=blk, r=r: E.tensor_scalar(
                kaug[1][64:128, blk * 512:(blk + 1) * 512],
                self.sreg(r, 64, 128), LAM, None, ALU.mult),
                inc=self.s_xv)
            P.wait("scalar", self.s_xv)
            if blk >= 2:
                P.wait("scalar", self.s_bn, bm_hist.get(blk - 2, 0))
            P.emit("scalar", lambda E, r=r, blk=blk: E.activation(
                sqb[blk % 2][:], self.sreg(r), AF.Square), inc=self.s_sc)
            sq_hist[blk] = self.s_sc.n
            if blk >= 1:
                emit_bm(blk - 1)
        emit_bm(NB - 1)

        for gk in range(NB):
            r = gk % 4
            self.use_region(r)
            for sub in range(4):
                kt = gk * 4 + sub
                for ot in range(2):
                    P.emit("tensor", lambda E, kt=kt, sub=sub, ot=ot, r=r, p=p: E.matmul(
                        self.sreg(r, c0=sub * 128, w=128),
                        ckv[:, ot * P_TOT + kt * 128:ot * P_TOT + (kt + 1) * 128],
                        wv_sb[:, ot * 1024 + p * 128:ot * 1024 + (p + 1) * 128],
                        start=(ot == 0), stop=(ot == 1), skip_group_check=True),
                        inc=(self.s_ex if (ot == 1 and sub == 3) else None))
            P.wait("vector", self.s_ex)
            for lane in range(2):
                P.emit("vector", lambda E, gk=gk, r=r, lane=lane: E.tensor_copy(
                    v_sb[:].rearrange("p (k c) -> p k c", c=130)
                    [:, gk * 4:(gk + 1) * 4, lane * 65:lane * 65 + 64],
                    self.sreg(r)
                    .rearrange("p (k c) -> p k c", c=128)
                    [:, :, lane * 64:(lane + 1) * 64]),
                    inc=(self.s_xv if lane == 1 else None))
            self.release_region(r, self.s_xv)

        for qc in range(4):
            r = qc
            self.use_region(r)
            for m in range(6):
                P.emit("tensor", lambda E, qc=qc, m=m, r=r, p=p: E.matmul(
                    self.sreg(r),
                    wqn_sb[:, m * 1024 + p * 128:m * 1024 + (p + 1) * 128],
                    normq[:, m * QL + qc * 512:m * QL + (qc + 1) * 512],
                    start=(m == 0), stop=(m == 5), skip_group_check=True),
                    inc=(self.s_ex if m == 5 else None))
            P.wait("vector", self.s_ex)
            P.emit("vector", lambda E, qc=qc, r=r: E.tensor_scalar(
                qcur[0][0:64, qc * 512:(qc + 1) * 512],
                self.sreg(r, 0, 64), LAM, None, ALU.mult))
            P.emit("vector", lambda E, qc=qc, r=r: E.tensor_scalar(
                qcur[1][64:128, qc * 512:(qc + 1) * 512],
                self.sreg(r, 64, 128), LAM, None, ALU.mult),
                inc=self.s_qe)
            self.release_region(r, self.s_qe)
        for par in range(2):
            hh = 2 * p + par
            if par == 0:
                c0, rl, rh, mw = hh * 32, 64, 96, 96
            else:
                c0, rl, rh, mw = 32 + hh * 32, 32, 64, 64
            for qc in range(4):
                r = qc
                self.use_region(r)
                for m in range(6):
                    P.emit("tensor", lambda E, qc=qc, m=m, r=r, c0=c0, mw=mw: E.matmul(
                        self.sreg(r, 0, mw),
                        wqr_sb[:, m * 576 + c0:m * 576 + c0 + mw],
                        normq[:, m * QL + qc * 512:m * QL + (qc + 1) * 512],
                        start=(m == 0), stop=(m == 5), skip_group_check=True),
                        inc=(self.s_ex if m == 5 else None))
                P.wait("vector", self.s_ex)
                P.emit("vector", lambda E, qc=qc, r=r, par=par, rl=rl, rh=rh: E.tensor_mul(
                    qcur[par][rl:rh, qc * 512:(qc + 1) * 512],
                    self.sreg(r, rl, rh),
                    self.tabq_sb[rl:rh, qc * 512:(qc + 1) * 512]), inc=self.s_qe)
                self.release_region(r, self.s_qe)
        P.wait("gpsimd", self.s_bn)
        P.dma("gpsimd", lambda E, p=p: E.dma_start(
            out=self.bmo[0:1, p:p + 1], in_=self.bm2[1:2, p:p + 1]), self.s_od)

    def emit_mhat(self, h, par, p, qcur, qsq, mrow):
        P = self.P
        rl, rh = (0, 96) if par == 0 else (32, 128)
        arow = 96 if par == 0 else 0
        if par == 0:
            bsrc = self.bm2[0:1, p:p + 1]
            P.wait("vector", self.s_bn)
        else:
            P.wait("vector", self.s_od)
            bsrc = self.bmo[0:1, p:p + 1]
        P.emit("vector", lambda E, bsrc=bsrc: E.tensor_add(
            self.lnb[0:1, 0:1], bsrc, self.krm[0:1, 0:1]), inc=self.s_vc)
        P.wait("vector", self.s_qe)
        if len(self.mh_qsq_at) >= 1:
            P.wait("vector", self.s_mh, self.mh_qsq_at[-1])
        qpieces = [(rl, rh)] if rl == 0 else [(32, 64), (64, 128)]
        for pi, (a0, a1) in enumerate(qpieces):
            P.emit("vector", lambda E, par=par, a0=a0, a1=a1: E.tensor_mul(
                qsq[a0:a1, :], qcur[par][a0:a1, :], qcur[par][a0:a1, :]),
                inc=(self.s_vc if pi == len(qpieces) - 1 else None))
        P.wait("scalar", self.s_vc)
        P.emit("scalar", lambda E: E.activation(
            self.lnb[0:1, 1:2], self.lnb[0:1, 0:1], AF.Ln), inc=self.s_sc)
        P.wait("scalar", self.s_sc)
        P.emit("scalar", lambda E: E.activation(
            self.lnb[0:1, 1:2], self.lnb[0:1, 1:2], AF.Copy, bias=0.0, scale=0.5),
            inc=self.s_sc)
        for qc in range(4):
            self.use_region(qc)
            P.wait("tensor", self.s_vc)
            pieces = [(rl, rh)] if rl == 0 else [(32, 64), (64, 128)]
            for pi, (a0, a1) in enumerate(pieces):
                P.emit("tensor", lambda E, qc=qc, a0=a0, a1=a1, pi=pi,
                       np_=len(pieces): E.matmul(
                    self.sreg(qc, 0, 1),
                    self.ones_bf[a0:a1, :], qsq[a0:a1, qc * 512:(qc + 1) * 512],
                    start=(pi == 0), stop=(pi == np_ - 1), skip_group_check=True),
                    inc=(self.s_mh if pi == len(pieces) - 1 else None))
        self.mh_qsq_at.append(self.s_mh.n)
        P.wait("scalar", self.s_mh)
        P.emit("scalar", lambda E: E.activation(
            mrow[0:1, 0:1024], self.S[0][0:1, 0:1024], AF.Ln), inc=self.s_sc)
        P.emit("scalar", lambda E: E.activation(
            mrow[0:1, 1024:2048], self.S[1][0:1, 0:1024], AF.Ln), inc=self.s_sc)
        P.wait("scalar", self.s_sc)
        P.emit("scalar", lambda E, par=par, arow=arow: E.activation(
            qcur[par][arow:arow + 1, :], mrow[:], AF.Exp,
            bias=self.lnb[0:1, 1:2], scale=0.5), inc=self.s_mh)
        for r in range(4):
            self.release_region(r, self.s_mh)

    def emit_head(self, h, par, qcur, pp, o_ps, dn_tmp, aostg):
        P = self.P
        kaug, v_sb = self.kaug, self.v_sb
        gbase = self.git_base
        P.wait("tensor", self.s_oe)
        for r in range(4):
            P.emit("tensor", lambda E, r=r: E.matmul(
                o_ps[0:65, r * 512:(r + 1) * 512],
                self.zer_bf[:, 0:65], self.zer_bf[:, 0:512],
                start=True, stop=False, skip_group_check=True), inc=self.s_pv)
        P.wait("tensor", self.s_mh)
        P.wait("tensor", self.s_qe)
        P.wait("tensor", self.s_xv)

        class Sub:

            def __init__(self, sems):
                self.ops = []
                self.sems = sems

            def emit(self, fn, inc=None, k=1, count=True):
                if inc is None:
                    self.ops.append(fn)
                    return

                def g(E, fn=fn, sem=inc, k=k):
                    fn(E).then_inc(sem.h, k)
                self.ops.append(g)
                if count:
                    inc.n += k

            def wait(self, sem, val):
                if val > 0:
                    self.ops.append(lambda E, hh=sem.h, v=val: E.wait_ge(hh, v))

        def pps(t, hf=None):
            j = t % 4
            buf = pp[j // 2]
            c0 = (j % 2) * 1024
            if hf is None:
                return buf[:, c0:c0 + 1024]
            return buf[:, c0 + hf * 512:c0 + hf * 512 + 512]

        def emit_exp(t, hf=None):
            P.wait("scalar", self.s_sg)
            gi = gbase + t
            if gi - 4 in self.pv_done:
                P.wait("scalar", self.s_pv, self.pv_done[gi - 4])
            if hf is None:
                P.emit("scalar", lambda E, t=t: E.activation(
                    pps(t), self.S[t % 2][:, 0:1024], AF.Exp), inc=self.s_eg)
            else:
                P.emit("scalar", lambda E, t=t, hf=hf: E.activation(
                    pps(t, hf), self.S[t % 2][:, hf * 512:(hf + 1) * 512],
                    AF.Exp), inc=self.s_eg)
            self.eg_hist.append(self.s_eg.n)

        for sg in range(7):
            bodyA = Sub(self)
            bodyB = Sub(self)
            sinks = [bodyA, bodyB]

            def rec_pv(t):
                for s in sinks:
                    s.wait(self.s_eg, self.eg_hist[-1])
                for hf in range(2):
                    for si, s in enumerate(sinks):
                        cqv = si * 2 + hf
                        s.emit(lambda E, t=t, par=par, cqv=cqv, hf=hf: E.matmul(
                            o_ps[0:65, cqv * 512:(cqv + 1) * 512],
                            v_sb[:, t * 130 + par * 65:t * 130 + par * 65 + 65],
                            pps(t, hf),
                            start=False, stop=False, skip_group_check=True),
                            inc=(self.s_pv if hf == 1 else None), count=(si == 0))
                self.pv_done[gbase + t] = self.s_pv.n

            for i in range(8):
                t = sg * 8 + i
                if len(self.eg_hist) >= 2:
                    for s in sinks:
                        s.wait(self.s_eg, self.eg_hist[-2])
                for hf in range(2):
                    for si, s in enumerate(sinks):
                        cqv = si * 2 + hf
                        s.emit(lambda E, t=t, par=par, cqv=cqv, hf=hf: E.matmul(
                            self.sreg((t % 2) * 2 + hf),
                            kaug[par][:, t * 128:(t + 1) * 128],
                            qcur[par][:, cqv * 512:(cqv + 1) * 512],
                            start=True, stop=True, skip_group_check=True),
                            inc=(self.s_sg if hf == 1 else None), count=(si == 0))
                if i > 0:
                    rec_pv(t - 1)
                emit_exp(t)
            rec_pv(sg * 8 + 7)

            def run_branch(E, sg=sg, A=bodyA.ops, B=bodyB.ops):
                E.reg_load(self.selreg, self.qtab_sb[0:1, sg:sg + 1])
                with E.If_eq(self.selreg, 0):
                    for f in A:
                        f(E)
                with E.Else():
                    for f in B:
                        f(E)
            P.q["tensor"].append(run_branch)

        def iter_info(t):
            i = t - NINT
            cq = 0 if i < 8 else 2
            rr = (i % 8) * 128
            if rr < 512:
                return [(0, cq, 1024 - rr), (1, cq + 1, None)]
            return [(1, cq + 1, 1024 - (rr - 512))]

        def rec_pv_diag(prev):
            t, info, (sem, val) = prev
            P.wait("tensor", sem, val)
            for idx, (hf, cqv, moff) in enumerate(info):
                last = idx == len(info) - 1
                P.emit("tensor", lambda E, t=t, par=par, hf=hf, cqv=cqv: E.matmul(
                    o_ps[0:65, cqv * 512:(cqv + 1) * 512],
                    v_sb[:, t * 130 + par * 65:t * 130 + par * 65 + 65],
                    pps(t, hf),
                    start=False, stop=False, skip_group_check=True),
                    inc=(self.s_pv if last else None))
            self.pv_done[gbase + t] = self.s_pv.n

        prev = None
        for t in range(NINT, NKS):
            info = iter_info(t)
            if len(self.eg_hist) >= 2:
                P.wait("tensor", self.s_eg, self.eg_hist[-2])
            for idx, (hf, cqv, moff) in enumerate(info):
                last = idx == len(info) - 1
                P.emit("tensor", lambda E, t=t, par=par, hf=hf, cqv=cqv: E.matmul(
                    self.sreg((t % 2) * 2 + hf),
                    kaug[par][:, t * 128:(t + 1) * 128],
                    qcur[par][:, cqv * 512:(cqv + 1) * 512],
                    start=True, stop=True, skip_group_check=True),
                    inc=(self.s_sg if last else None))
            if prev is not None:
                rec_pv_diag(prev)
            emit_exp(t, hf=(info[0][0] if len(info) == 1 else None))
            ready = (self.s_eg, self.eg_hist[-1])
            nm = 0
            for (hf, cqv, moff) in info:
                if moff is None:
                    continue
                P.wait("vector", self.s_eg)
                P.emit("vector", lambda E, t=t, hf=hf, moff=moff: E.tensor_mul(
                    pps(t, hf), pps(t, hf),
                    self.stair_sb[:, moff:moff + 512]), inc=self.s_mk)
                nm += 1
            if nm:
                ready = (self.s_mk, self.s_mk.n)
            prev = (t, info, ready)
        rec_pv_diag(prev)
        self.git_base += NKS
        for r in range(4):
            self.release_region(r, self.s_eg)

        P.wait("vector", self.s_pv)
        if len(self.od_at) >= 1:
            P.wait("vector", self.s_od, self.od_at[-1])
        P.emit("vector", lambda E, h=h: E.tensor_copy(
            aostg[:], o_ps[0:64, :]), inc=self.s_oe)
        P.emit("vector", lambda E: E.tensor_copy(
            dn_tmp[64:65, :], o_ps[64:65, :]), inc=self.s_oe)
        P.wait("gpsimd", self.s_oe)
        P.dma("gpsimd", lambda E, h=h: E.dma_start(
            out=self.ao_d[h], in_=aostg[:]), self.s_od)
        P.dma("gpsimd", lambda E, h=h: E.dma_start(
            out=self.denoms[h:h + 1, :], in_=dn_tmp[64:65, :]), self.s_od)
        self.od_at.append(self.s_od.n)

    def proj(self):
        nc, P, inp = self.nc, self.P, self.inp
        pj = contextlib.ExitStack()
        with pj:
            attn = self.sb(pj, "attn_all", [128, 8 * QL], BF16)
            wpj = self.sb(pj, "wproj_sb", [128, 8 * D], BF16)
            xown = [self.sb(pj, f"xown{i}", [128, D], F32) for i in range(2)]
            self.rrow = self.sb(pj, "rrow", [1, QL], F32)
            x2st = [self.sb(pj, f"x2st{i}", [128, D], F32) for i in range(2)]
            with (nc.psum_tensor("pj_ps", [128, 2048], F32) as pj_ps,
                  nc.psum_tensor("bc_ps", [128, 1024], F32) as bc_ps):
                self.phase_barrier(("sync", "gpsimd", "tensor"),
                                   ("eg", "pv", "oe", "od", "mh", "xv", "bn",
                                    "mk", "qe", "rc"))
                P.dma("gpsimd", lambda E: E.dma_start(
                    out=wpj[:].rearrange("p (j n) -> p j n", n=D),
                    in_=inp["w_proj"].rearrange("(j p) n -> p j n", p=128)), self.s_wc)
                P.wait("gpsimd", self.s_od)
                for j in range(8):
                    for sub in range(2):
                        P.dma("gpsimd", lambda E, j=j, sub=sub: E.dma_start(
                            out=attn[sub * 64:(sub + 1) * 64, j * QL:(j + 1) * QL],
                            in_=self.ao_d[2 * j + sub]), self.s_pj)
                P.wait("gpsimd", self.s_rc)
                ma_at = []
                for hh in range(H):
                    P.wait("gpsimd", self.s_pq)
                    P.dma("gpsimd", lambda E, hh=hh: E.dma_start(
                        out=self.rrow[:], in_=self.recip[hh:hh + 1, :]), self.s_pj)
                    P.wait("tensor", self.s_pj)
                    for k4 in range(4):
                        reg = k4 % 2
                        if len(ma_at) >= 2:
                            P.wait("tensor", self.s_ma, ma_at[-2])
                        P.emit("tensor", lambda E, k4=k4, reg=reg: E.matmul(
                            bc_ps[:, reg * 512:(reg + 1) * 512],
                            self.ones_row[:],
                            self.rrow[0:1, k4 * 512:(k4 + 1) * 512],
                            start=True, stop=True, skip_group_check=True),
                            inc=self.s_pq)
                        P.wait("vector", self.s_pq)
                        r0 = (hh % 2) * 64
                        P.emit("vector", lambda E, hh=hh, k4=k4, reg=reg, r0=r0:
                               E.tensor_mul(
                            attn[r0:r0 + 64,
                                 (hh // 2) * QL + k4 * 512:(hh // 2) * QL + (k4 + 1) * 512],
                            attn[r0:r0 + 64,
                                 (hh // 2) * QL + k4 * 512:(hh // 2) * QL + (k4 + 1) * 512],
                            bc_ps[r0:r0 + 64, reg * 512:(reg + 1) * 512]),
                            inc=self.s_ma)
                        ma_at.append(self.s_ma.n)
                if "pn" in self.dbg:
                    P.wait("gpsimd", self.s_ma)
                    P.dma("gpsimd", lambda E: E.dma_start(
                        out=self.dbg_t["pn"], in_=attn[:]), self.s_ou)
                    P.dma("gpsimd", lambda E: E.dma_start(
                        out=self.dbg_t["rcp"], in_=self.recip[:]), self.s_ou)
                P.wait("tensor", self.s_ma)
                P.wait("tensor", self.s_wc)
                adds_at = []
                for tt in range(16):
                    if tt >= 2:
                        P.wait("sync", self.s_vc, adds_at[tt - 2])
                    P.dma("sync", lambda E, tt=tt: E.dma_start(
                        out=xown[tt % 2][:],
                        in_=inp["xp"][(56 + tt) * 128:(57 + tt) * 128, :]), self.s_x)
                    P.wait("tensor", self.s_vc)
                    for j in range(8):
                        for dc in range(4):
                            P.emit("tensor", lambda E, tt=tt, j=j, dc=dc: E.matmul(
                                pj_ps[:, dc * 512:(dc + 1) * 512],
                                attn[:, j * QL + tt * 128:j * QL + (tt + 1) * 128],
                                wpj[:, j * D + dc * 512:j * D + (dc + 1) * 512],
                                start=(j == 0), stop=(j == 7)),
                                inc=(self.s_mm if j == 7 and dc == 3 else None))
                    P.wait("vector", self.s_mm)
                    P.wait("vector", self.s_x)
                    if tt >= 2:
                        P.wait("vector", self.s_x2, 16 * (tt - 1))
                    P.emit("vector", lambda E, tt=tt: E.tensor_add(
                        x2st[tt % 2][:], pj_ps[:], xown[tt % 2][:]), inc=self.s_vc)
                    adds_at.append(self.s_vc.n)
                    P.wait("sync", self.s_vc)
                    P.dma("sync", lambda E, tt=tt: E.dma_start(
                        out=self.x2_d[tt * 128:(tt + 1) * 128, :],
                        in_=x2st[tt % 2][:]), self.s_x2)

    def ffn(self):
        nc, P, inp = self.nc, self.P, self.inp
        ff = contextlib.ExitStack()
        with ff:
            x2t = [self.sb(ff, f"fx2t{i}", [128, D], F32) for i in range(2)]
            dump = self.sb(ff, "fdump", [128, D], BF16)
            xlnbf = [self.sb(ff, f"fxlnbf{i}", [128, D], BF16) for i in range(2)]
            xlnT = self.sb(ff, "fxlnT", [128, 16 * 512], BF16)
            hbuf = self.sb(ff, "hbuf", [128, 64 * 512], BF16)
            wg8 = [self.sb(ff, f"wg8{i}", [128, 2048], U8) for i in range(2)]
            wu8 = [self.sb(ff, f"wu8{i}", [128, 2048], U8) for i in range(2)]
            wgb = [self.sb(ff, f"wgb{i}", [128, 2048], BF16) for i in range(2)]
            wub = [self.sb(ff, f"wub{i}", [128, 2048], BF16) for i in range(2)]
            wd8 = self.sb(ff, "wd8", [128, 8192], U8)
            wdb = [self.sb(ff, f"wdb{i}", [128, 4096], BF16) for i in range(2)]
            sil = self.sb(ff, "sil", [128, 512], F32)
            ytmp = self.sb(ff, "ytmp", [128, 16 * 512], BF16)
            x3st = [self.sb(ff, f"x3st{i}", [128, D], F32) for i in range(2)]

            with (nc.psum_tensor("big_ps", [128, 2048], F32) as big_ps,
                  nc.psum_tensor("g_ps", [128, 512], F32) as g_ps,
                  nc.psum_tensor("u_ps", [128, 512], F32) as u_ps,
                  nc.psum_tensor("y_ps", [128, 1024], F32) as y_ps):
                self.phase_barrier(("sync", "gpsimd", "tensor"),
                                    ("x2", "vc", "mm", "ma", "pq", "x"))
                fo_at = []
                fw0 = self.s_fw.n
                fz0 = self.s_fz.n
                te_mark = [0]
                self.fd_marks = []

                def stage_a(pa):
                    for it in range(4):
                        g = pa * 4 + it
                        buf = g % 2
                        if g >= 2:
                            P.wait("gpsimd", self.s_ap, self.s_ap.n - 1)
                            if fo_at:
                                P.wait("gpsimd", self.s_fo, fo_at[-1])
                        P.dma("gpsimd", lambda E, g=g, buf=buf: E.dma_start(
                            out=x2t[buf][:],
                            in_=self.x2_d[g * 128:(g + 1) * 128, :]), self.s_x)
                        P.wait("scalar", self.s_x)
                        P.wait("vector", self.s_x)
                        P.wait("vector", self.s_tp, self.s_tp.n - 4)
                        self.emit_ln(x2t[buf][:], xlnbf[g % 2][:], dump, g % 2)
                        P.wait("tensor", self.s_ap)
                        if fo_at:
                            P.wait("tensor", self.s_fo, fo_at[-1])
                        self.emit_transpose(xlnbf[g % 2], xlnT, it, big_ps)
                    te_mark[0] = self.s_te.n

                def stage_b(pa):
                    P.wait("tensor", self.s_te)
                    P.dma("sync", lambda E: E.dma_start(
                        out=wg8[0][:], in_=inp["wg_t"][0]), self.s_fw)
                    P.dma("sync", lambda E: E.dma_start(
                        out=wu8[0][:], in_=inp["wu_t"][0]), self.s_fw)
                    for i in range(64):
                        if i + 1 < 64:
                            P.wait("sync", self.s_sc, self.cast_at[-1]
                                   if getattr(self, "cast_at", None) else 0)
                            P.dma("sync", lambda E, i=i: E.dma_start(
                                out=wg8[(i + 1) % 2][:], in_=inp["wg_t"][i + 1]),
                                self.s_fw)
                            P.dma("sync", lambda E, i=i: E.dma_start(
                                out=wu8[(i + 1) % 2][:], in_=inp["wu_t"][i + 1]),
                                self.s_fw)
                        P.wait("scalar", self.s_fw, fw0 + 32 * (64 * pa + i + 1))
                        P.wait("scalar", self.s_fg, self.s_fg.n - 2)
                        P.emit("scalar", lambda E, i=i: E.activation(
                            wgb[i % 2][:], wg8[i % 2][:], AF.Copy, bias=-128.0,
                            scale=1.0))
                        P.emit("scalar", lambda E, i=i: E.activation(
                            wub[i % 2][:], wu8[i % 2][:], AF.Copy, bias=-128.0,
                            scale=1.0), inc=self.s_sc)
                        if not hasattr(self, "cast_at"):
                            self.cast_at = []
                        self.cast_at.append(self.s_sc.n)
                        P.wait("tensor", self.s_sc)
                        for j in range(16):
                            P.emit("tensor", lambda E, i=i, j=j: E.matmul(
                                g_ps[:], wgb[i % 2][:, j * 128:(j + 1) * 128],
                                xlnT[:, j * 512:(j + 1) * 512],
                                start=(j == 0), stop=(j == 15)),
                                inc=(self.s_fg if j == 15 else None))
                        P.wait("tensor", self.s_fh)
                        for j in range(16):
                            P.emit("tensor", lambda E, i=i, j=j: E.matmul(
                                u_ps[:], wub[i % 2][:, j * 128:(j + 1) * 128],
                                xlnT[:, j * 512:(j + 1) * 512],
                                start=(j == 0), stop=(j == 15)),
                                inc=(self.s_fg if j == 15 else None))
                        P.wait("scalar", self.s_fg, self.s_fg.n - 1)
                        P.emit("scalar", lambda E, i=i: E.activation(
                            sil[:], g_ps[:], AF.Silu, bias=0.0,
                            scale=self.gsc_sb[:, i:i + 1]), inc=self.s_sc)
                        P.wait("vector", self.s_sc)
                        P.wait("vector", self.s_fg)
                        P.emit("vector", lambda E, i=i: E.scalar_tensor_tensor(
                            hbuf[:, i * 512:(i + 1) * 512], u_ps[:],
                            self.usc_sb[:, i:i + 1], sil[:], ALU.mult, ALU.mult),
                            inc=self.s_fh)

                def stage_c(pa):
                    P.dma("sync", lambda E: E.dma_start(
                        out=wd8[:], in_=inp["wd_t"][0]), self.s_fz)
                    dcast_at = []
                    for ht in range(16):
                        P.wait("scalar", self.s_fz, fz0 + 16 * (16 * pa + ht + 1))
                        for half in range(2):
                            if self.fd_marks:
                                sem, val = self.fd_marks[-1][half]
                                P.wait("scalar", sem, val)
                            P.emit("scalar", lambda E, ht=ht, half=half: E.activation(
                                wdb[half][:], wd8[:, half * 4096:(half + 1) * 4096],
                                AF.Copy, bias=-128.0, scale=1.0), inc=self.s_sc)
                            dcast_at.append(self.s_sc.n)
                        if ht + 1 < 16:
                            P.wait("sync", self.s_sc, dcast_at[-1])
                            P.dma("sync", lambda E, ht=ht: E.dma_start(
                                out=wd8[:], in_=inp["wd_t"][ht + 1]), self.s_fz)
                        P.wait("tensor", self.s_sc, dcast_at[-2])
                        P.wait("tensor", self.s_fh)
                        P.wait("tensor", self.s_fy, self.s_fy.n - 1)
                        yp = y_ps[:, (ht % 2) * 512:(ht % 2 + 1) * 512]
                        marks = [None, None]
                        for j in range(64):
                            if j == 32:
                                P.wait("tensor", self.s_sc, dcast_at[-1])
                            P.emit("tensor", lambda E, ht=ht, j=j, yp=yp: E.matmul(
                                yp, wdb[j // 32][:, (j % 32) * 128:(j % 32 + 1) * 128],
                                hbuf[:, j * 512:(j + 1) * 512],
                                start=(j == 0), stop=(j == 63)),
                                inc=(self.s_fd if j == 31 else
                                     (self.s_fy if j == 63 else None)))
                            if j == 31:
                                marks[0] = (self.s_fd, self.s_fd.n)
                            if j == 63:
                                marks[1] = (self.s_fy, self.s_fy.n)
                        self.fd_marks.append(marks)
                        P.wait("vector", self.s_fy)
                        P.emit("vector", lambda E, ht=ht, yp=yp: E.tensor_scalar(
                            ytmp[:, ht * 512:(ht + 1) * 512], yp,
                            self.dsc_sb[:, ht:ht + 1], None, ALU.mult), inc=self.s_fy)
                    if "ffx" in self.dbg and pa == 0:
                        P.wait("gpsimd", self.s_fy)
                        P.wait("gpsimd", self.s_te)
                        P.dma("gpsimd", lambda E: E.dma_start(
                            out=self.dbg_t["f1"], in_=xlnT[:]), self.s_ou)
                        P.dma("gpsimd", lambda E: E.dma_start(
                            out=self.dbg_t["f2"], in_=hbuf[:]), self.s_ou)
                        P.dma("gpsimd", lambda E: E.dma_start(
                            out=self.dbg_t["f3"], in_=ytmp[:]), self.s_ou)

                def stage_d(pa):
                    for it in range(4):
                        g = pa * 4 + it
                        P.wait("tensor", self.s_fy)
                        P.wait("tensor", self.s_te, te_mark[0])
                        if fo_at:
                            P.wait("tensor", self.s_fo, fo_at[-1])
                        for ht in range(16):
                            P.emit("tensor", lambda E, it=it, ht=ht: E.matmul(
                                big_ps[:, ht * 128:(ht + 1) * 128],
                                ytmp[:, ht * 512 + it * 128:ht * 512 + (it + 1) * 128],
                                self.ident_sb[:], start=True, stop=True),
                                inc=self.s_tp)
                        P.wait("gpsimd", self.s_ap)
                        P.wait("gpsimd", self.s_fo)
                        P.dma("gpsimd", lambda E, g=g: E.dma_start(
                            out=x2t[g % 2][:],
                            in_=self.x2_d[g * 128:(g + 1) * 128, :]), self.s_x)
                        P.wait("vector", self.s_tp)
                        P.wait("vector", self.s_x)
                        if len(fo_at) >= 2:
                            P.wait("vector", self.s_ou, 16 * (len(fo_at) - 1))
                        P.emit("vector", lambda E, g=g: E.tensor_add(
                            x3st[g % 2][:], big_ps[:], x2t[g % 2][:]), inc=self.s_fo)
                        fo_at.append(self.s_fo.n)
                        P.wait("gpsimd", self.s_fo)
                        P.dma("gpsimd", lambda E, g=g: E.dma_start(
                            out=self.out[g * 128:(g + 1) * 128, :],
                            in_=x3st[g % 2][:]), self.s_ou)

                stage_a(0)
                for pa in range(4):
                    stage_b(pa)
                    stage_c(pa)
                    if pa + 1 < 4:
                        stage_a(pa + 1)
                    stage_d(pa)


def kernel(**inputs):
    mk = MK(DBG)
    nc = mk.build()
    maps = [core_inputs(c, inputs) for c in range(8)]
    res = run_bass_kernel_spmd(nc, maps, core_ids=list(range(8)))
    outp = np.zeros((2, 8192, D), np.float32)
    for c in range(8):
        o = res.results[c]["out"]
        outp[0, 1024 * c:1024 * (c + 1)] = o[0:1024]
        outp[1, 1024 * (7 - c):1024 * (8 - c)] = o[1024:2048]
    return outp


# version 12 (final)
# speedup vs baseline: 1.1432x; 1.0075x over previous
import contextlib
import numpy as np
import ml_dtypes
import concourse.bass as bass
import concourse.mybir as mybir
from concourse.bass_utils import run_bass_kernel_spmd
from concourse.expressions_rust import RuntimeValue

BFNP = ml_dtypes.bfloat16
F32 = mybir.dt.float32
BF16 = mybir.dt.bfloat16
U8 = mybir.dt.uint8
U32 = mybir.dt.uint32
AF = mybir.ActivationFunctionType
ALU = mybir.AluOpType
AX = mybir.AxisListType

D = 2048
H = 16
QR, KVR = 768, 256
P_TOT = 9216
NKS = 72
NINT = 56
QL = 2048
NB = 18
EPS_LN = 1e-5
EPS_RMS = 1e-6
LAM = float(96.0 ** -0.25)
SKR = float((16.0 / np.sqrt(96.0)) ** 0.5)
SQRP = SKR / 16.0
DBG = ()
SPLIT_EVAC = False

ENGS = ("tensor", "vector", "scalar", "sync", "gpsimd")


class Sem:
    __slots__ = ("h", "n")

    def __init__(self, h):
        self.h = h
        self.n = 0


class Prog:
    def __init__(self):
        self.q = {e: [] for e in ENGS}

    def emit(self, e, fn, inc=None, k=1):
        if inc is None:
            self.q[e].append(fn)
            return

        def g(E, fn=fn, sem=inc, k=k):
            fn(E).then_inc(sem.h, k)
        self.q[e].append(g)
        inc.n += k

    def dma(self, e, fn, sem):
        self.emit(e, fn, inc=sem, k=16)

    def wait(self, e, sem, val=None):
        v = sem.n if val is None else val
        if v <= 0:
            return
        self.q[e].append(lambda E, h=sem.h, v=v: E.wait_ge(h, v))

    def replay(self, block):
        for e in ENGS:
            ops = self.q[e]
            getattr(block, e)(lambda E, ops=ops: [f(E) for f in ops])


def _rope_tab(positions):
    f = np.arange(32) // 2
    invf = 10000.0 ** (-(f / 16.0))
    return np.cos(positions[None, :].astype(np.float64) * invf[:, None])


def core_inputs(c, a):
    QA0 = 1024 * c
    QB0 = 1024 * (7 - c)
    PA = QA0 + 1024
    PB = QB0 + 1024
    x = np.asarray(a["L0_attn_x"])
    d = {}
    d["xp"] = np.ascontiguousarray(
        np.concatenate([x[0, :QA0], x[1, :QB0], x[0, QA0:PA], x[1, QB0:PB]], axis=0))
    pos_k = np.concatenate([np.arange(QA0), np.arange(QB0),
                            np.arange(QA0, PA), np.arange(QB0, PB)])
    pos_q = np.concatenate([np.arange(QA0, PA), np.arange(QB0, PB)])
    ck = _rope_tab(pos_k)
    cq = _rope_tab(pos_q)
    tabK = np.zeros((128, P_TOT), np.float32)
    tabK[32:64] = tabK[64:96] = (np.sqrt(2.0) * (SKR / 16.0) * ck).astype(np.float32)
    tabQ = np.zeros((128, QL), np.float32)
    tabQ[32:64] = tabQ[64:96] = (np.sqrt(2.0) * SQRP * cq).astype(np.float32)
    d["tabK"] = tabK
    d["tabQ"] = tabQ
    qt = np.zeros((1, 128), np.uint32)
    for g in range(NINT // 8):
        qt[0, g] = 0 if g < c else 1
    d["qtab"] = qt
    d["w_cq"] = np.asarray(a["L0_attn_w_cq"])
    d["w_qn"] = np.asarray(a["L0_attn_w_q_nope"])
    d["w_qr_pad"] = np.ascontiguousarray(np.concatenate(
        [np.zeros((QR, 64), np.float32), np.asarray(a["L0_attn_w_q_rope"])], axis=1))
    d["q_g"] = np.ascontiguousarray(np.asarray(a["L0_attn_q_g"]).reshape(6, 128).T)
    d["w_ckv"] = np.asarray(a["L0_attn_w_ckv"])
    d["w_kn"] = np.asarray(a["L0_attn_w_k_nope"])
    d["w_v"] = np.asarray(a["L0_attn_w_v"])
    d["kv_g"] = np.ascontiguousarray(np.asarray(a["L0_attn_kv_g"]).reshape(2, 128).T)
    wkr = np.asarray(a["L0_attn_w_k_rope"])
    d["w_kr_pad"] = np.ascontiguousarray(np.concatenate(
        [np.zeros((D, 32), np.float32), wkr, wkr], axis=1))
    d["w_proj"] = np.asarray(a["L0_attn_w_proj"])
    wgT = np.asarray(a["L0_ffn_gate_w"]).astype(np.uint8).T
    wuT = np.asarray(a["L0_ffn_up_w"]).astype(np.uint8).T
    wdT = np.asarray(a["L0_ffn_down_w"]).astype(np.uint8).T
    d["wg_t"] = np.ascontiguousarray(
        wgT.reshape(16, 128, 64, 128).transpose(2, 1, 0, 3).reshape(64, 128, 2048))
    d["wu_t"] = np.ascontiguousarray(
        wuT.reshape(16, 128, 64, 128).transpose(2, 1, 0, 3).reshape(64, 128, 2048))
    d["wd_t"] = np.ascontiguousarray(
        wdT.reshape(64, 128, 16, 128).transpose(2, 1, 0, 3).reshape(16, 128, 8192))
    d["gsc"] = np.ascontiguousarray(np.asarray(a["L0_ffn_gate_scale"]).reshape(64, 128).T)
    d["usc"] = np.ascontiguousarray(np.asarray(a["L0_ffn_up_scale"]).reshape(64, 128).T)
    d["dsc"] = np.ascontiguousarray(np.asarray(a["L0_ffn_down_scale"]).reshape(16, 128).T)
    stair = np.zeros((128, 2048), np.float32)
    for i in range(128):
        stair[i, 1024 + i:] = 1.0
    d["stair"] = stair.astype(BFNP)
    d["ident"] = np.eye(128, dtype=BFNP)
    return d


class MK:
    def __init__(self, dbg=()):
        self.dbg = dbg
        self.nc = bass.Bass("TRN2", target_bir_lowering=False)
        self.P = Prog()

    def sem(self, name):
        s = Sem(self.st.enter_context(self.nc.semaphore(name)))
        setattr(self, "s_" + name, s)
        return s

    def sb(self, stack, name, shape, dt):
        return stack.enter_context(self.nc.sbuf_tensor(name, shape, dt))

    def build(self):
        nc, P = self.nc, self.P
        dbg = self.dbg
        self.inp = {}
        for name, shape, dt in [
            ("xp", [P_TOT, D], F32), ("tabK", [128, P_TOT], F32),
            ("tabQ", [128, QL], F32), ("qtab", [1, 128], U32),
            ("w_cq", [D, QR], F32), ("w_qn", [QR, 1024], F32),
            ("w_qr_pad", [QR, 576], F32), ("q_g", [128, 6], F32),
            ("w_ckv", [D, KVR], F32), ("w_kn", [KVR, 1024], F32),
            ("w_v", [KVR, 1024], F32), ("kv_g", [128, 2], F32),
            ("w_kr_pad", [D, 96], F32), ("w_proj", [1024, D], F32),
            ("wg_t", [64, 128, 2048], U8), ("wu_t", [64, 128, 2048], U8),
            ("wd_t", [16, 128, 8192], U8),
            ("gsc", [128, 64], F32), ("usc", [128, 64], F32), ("dsc", [128, 16], F32),
            ("stair", [128, 2048], BF16), ("ident", [128, 128], BF16),
        ]:
            self.inp[name] = nc.dram_tensor(name, shape, dt, kind="ExternalInput").ap()
        inp = self.inp
        self.out = nc.dram_tensor("out", [QL, D], F32, kind="ExternalOutput").ap()
        self.x2_d = nc.dram_tensor(
            "x2_scr", [QL, D], F32,
            kind="ExternalOutput" if "x2" in dbg else "Internal").ap()
        self.ao_d = nc.dram_tensor(
            "ao_scr", [H, 64, QL], BF16,
            kind="ExternalOutput" if "ao" in dbg else "Internal").ap()
        self.dbg_t = {}
        for key, nm, shape, dt, want in [
            ("ckv", "dbg_ckv", [128, 2 * P_TOT], BF16, "ckv"),
            ("k0", "dbg_k0", [128, P_TOT], BF16, "kaug"),
            ("k1", "dbg_k1", [128, P_TOT], BF16, "kaug"),
            ("vv", "dbg_v", [128, NKS * 130], BF16, "kaug"),
            ("normq", "dbg_normq", [128, 6 * QL], BF16, "normq"),
            ("q0", "dbg_q0", [128, QL], BF16, "qcur"),
            ("q1", "dbg_q1", [128, QL], BF16, "qcur"),
            ("den", "dbg_den", [16, QL], F32, "den"),
            ("xlnT", "dbg_xlnT", [128, 16 * 512], BF16, ("p1x", "prx")),
            ("stm", "dbg_stm", [128, 8], F32, "p1x"),
            ("rstdc", "dbg_rstdc", [128, 4], F32, "p1x"),
            ("rstdb", "dbg_rstdb", [128, 512], F32, ("p1x", "prx")),
            ("bm", "dbg_bm", [2, 16], F32, "bm"),
            ("pn", "dbg_pn", [128, 8 * QL], BF16, "pn"),
            ("f1", "dbg_f1", [128, 16 * 512], BF16, "ffx"),
            ("f2", "dbg_f2", [128, 64 * 512], BF16, "ffx"),
            ("f3", "dbg_f3", [128, 16 * 512], BF16, "ffx"),
            ("rcp", "dbg_rcp", [16, QL], F32, "pn"),
        ]:
            wl = want if isinstance(want, tuple) else (want,)
            if any(w in dbg for w in wl):
                self.dbg_t[key] = nc.dram_tensor(nm, shape, dt, kind="ExternalOutput").ap()

        self.ln_at = [None, None]
        self.sq_at = [None, None]
        self.krow_at = 0
        self.st = contextlib.ExitStack()
        with self.st:
            st = self.st
            for nm in ["ini", "x", "tk", "sc", "vc", "ap", "tp", "te", "mm", "ax",
                       "cw", "wc", "qe", "ex", "xv", "bn", "sg", "eg", "mk", "pv",
                       "oe", "od", "mh", "pj", "pq", "fw", "fg", "fh", "fy", "fo",
                       "ou", "x2", "rc", "ma", "fz", "fd"]:
                self.sem(nm)

            self.ones_bf = self.sb(st, "ones_bf", [128, 1], BF16)
            self.ones_row = self.sb(st, "ones_row", [1, 128], F32)
            self.eps_t = self.sb(st, "eps_t", [128, 2], F32)
            self.stair_sb = self.sb(st, "stair_sb", [128, 2048], BF16)
            self.ident_sb = self.sb(st, "ident_sb", [128, 128], BF16)
            self.qtab_sb = self.sb(st, "qtab_sb", [1, 128], U32)
            self.zer_bf = self.sb(st, "zer_bf", [128, 512], BF16)
            self.stm = self.sb(st, "stm", [128, 8], F32)
            self.rstd_c = self.sb(st, "rstd_c", [128, 4], F32)
            self.mneg_bf = self.sb(st, "mneg_bf", [128, 2], F32)
            self.rstd_bf = self.sb(st, "rstd_bf", [128, 4], BF16)
            self.ones_rbf = self.sb(st, "ones_rbf", [1, 128], BF16)
            self.bm2 = self.sb(st, "bm2", [2, 16], F32)
            self.bmo = self.sb(st, "bmo", [1, 16], F32)
            self.bm_scr = self.sb(st, "bm_scr", [2, 1], F32)
            self.krm = self.sb(st, "krm", [1, 2], F32)
            self.lnb = self.sb(st, "lnb", [1, 2], F32)
            self.gsc_sb = self.sb(st, "gsc_sb", [128, 64], F32)
            self.usc_sb = self.sb(st, "usc_sb", [128, 64], F32)
            self.dsc_sb = self.sb(st, "dsc_sb", [128, 16], F32)
            self.kvg_sb = self.sb(st, "kvg_sb", [128, 2], F32)
            self.qg_sb = self.sb(st, "qg_sb", [128, 6], F32)

            for nm, dst in [("stair", self.stair_sb), ("ident", self.ident_sb),
                            ("qtab", self.qtab_sb), ("gsc", self.gsc_sb),
                            ("usc", self.usc_sb), ("dsc", self.dsc_sb),
                            ("kv_g", self.kvg_sb), ("q_g", self.qg_sb)]:
                P.dma("sync", lambda E, nm=nm, dst=dst: E.dma_start(
                    out=dst[:], in_=inp[nm]), self.s_ini)
            P.emit("vector", lambda E: E.memset(self.ones_bf[:], 1.0))
            P.emit("vector", lambda E: E.memset(self.ones_row[:], 1.0))
            P.emit("vector", lambda E: E.memset(self.ones_rbf[:], 1.0))
            P.emit("vector", lambda E: E.memset(self.eps_t[:, 0:1], EPS_LN))
            P.emit("vector", lambda E: E.memset(self.eps_t[:, 1:2], EPS_RMS))
            P.emit("vector", lambda E: E.memset(self.zer_bf[:], 0.0))
            P.emit("vector", lambda E: E.memset(self.bm2[:], 0.0))
            P.emit("vector", lambda E: E.memset(self.krm[:], 0.0), inc=self.s_vc)

            dn_sc = contextlib.ExitStack()
            with dn_sc:
              self.denoms = self.sb(dn_sc, "denoms", [16, QL], F32)
              self.recip = self.sb(dn_sc, "recip", [16, QL], F32)
              ca = contextlib.ExitStack()
              with ca:
                  self.ckv_sb = self.sb(ca, "ckv_sb", [128, 2 * P_TOT], BF16)
                  self.kaug = [self.sb(ca, f"kaug{i}", [128, P_TOT], BF16) for i in range(2)]
                  self.v_sb = self.sb(ca, "v_sb", [128, NKS * 130], BF16)
                  self.normq = self.sb(ca, "normq", [128, 6 * QL], BF16)
                  self.tabq_sb = self.sb(ca, "tabq_sb", [128, QL], F32)

                  P.emit("vector", lambda E: E.memset(self.v_sb[:], 1.0))
                  P.emit("vector", lambda E: E.memset(self.kaug[0][96:128, :], 0.0))
                  P.emit("vector", lambda E: E.memset(self.kaug[0][96:97, :], -1.0))
                  P.emit("vector", lambda E: E.memset(self.kaug[1][0:32, :], 0.0))
                  P.emit("vector", lambda E: E.memset(self.kaug[1][0:1, :], -1.0),
                         inc=self.s_vc)

                  self.phase1()
                  if "p1only" not in dbg:
                      self.prep()
                  if "p1only" not in dbg and "noattn" not in dbg:
                      self.attention()

                  if "ckv" in dbg:
                      P.wait("gpsimd", self.s_cw)
                      P.dma("gpsimd", lambda E: E.dma_start(
                          out=self.dbg_t["ckv"], in_=self.ckv_sb[:]), self.s_ou)
                  if "kaug" in dbg:
                      P.wait("gpsimd", self.s_xv)
                      P.dma("gpsimd", lambda E: E.dma_start(
                          out=self.dbg_t["k0"], in_=self.kaug[0][:]), self.s_ou)
                      P.dma("gpsimd", lambda E: E.dma_start(
                          out=self.dbg_t["k1"], in_=self.kaug[1][:]), self.s_ou)
                      P.dma("gpsimd", lambda E: E.dma_start(
                          out=self.dbg_t["vv"], in_=self.v_sb[:]), self.s_ou)
                  if "normq" in dbg and "p1only" not in dbg:
                      P.wait("gpsimd", self.s_qe)
                      P.dma("gpsimd", lambda E: E.dma_start(
                          out=self.dbg_t["normq"], in_=self.normq[:]), self.s_ou)
                  if "bm" in dbg:
                      P.wait("gpsimd", self.s_bn)
                      P.dma("gpsimd", lambda E: E.dma_start(
                          out=self.dbg_t["bm"], in_=self.bm2[:]), self.s_ou)

              if "den" in dbg:
                  P.wait("gpsimd", self.s_od)
                  P.dma("gpsimd", lambda E: E.dma_start(
                      out=self.dbg_t["den"], in_=self.denoms[:]), self.s_ou)
              if "noftail" not in dbg:
                  self.proj()
            if "noftail" not in dbg:
                self.ffn()

            P.wait("gpsimd", self.s_ou)
            P.wait("sync", self.s_x2)

            with nc.Block() as block:
                P.replay(block)
        return nc

    def emit_ln(self, src, dst, dump, pa=0):
        P = self.P
        stm = self.stm[:, pa * 4:pa * 4 + 4]
        rsc = self.rstd_c[:, pa * 2:pa * 2 + 2]
        if self.ln_at[pa] is not None:
            P.wait("scalar", self.s_ap, self.ln_at[pa])
            P.wait("vector", self.s_ap, self.ln_at[pa])
        P.emit("scalar", lambda E: E.activation(
            dump[:], src, AF.Square, accum_out=stm[:, 1:2]), inc=self.s_sc)
        P.emit("vector", lambda E: E.reduce_sum(stm[:, 0:1], src, axis=AX.X),
               inc=self.s_vc)
        P.wait("scalar", self.s_vc)
        P.emit("scalar", lambda E: E.activation(
            stm[:, 2:3], stm[:, 0:1], AF.Square, bias=0.0, scale=1.0 / D),
            inc=self.s_sc)
        P.wait("scalar", self.s_sc)
        P.emit("scalar", lambda E: E.activation(
            stm[:, 2:3], stm[:, 2:3], AF.Identity, bias=0.0, scale=-1.0),
            inc=self.s_sc)
        P.wait("scalar", self.s_sc)
        P.emit("scalar", lambda E: E.activation(
            stm[:, 3:4], stm[:, 1:2], AF.Identity, bias=stm[:, 2:3], scale=1.0 / D),
            inc=self.s_sc)
        P.wait("scalar", self.s_sc)
        P.emit("scalar", lambda E: E.activation(
            rsc[:, 0:1], stm[:, 3:4], AF.Ln, bias=self.eps_t[:, 0:1]),
            inc=self.s_sc)
        P.wait("scalar", self.s_sc)
        P.emit("scalar", lambda E: E.activation(
            rsc[:, 0:1], rsc[:, 0:1], AF.Exp, bias=0.0, scale=-0.5),
            inc=self.s_sc)
        P.wait("vector", self.s_sc)
        P.emit("vector", lambda E: E.scalar_tensor_tensor(
            rsc[:, 1:2], stm[:, 0:1], -1.0 / D, rsc[:, 0:1],
            ALU.mult, ALU.mult), inc=self.s_vc)
        P.wait("vector", self.s_vc)
        P.emit("vector", lambda E: E.tensor_scalar(
            dst, src, rsc[:, 0:1], rsc[:, 1:2], ALU.mult, ALU.add),
            inc=self.s_ap)
        self.ln_at[pa] = self.s_ap.n

    def emit_center(self, src, dst, dump, pa, rslot=None):
        P = self.P
        stm = self.stm[:, pa * 4:pa * 4 + 4]
        if self.ln_at[pa] is not None:
            P.wait("vector", self.s_ap, self.ln_at[pa])
        P.emit("vector", lambda E: E.reduce_sum(stm[:, 0:1], src, axis=AX.X),
               inc=self.s_vc)
        P.wait("scalar", self.s_vc)
        P.emit("scalar", lambda E: E.activation(
            self.mneg_bf[:, pa:pa + 1], stm[:, 0:1], AF.Identity,
            bias=0.0, scale=-1.0 / D), inc=self.s_sc)
        P.wait("scalar", self.s_sc)
        P.wait("scalar", self.s_tp, self.s_tp.n)
        P.emit("scalar", lambda E: E.activation(
            dst, src, AF.Identity, bias=self.mneg_bf[:, pa:pa + 1], scale=1.0),
            inc=self.s_ap)
        self.ln_at[pa] = self.s_ap.n
        return self.s_ap.n

    def emit_rstd(self, dst, dump, pa, rslot):
        P = self.P
        stm = self.stm[:, pa * 4:pa * 4 + 4]
        if self.krow_at:
            P.wait("scalar", self.s_ax, self.krow_at)
        P.emit("scalar", lambda E: E.activation(
            dump[:], dst, AF.Square, accum_out=stm[:, 1:2]), inc=self.s_sc)
        self.sq_at[pa] = self.s_sc.n
        P.wait("scalar", self.s_sc)
        P.emit("scalar", lambda E: E.activation(
            stm[:, 2:3], stm[:, 1:2], AF.Ln, bias=self.eps_t[:, 0:1],
            scale=1.0 / D), inc=self.s_sc)
        P.wait("scalar", self.s_sc)
        P.emit("scalar", lambda E, rslot=rslot: E.activation(
            self.rstd_bf[:, rslot:rslot + 1], stm[:, 2:3], AF.Exp,
            bias=0.0, scale=-0.5), inc=self.s_sc)

    def emit_transpose(self, src_t, xlnT_t, it, tp_ps):
        P = self.P
        for grp in range(4):
            P.wait("tensor", self.s_te, self.s_te.n - 1)
            for j4 in range(4):
                j = grp * 4 + j4
                P.emit("tensor", lambda E, j=j, grp=grp, j4=j4: E.matmul(
                    tp_ps[:, (grp % 2) * 512 + j4 * 128:(grp % 2) * 512 + (j4 + 1) * 128],
                    src_t[:, j * 128:(j + 1) * 128], self.ident_sb[:],
                    start=True, stop=True),
                    inc=(self.s_tp if j4 == 3 else None))
            dst = (xlnT_t[:].rearrange("p (j n) -> p j n", n=512)
                   [:, grp * 4:(grp + 1) * 4, it * 128:(it + 1) * 128])
            srcv = (tp_ps[:, (grp % 2) * 512:((grp % 2) + 1) * 512]
                    .rearrange("p (j n) -> p j n", n=128))
            if grp % 2 == 0 or not SPLIT_EVAC:
                P.wait("vector", self.s_tp)
                P.emit("vector", lambda E, dst=dst, srcv=srcv: E.tensor_copy(
                    dst, srcv), inc=self.s_te)
            else:
                P.wait("scalar", self.s_tp)
                P.emit("scalar", lambda E, dst=dst, srcv=srcv: E.activation(
                    dst, srcv, AF.Copy, bias=0.0, scale=1.0), inc=self.s_te)

    def phase1(self):
        nc, P, inp = self.nc, self.P, self.inp
        p1 = contextlib.ExitStack()
        with p1:
            xst = [self.sb(p1, f"xst{i}", [128, D], BF16) for i in range(2)]
            dump = self.sb(p1, "dump", [128, D], BF16)
            rxrow = [self.sb(p1, f"rxrow{i}", [1, 512], BF16) for i in range(2)]
            xlnbf = [self.sb(p1, f"xlnbf{i}", [128, D], BF16) for i in range(3)]
            xlnT = self.sb(p1, "xlnT", [128, 16 * 512], BF16)
            wckv_sb = self.sb(p1, "wckv_sb", [128, 16 * KVR], BF16)
            wkr_sb = self.sb(p1, "wkr_sb", [128, 16 * 96], BF16)
            tabk_sb = [self.sb(p1, f"tabk_sb{i}", [128, 512], F32) for i in range(2)]
            sqc = self.sb(p1, "sqc", [128, 512], BF16)
            rmsr = self.sb(p1, "rmsr", [1, 512], F32)
            rstdb = self.sb(p1, "rstdb", [128, 512], F32)

            with (nc.psum_tensor("tp_ps", [128, 1024], F32) as tp_ps,
                  nc.psum_tensor("ckv_ps", [128, 1024], F32) as ckv_ps,
                  nc.psum_tensor("kr_ps", [128, 1024], F32) as kr_ps,
                  nc.psum_tensor("ax_ps", [128, 1024], F32) as ax_ps):

                P.dma("gpsimd", lambda E: E.dma_start(
                    out=wckv_sb[:].rearrange("p (j n) -> p j n", n=KVR),
                    in_=inp["w_ckv"].rearrange("(j p) n -> p j n", p=128)), self.s_ini)
                P.dma("gpsimd", lambda E: E.dma_start(
                    out=wkr_sb[:].rearrange("p (j n) -> p j n", n=96),
                    in_=inp["w_kr_pad"].rearrange("(j p) n -> p j n", p=128)), self.s_ini)
                for e in ENGS:
                    P.wait(e, self.s_ini)
                P.wait("tensor", self.s_vc)
                P.wait("scalar", self.s_vc)

                krm_red_done = 0
                rstdb_done = 0
                cache_done = 0
                rope_done = 0
                tabk_mul_done = 0
                rx_done = [0, 0]
                rope_hist = {}

                xc_at = {}

                def center_tile(g):
                    buf = g % 2
                    if g >= 2:
                        P.wait("gpsimd", self.s_ap, self.s_ap.n - 1)
                    P.dma("gpsimd", lambda E, g=g, buf=buf: E.dma_start(
                        out=xst[buf][:], in_=inp["xp"][g * 128:(g + 1) * 128, :]),
                        self.s_x)
                    P.wait("vector", self.s_x)
                    xc_at[g] = self.emit_center(xst[buf][:], xlnbf[g % 3][:],
                                                dump, g % 2)

                def tiles(b):
                    P.wait("sync", self.s_cw, rope_hist.get(b - 2, 0))
                    P.dma("sync", lambda E, b=b: E.dma_start(
                        out=tabk_sb[b % 2][:],
                        in_=inp["tabK"][:, b * 512:(b + 1) * 512]), self.s_tk)
                    for it in range(3):
                        center_tile(b * 4 + it)
                    for it in range(4):
                        g = b * 4 + it
                        P.wait("tensor", self.s_ap, xc_at[g])
                        self.emit_transpose(xlnbf[g % 3], xlnT, it, tp_ps)
                        self.emit_rstd(xlnbf[g % 3][:], dump, g % 2, it)
                        if it == 0:
                            center_tile(b * 4 + 3)

                def mms(b):
                    P.wait("tensor", self.s_sc)
                    if b > 0:
                        P.wait("tensor", self.s_sc, rx_done[(b - 1) % 2])
                        P.wait("tensor", self.s_cw, tabk_mul_done)
                    for it2 in range(4):
                        P.emit("tensor", lambda E, it2=it2: E.matmul(
                            kr_ps[64:65, 512 + it2 * 128:512 + (it2 + 1) * 128],
                            self.rstd_bf[:, it2:it2 + 1], self.ident_sb[:],
                            start=True, stop=True, skip_group_check=True),
                            inc=(self.s_ax if it2 == 3 else None))
                    self.krow_at = self.s_ax.n
                    P.wait("scalar", self.s_ax)
                    P.emit("scalar", lambda E, b=b: E.activation(
                        rxrow[b % 2][:], kr_ps[64:65, 512:1024], AF.Copy,
                        bias=0.0, scale=1.0), inc=self.s_sc)
                    rx_done[b % 2] = self.s_sc.n
                    P.wait("tensor", self.s_te)
                    if b > 0:
                        P.wait("tensor", self.s_cw, cache_done)
                    for ot in range(2):
                        for j in range(16):
                            P.emit("tensor", lambda E, ot=ot, j=j: E.matmul(
                                ckv_ps[:, ot * 512:(ot + 1) * 512],
                                wckv_sb[:, j * KVR + ot * 128:j * KVR + (ot + 1) * 128],
                                xlnT[:, j * 512:(j + 1) * 512],
                                start=(j == 0), stop=(j == 15)),
                                inc=(self.s_mm if j == 15 else None))
                    P.wait("scalar", self.s_mm, self.s_mm.n - 1)
                    P.emit("scalar", lambda E: E.activation(
                        sqc[:], ckv_ps[:, 0:512], AF.Square), inc=self.s_sc)
                    sqa_done = self.s_sc.n
                    if b > 0:
                        P.wait("tensor", self.s_cw, rope_done)
                    for j in range(16):
                        P.emit("tensor", lambda E, j=j: E.matmul(
                            kr_ps[0:96, 0:512],
                            wkr_sb[:, j * 96:j * 96 + 96],
                            xlnT[:, j * 512:(j + 1) * 512],
                            start=(j == 0), stop=(j == 15)),
                            inc=(self.s_mm if j == 15 else None))
                        if j == 2:
                            P.wait("tensor", self.s_sc, sqa_done)
                            P.wait("tensor", self.s_vc, krm_red_done)
                            P.emit("tensor", lambda E: E.matmul(
                                ax_ps[0:1, 0:512], self.ones_bf[:], sqc[:],
                                start=True, stop=False,
                                skip_group_check=True), inc=self.s_ax)
                            P.wait("scalar", self.s_ax)
                            P.emit("scalar", lambda E: E.activation(
                                sqc[:], ckv_ps[:, 512:1024], AF.Square),
                                inc=self.s_sc)
                            sqb_done = self.s_sc.n
                        if j == 8:
                            P.wait("tensor", self.s_sc, sqb_done)
                            P.emit("tensor", lambda E: E.matmul(
                                ax_ps[0:1, 0:512], self.ones_bf[:], sqc[:],
                                start=False, stop=True,
                                skip_group_check=True), inc=self.s_ax)
                            P.wait("scalar", self.s_ax)
                            P.emit("scalar", lambda E: E.activation(
                                rmsr[:], ax_ps[0:1, 0:512], AF.Ln,
                                bias=self.eps_t[0:1, 1:2], scale=1.0 / KVR),
                                inc=self.s_sc)
                            P.wait("scalar", self.s_sc)
                            P.emit("scalar", lambda E: E.activation(
                                rmsr[:], rmsr[:], AF.Exp, bias=0.0, scale=-0.5),
                                inc=self.s_sc)
                    P.wait("tensor", self.s_sc, rx_done[b % 2])
                    P.emit("tensor", lambda E, b=b: E.matmul(
                        kr_ps[0:96, 512:1024], self.ones_rbf[0:1, 0:96],
                        rxrow[b % 2][:],
                        start=True, stop=True, skip_group_check=True), inc=self.s_ax)
                    P.wait("tensor", self.s_sc)
                    P.wait("tensor", self.s_vc, rstdb_done)
                    P.emit("tensor", lambda E: E.matmul(
                        ax_ps[:, 512:1024], self.ones_row[:], rmsr[:],
                        start=True, stop=True, skip_group_check=True),
                        inc=self.s_ax)

                def block_end(b):
                    nonlocal krm_red_done, rstdb_done, cache_done, rope_done
                    nonlocal tabk_mul_done
                    P.wait("vector", self.s_tk)
                    P.wait("vector", self.s_ax)
                    P.emit("vector", lambda E, b=b: E.tensor_mul(
                        tabk_sb[b % 2][32:64, :], tabk_sb[b % 2][32:64, :],
                        kr_ps[32:64, 512:1024]),
                        inc=self.s_cw)
                    P.emit("vector", lambda E, b=b: E.tensor_mul(
                        tabk_sb[b % 2][64:96, :], tabk_sb[b % 2][64:96, :],
                        kr_ps[64:96, 512:1024]),
                        inc=self.s_cw)
                    tabk_mul_done = self.s_cw.n
                    P.emit("vector", lambda E, b=b: E.tensor_mul(
                        self.kaug[0][64:96, b * 512:(b + 1) * 512],
                        kr_ps[64:96, 0:512],
                        tabk_sb[b % 2][64:96, :]), inc=self.s_cw)
                    P.emit("vector", lambda E, b=b: E.tensor_mul(
                        self.kaug[1][32:64, b * 512:(b + 1) * 512],
                        kr_ps[32:64, 0:512],
                        tabk_sb[b % 2][32:64, :]), inc=self.s_cw)
                    rope_done = self.s_cw.n
                    rope_hist[b] = rope_done
                    P.emit("vector", lambda E: E.tensor_copy(
                        rstdb[:], ax_ps[:, 512:1024]), inc=self.s_vc)
                    rstdb_done = self.s_vc.n
                    for ot in range(2):
                        P.emit("vector", lambda E, b=b, ot=ot: E.scalar_tensor_tensor(
                            self.ckv_sb[:, ot * P_TOT + b * 512:ot * P_TOT + (b + 1) * 512],
                            ckv_ps[:, ot * 512:(ot + 1) * 512],
                            self.kvg_sb[:, ot:ot + 1], rstdb[:], ALU.mult, ALU.mult),
                            inc=self.s_cw)
                    cache_done = self.s_cw.n
                    P.wait("scalar", self.s_cw)
                    P.emit("scalar", lambda E, b=b: E.activation(
                        sqc[64:96, :], self.kaug[0][64:96, b * 512:(b + 1) * 512],
                        AF.Square), inc=self.s_sc)
                    P.wait("tensor", self.s_sc)
                    P.emit("tensor", lambda E: E.matmul(
                        ax_ps[0:1, 0:512], self.ones_bf[64:96, :], sqc[64:96, :],
                        start=True, stop=True), inc=self.s_ax)
                    P.wait("vector", self.s_ax)
                    P.emit("vector", lambda E: E.reduce_max(
                        self.krm[0:1, 1:2], ax_ps[0:1, 0:512], axis=AX.X),
                        inc=self.s_vc)
                    krm_red_done = self.s_vc.n
                    P.wait("vector", self.s_vc)
                    P.emit("vector", lambda E: E.tensor_max(
                        self.krm[0:1, 0:1], self.krm[0:1, 0:1], self.krm[0:1, 1:2]),
                        inc=self.s_cw)
                    return

                pend = None
                for b in range(NB):
                    tiles(b)
                    if pend is not None:
                        block_end(pend)
                    mms(b)
                    pend = b
                block_end(pend)
                if "p1x" in self.dbg:
                    P.wait("gpsimd", self.s_cw)
                    P.wait("gpsimd", self.s_ap)
                    for nm2, src2 in [("xlnT", xlnT), ("stm", self.stm),
                                      ("rstdc", self.rstd_c), ("rstdb", rstdb)]:
                        P.dma("gpsimd", lambda E, nm2=nm2, src2=src2: E.dma_start(
                            out=self.dbg_t[nm2], in_=src2[:]), self.s_ou)

    def prep(self):
        nc, P, inp = self.nc, self.P, self.inp
        pr = contextlib.ExitStack()
        with pr:
            xst = [self.sb(pr, f"pxst{i}", [128, D], F32) for i in range(2)]
            dump = self.sb(pr, "pdump", [128, D], BF16)
            xlnbf = [self.sb(pr, f"pxlnbf{i}", [128, D], BF16) for i in range(2)]
            xlnT = self.sb(pr, "pxlnT", [128, 16 * 512], BF16)
            wcq = [self.sb(pr, f"wcq{i}", [128, QR], BF16) for i in range(3)]
            sq2 = [self.sb(pr, f"sq2{i}", [128, 512], BF16) for i in range(2)]
            rmsr2 = self.sb(pr, "rmsr2", [1, 512], F32)
            rstdb2 = self.sb(pr, "rstdb2", [128, 512], F32)

            with (nc.psum_tensor("cq_ps", [128, 6 * 512], F32) as cq_ps,
                  nc.psum_tensor("pax_ps", [128, 1024], F32) as pax_ps):
                self.phase_barrier(("sync", "gpsimd", "tensor"),
                                   ("sc", "vc", "ap", "cw", "ax", "te", "tk"))
                P.dma("sync", lambda E: E.dma_start(
                    out=self.tabq_sb[:], in_=inp["tabQ"]), self.s_ini)
                P.dma("gpsimd", lambda E: E.dma_start(
                    out=wcq[0][:], in_=inp["w_cq"][0:128, :]), self.s_wc)
                self.cq_at = []
                for qb in range(4):
                    for it in range(4):
                        g = qb * 4 + it
                        buf = g % 2
                        if g >= 2:
                            P.wait("sync", self.s_ap, self.s_ap.n - 1)
                        P.dma("sync", lambda E, g=g, buf=buf: E.dma_start(
                            out=xst[buf][:],
                            in_=inp["xp"][(56 + g) * 128:(57 + g) * 128, :]), self.s_x)
                        P.wait("vector", self.s_x)
                        P.wait("vector", self.s_tp, self.s_tp.n - 4)
                        self.emit_center(xst[buf][:], xlnbf[g % 2][:], dump, g % 2)
                        P.wait("tensor", self.s_ap)
                        self.emit_transpose(xlnbf[g % 2], xlnT, it, pax_ps)
                    P.wait("tensor", self.s_te)
                    P.wait("tensor", self.s_qe)
                    for j in range(16):
                        if not (qb == 3 and j == 15):
                            gj = 16 * qb + j + 1
                            if len(self.cq_at) >= 2:
                                P.wait("gpsimd", self.s_pq, self.cq_at[-2])
                            P.dma("gpsimd", lambda E, jn=(j + 1) % 16, gj=gj: E.dma_start(
                                out=wcq[gj % 3][:],
                                in_=inp["w_cq"][jn * 128:(jn + 1) * 128, :]), self.s_wc)
                        P.wait("tensor", self.s_wc, 16 * (16 * qb + j + 1))
                        for m in range(6):
                            P.emit("tensor", lambda E, j=j, m=m, qb=qb: E.matmul(
                                cq_ps[:, m * 512:(m + 1) * 512],
                                wcq[(16 * qb + j) % 3][:, m * 128:(m + 1) * 128],
                                xlnT[:, j * 512:(j + 1) * 512],
                                start=(j == 0), stop=(j == 15)),
                                inc=(self.s_pq if m == 5 else None))
                        self.cq_at.append(self.s_pq.n)
                    P.wait("scalar", self.s_pq)
                    ax_at = []
                    for m in range(6):
                        if m >= 2:
                            P.wait("scalar", self.s_ax, ax_at[m - 2])
                        P.emit("scalar", lambda E, m=m: E.activation(
                            sq2[m % 2][:], cq_ps[:, m * 512:(m + 1) * 512], AF.Square),
                            inc=self.s_sc)
                        P.wait("tensor", self.s_sc)
                        P.emit("tensor", lambda E, m=m: E.matmul(
                            pax_ps[0:1, 0:512], self.ones_bf[:], sq2[m % 2][:],
                            start=(m == 0), stop=(m == 5)), inc=self.s_ax)
                        ax_at.append(self.s_ax.n)
                    P.wait("scalar", self.s_ax)
                    P.emit("scalar", lambda E: E.activation(
                        rmsr2[:], pax_ps[0:1, 0:512], AF.Ln,
                        bias=self.eps_t[0:1, 1:2], scale=1.0 / QR), inc=self.s_sc)
                    P.wait("scalar", self.s_sc)
                    P.emit("scalar", lambda E: E.activation(
                        rmsr2[:], rmsr2[:], AF.Exp, bias=0.0, scale=-0.5), inc=self.s_sc)
                    P.wait("tensor", self.s_sc)
                    P.emit("tensor", lambda E: E.matmul(
                        pax_ps[:, 512:1024], self.ones_row[:], rmsr2[:],
                        start=True, stop=True), inc=self.s_ax)
                    P.wait("vector", self.s_ax)
                    P.emit("vector", lambda E: E.tensor_copy(
                        rstdb2[:], pax_ps[:, 512:1024]), inc=self.s_vc)
                    for m in range(6):
                        P.emit("vector", lambda E, m=m, qb=qb: E.scalar_tensor_tensor(
                            self.normq[:, m * QL + qb * 512:m * QL + (qb + 1) * 512],
                            cq_ps[:, m * 512:(m + 1) * 512],
                            self.qg_sb[:, m:m + 1], rstdb2[:], ALU.mult, ALU.mult),
                            inc=self.s_qe)
                if "prx" in self.dbg:
                    P.wait("gpsimd", self.s_qe)
                    P.dma("gpsimd", lambda E: E.dma_start(
                        out=self.dbg_t["xlnT"], in_=xlnT[:]), self.s_ou)
                    P.dma("gpsimd", lambda E: E.dma_start(
                        out=self.dbg_t["rstdb"], in_=rstdb2[:]), self.s_ou)

    def attention(self):
        nc, P, inp = self.nc, self.P, self.inp
        p2 = contextlib.ExitStack()
        with p2:
            qcur = [self.sb(p2, f"qcur{i}", [128, QL], BF16) for i in range(2)]
            pp = [self.sb(p2, f"pp{i}", [128, 2048], BF16) for i in range(2)]
            qsq = self.sb(p2, "qsq", [128, QL], BF16)
            dn_tmp = self.sb(p2, "dn_tmp", [128, QL], F32)
            aostg = self.sb(p2, "aostg", [64, QL], BF16)
            sqb = [self.sb(p2, f"sqb{i}", [128, 512], BF16) for i in range(2)]
            wvec = self.sb(p2, "wvec", [128, 2], BF16)
            wkn_sb = self.sb(p2, "wkn_sb", [128, 2 * 1024], BF16)
            wv_sb = self.sb(p2, "wv_sb", [128, 2 * 1024], BF16)
            wqn_sb = self.sb(p2, "wqn_sb", [128, 6 * 1024], BF16)
            wqr_sb = self.sb(p2, "wqr_sb", [128, 6 * 576], BF16)

            self.phase_barrier(("sync", "gpsimd", "tensor", "vector", "scalar"),
                               ("sc", "vc", "ap", "qe", "ax", "te", "pq", "mm"))
            P.emit("vector", lambda E: E.memset(wvec[:], 0.0))
            P.emit("vector", lambda E: E.memset(qcur[0][96:128, :], 0.0))
            P.emit("vector", lambda E: E.memset(qcur[1][0:32, :], 0.0))
            P.emit("vector", lambda E: E.memset(wvec[0:64, 0:1], float(LAM * LAM)))
            P.emit("vector", lambda E: E.memset(wvec[64:128, 1:2], float(LAM * LAM)),
                   inc=self.s_vc)
            for nm, dst, inner in [("w_kn", wkn_sb, 1024), ("w_v", wv_sb, 1024),
                                   ("w_qn", wqn_sb, 1024), ("w_qr_pad", wqr_sb, 576)]:
                P.dma("gpsimd", lambda E, nm=nm, dst=dst, inner=inner: E.dma_start(
                    out=dst[:].rearrange("p (j n) -> p j n", n=inner),
                    in_=inp[nm].rearrange("(j p) n -> p j n", p=128)), self.s_wc)

            with (nc.psum_tensor("s_ps0", [128, 1024], F32) as s0,
                  nc.psum_tensor("s_ps1", [128, 1024], F32) as s1,
                  nc.psum_tensor("o_ps", [65, 2048], F32) as o_ps):
                self.S = [s0, s1]
                P.wait("tensor", self.s_wc)
                P.wait("tensor", self.s_qe)
                P.wait("tensor", self.s_cw)
                P.wait("tensor", self.s_vc)
                self.selreg = nc.alloc_register(mybir.EngineType.PE, name="chsel")
                self.reg_free = [(None, 0)] * 4
                self.pv_done = {}
                self.eg_hist = []
                self.git_base = 0
                self.mh_qsq_at = []
                self.od_at = []

                mrow = dn_tmp[0:1, :]
                for p in range(H // 2):
                    self.emit_pair(p, qcur, sqb, wvec, wkn_sb, wv_sb,
                                   wqn_sb, wqr_sb)
                    self.emit_mhat(2 * p, 0, p, qcur, qsq, mrow)
                    self.emit_mhat(2 * p + 1, 1, p, qcur, qsq, mrow)
                    if "qcur" in self.dbg and p == 0:
                        P.wait("gpsimd", self.s_mh)
                        P.dma("gpsimd", lambda E: E.dma_start(
                            out=self.dbg_t["q0"], in_=qcur[0][:]), self.s_ou)
                        P.dma("gpsimd", lambda E: E.dma_start(
                            out=self.dbg_t["q1"], in_=qcur[1][:]), self.s_ou)
                    self.emit_head(2 * p, 0, qcur, pp, o_ps, dn_tmp, aostg)
                    self.emit_head(2 * p + 1, 1, qcur, pp, o_ps, dn_tmp, aostg)

        P.wait("vector", self.s_od)
        P.emit("vector", lambda E: E.reciprocal(self.recip[:], self.denoms[:]),
               inc=self.s_rc)

    def phase_barrier(self, engines, sem_names):
        for e in engines:
            for nm in sem_names:
                self.P.wait(e, getattr(self, "s_" + nm))

    def sreg(self, r, p0=0, p1=128, c0=0, w=512):
        base = (r % 2) * 512 + c0
        return self.S[r // 2][p0:p1, base:base + w]

    def use_region(self, r):
        sem, tgt = self.reg_free[r]
        if sem is not None:
            self.P.wait("tensor", sem, tgt)

    def release_region(self, r, sem):
        self.reg_free[r] = (sem, sem.n)

    def emit_pair(self, p, qcur, sqb, wvec, wkn_sb, wv_sb, wqn_sb, wqr_sb):
        P = self.P
        kaug, v_sb, ckv, normq = self.kaug, self.v_sb, self.ckv_sb, self.normq
        sq_hist = {}
        bm_hist = {}

        def emit_bm(bb):
            r = bb % 4
            P.wait("tensor", self.s_sc, sq_hist[bb])
            P.emit("tensor", lambda E, r=r, bb=bb: E.matmul(
                self.sreg(r, 0, 2),
                wvec[:], sqb[bb % 2][:], start=True, stop=True,
                skip_group_check=True), inc=self.s_bn)
            P.wait("vector", self.s_bn)
            P.emit("vector", lambda E, r=r: E.reduce_max(
                self.bm_scr[:], self.sreg(r, 0, 2), axis=AX.X),
                inc=self.s_vc)
            P.wait("vector", self.s_vc)
            P.emit("vector", lambda E, p=p: E.tensor_max(
                self.bm2[:, p:p + 1], self.bm2[:, p:p + 1], self.bm_scr[:]),
                inc=self.s_bn)
            bm_hist[bb] = self.s_bn.n
            self.release_region(r, self.s_bn)

        for blk in range(NB):
            r = blk % 4
            self.use_region(r)
            for ot in range(2):
                P.emit("tensor", lambda E, blk=blk, ot=ot, r=r, p=p: E.matmul(
                    self.sreg(r),
                    wkn_sb[:, ot * 1024 + p * 128:ot * 1024 + (p + 1) * 128],
                    ckv[:, ot * P_TOT + blk * 512:ot * P_TOT + (blk + 1) * 512],
                    start=(ot == 0), stop=(ot == 1), skip_group_check=True),
                    inc=(self.s_ex if ot == 1 else None))
            P.wait("vector", self.s_ex)
            P.emit("vector", lambda E, blk=blk, r=r: E.tensor_scalar(
                kaug[0][0:64, blk * 512:(blk + 1) * 512],
                self.sreg(r, 0, 64), LAM, None, ALU.mult))
            P.emit("vector", lambda E, blk=blk, r=r: E.tensor_scalar(
                kaug[1][64:128, blk * 512:(blk + 1) * 512],
                self.sreg(r, 64, 128), LAM, None, ALU.mult),
                inc=self.s_xv)
            P.wait("scalar", self.s_xv)
            if blk >= 2:
                P.wait("scalar", self.s_bn, bm_hist.get(blk - 2, 0))
            P.emit("scalar", lambda E, r=r, blk=blk: E.activation(
                sqb[blk % 2][:], self.sreg(r), AF.Square), inc=self.s_sc)
            sq_hist[blk] = self.s_sc.n
            if blk >= 1:
                emit_bm(blk - 1)
        emit_bm(NB - 1)

        for gk in range(NB):
            r = gk % 4
            self.use_region(r)
            for sub in range(4):
                kt = gk * 4 + sub
                for ot in range(2):
                    P.emit("tensor", lambda E, kt=kt, sub=sub, ot=ot, r=r, p=p: E.matmul(
                        self.sreg(r, c0=sub * 128, w=128),
                        ckv[:, ot * P_TOT + kt * 128:ot * P_TOT + (kt + 1) * 128],
                        wv_sb[:, ot * 1024 + p * 128:ot * 1024 + (p + 1) * 128],
                        start=(ot == 0), stop=(ot == 1), skip_group_check=True),
                        inc=(self.s_ex if (ot == 1 and sub == 3) else None))
            P.wait("vector", self.s_ex)
            for lane in range(2):
                P.emit("vector", lambda E, gk=gk, r=r, lane=lane: E.tensor_copy(
                    v_sb[:].rearrange("p (k c) -> p k c", c=130)
                    [:, gk * 4:(gk + 1) * 4, lane * 65:lane * 65 + 64],
                    self.sreg(r)
                    .rearrange("p (k c) -> p k c", c=128)
                    [:, :, lane * 64:(lane + 1) * 64]),
                    inc=(self.s_xv if lane == 1 else None))
            self.release_region(r, self.s_xv)

        for qc in range(4):
            r = qc
            self.use_region(r)
            for m in range(6):
                P.emit("tensor", lambda E, qc=qc, m=m, r=r, p=p: E.matmul(
                    self.sreg(r),
                    wqn_sb[:, m * 1024 + p * 128:m * 1024 + (p + 1) * 128],
                    normq[:, m * QL + qc * 512:m * QL + (qc + 1) * 512],
                    start=(m == 0), stop=(m == 5), skip_group_check=True),
                    inc=(self.s_ex if m == 5 else None))
            P.wait("vector", self.s_ex)
            P.emit("vector", lambda E, qc=qc, r=r: E.tensor_scalar(
                qcur[0][0:64, qc * 512:(qc + 1) * 512],
                self.sreg(r, 0, 64), LAM, None, ALU.mult))
            P.emit("vector", lambda E, qc=qc, r=r: E.tensor_scalar(
                qcur[1][64:128, qc * 512:(qc + 1) * 512],
                self.sreg(r, 64, 128), LAM, None, ALU.mult),
                inc=self.s_qe)
            self.release_region(r, self.s_qe)
        for par in range(2):
            hh = 2 * p + par
            if par == 0:
                c0, rl, rh, mw = hh * 32, 64, 96, 96
            else:
                c0, rl, rh, mw = 32 + hh * 32, 32, 64, 64
            for qc in range(4):
                r = qc
                self.use_region(r)
                for m in range(6):
                    P.emit("tensor", lambda E, qc=qc, m=m, r=r, c0=c0, mw=mw: E.matmul(
                        self.sreg(r, 0, mw),
                        wqr_sb[:, m * 576 + c0:m * 576 + c0 + mw],
                        normq[:, m * QL + qc * 512:m * QL + (qc + 1) * 512],
                        start=(m == 0), stop=(m == 5), skip_group_check=True),
                        inc=(self.s_ex if m == 5 else None))
                P.wait("vector", self.s_ex)
                P.emit("vector", lambda E, qc=qc, r=r, par=par, rl=rl, rh=rh: E.tensor_mul(
                    qcur[par][rl:rh, qc * 512:(qc + 1) * 512],
                    self.sreg(r, rl, rh),
                    self.tabq_sb[rl:rh, qc * 512:(qc + 1) * 512]), inc=self.s_qe)
                self.release_region(r, self.s_qe)
        P.wait("gpsimd", self.s_bn)
        P.dma("gpsimd", lambda E, p=p: E.dma_start(
            out=self.bmo[0:1, p:p + 1], in_=self.bm2[1:2, p:p + 1]), self.s_od)

    def emit_mhat(self, h, par, p, qcur, qsq, mrow):
        P = self.P
        rl, rh = (0, 96) if par == 0 else (32, 128)
        arow = 96 if par == 0 else 0
        if par == 0:
            bsrc = self.bm2[0:1, p:p + 1]
            P.wait("vector", self.s_bn)
        else:
            P.wait("vector", self.s_od)
            bsrc = self.bmo[0:1, p:p + 1]
        P.emit("vector", lambda E, bsrc=bsrc: E.tensor_add(
            self.lnb[0:1, 0:1], bsrc, self.krm[0:1, 0:1]), inc=self.s_vc)
        P.wait("vector", self.s_qe)
        if len(self.mh_qsq_at) >= 1:
            P.wait("vector", self.s_mh, self.mh_qsq_at[-1])
        qpieces = [(rl, rh)] if rl == 0 else [(32, 64), (64, 128)]
        for pi, (a0, a1) in enumerate(qpieces):
            P.emit("vector", lambda E, par=par, a0=a0, a1=a1: E.tensor_mul(
                qsq[a0:a1, :], qcur[par][a0:a1, :], qcur[par][a0:a1, :]),
                inc=(self.s_vc if pi == len(qpieces) - 1 else None))
        P.wait("scalar", self.s_vc)
        P.emit("scalar", lambda E: E.activation(
            self.lnb[0:1, 1:2], self.lnb[0:1, 0:1], AF.Ln), inc=self.s_sc)
        P.wait("scalar", self.s_sc)
        P.emit("scalar", lambda E: E.activation(
            self.lnb[0:1, 1:2], self.lnb[0:1, 1:2], AF.Copy, bias=0.0, scale=0.5),
            inc=self.s_sc)
        for qc in range(4):
            self.use_region(qc)
            P.wait("tensor", self.s_vc)
            pieces = [(rl, rh)] if rl == 0 else [(32, 64), (64, 128)]
            for pi, (a0, a1) in enumerate(pieces):
                P.emit("tensor", lambda E, qc=qc, a0=a0, a1=a1, pi=pi,
                       np_=len(pieces): E.matmul(
                    self.sreg(qc, 0, 1),
                    self.ones_bf[a0:a1, :], qsq[a0:a1, qc * 512:(qc + 1) * 512],
                    start=(pi == 0), stop=(pi == np_ - 1), skip_group_check=True),
                    inc=(self.s_mh if pi == len(pieces) - 1 else None))
        self.mh_qsq_at.append(self.s_mh.n)
        P.wait("scalar", self.s_mh)
        P.emit("scalar", lambda E: E.activation(
            mrow[0:1, 0:1024], self.S[0][0:1, 0:1024], AF.Ln), inc=self.s_sc)
        P.emit("scalar", lambda E: E.activation(
            mrow[0:1, 1024:2048], self.S[1][0:1, 0:1024], AF.Ln), inc=self.s_sc)
        P.wait("scalar", self.s_sc)
        P.emit("scalar", lambda E, par=par, arow=arow: E.activation(
            qcur[par][arow:arow + 1, :], mrow[:], AF.Exp,
            bias=self.lnb[0:1, 1:2], scale=0.5), inc=self.s_mh)
        for r in range(4):
            self.release_region(r, self.s_mh)

    def emit_head(self, h, par, qcur, pp, o_ps, dn_tmp, aostg):
        P = self.P
        kaug, v_sb = self.kaug, self.v_sb
        gbase = self.git_base
        P.wait("tensor", self.s_oe)
        for r in range(4):
            P.emit("tensor", lambda E, r=r: E.matmul(
                o_ps[0:65, r * 512:(r + 1) * 512],
                self.zer_bf[:, 0:65], self.zer_bf[:, 0:512],
                start=True, stop=False, skip_group_check=True), inc=self.s_pv)
        P.wait("tensor", self.s_mh)
        P.wait("tensor", self.s_qe)
        P.wait("tensor", self.s_xv)

        class Sub:

            def __init__(self, sems):
                self.ops = []
                self.sems = sems

            def emit(self, fn, inc=None, k=1, count=True):
                if inc is None:
                    self.ops.append(fn)
                    return

                def g(E, fn=fn, sem=inc, k=k):
                    fn(E).then_inc(sem.h, k)
                self.ops.append(g)
                if count:
                    inc.n += k

            def wait(self, sem, val):
                if val > 0:
                    self.ops.append(lambda E, hh=sem.h, v=val: E.wait_ge(hh, v))

        def pps(t, hf=None):
            j = t % 4
            buf = pp[j // 2]
            c0 = (j % 2) * 1024
            if hf is None:
                return buf[:, c0:c0 + 1024]
            return buf[:, c0 + hf * 512:c0 + hf * 512 + 512]

        def emit_exp(t, hf=None):
            P.wait("scalar", self.s_sg)
            gi = gbase + t
            if gi - 4 in self.pv_done:
                P.wait("scalar", self.s_pv, self.pv_done[gi - 4])
            if hf is None:
                P.emit("scalar", lambda E, t=t: E.activation(
                    pps(t), self.S[t % 2][:, 0:1024], AF.Exp), inc=self.s_eg)
            else:
                P.emit("scalar", lambda E, t=t, hf=hf: E.activation(
                    pps(t, hf), self.S[t % 2][:, hf * 512:(hf + 1) * 512],
                    AF.Exp), inc=self.s_eg)
            self.eg_hist.append(self.s_eg.n)

        eg_half = {}
        sg_half = {}
        for sg in range(7):
            bodyA = Sub(self)
            bodyB = Sub(self)
            sinks = [bodyA, bodyB]

            def rec_pv(t):
                for hf in range(2):
                    for s in sinks:
                        s.wait(self.s_eg, eg_half[(t, hf)])
                    for si, s in enumerate(sinks):
                        cqv = si * 2 + hf
                        s.emit(lambda E, t=t, par=par, cqv=cqv, hf=hf: E.matmul(
                            o_ps[0:65, cqv * 512:(cqv + 1) * 512],
                            v_sb[:, t * 130 + par * 65:t * 130 + par * 65 + 65],
                            pps(t, hf),
                            start=False, stop=False, skip_group_check=True),
                            inc=(self.s_pv if hf == 1 else None), count=(si == 0))
                self.pv_done[gbase + t] = self.s_pv.n

            for i in range(8):
                t = sg * 8 + i
                for hf in range(2):
                    w = eg_half.get((t - 2, hf))
                    if w is None and len(self.eg_hist) >= 2:
                        w = self.eg_hist[-2]
                    if w:
                        for s in sinks:
                            s.wait(self.s_eg, w)
                    for si, s in enumerate(sinks):
                        cqv = si * 2 + hf
                        s.emit(lambda E, t=t, par=par, cqv=cqv, hf=hf: E.matmul(
                            self.sreg((t % 2) * 2 + hf),
                            kaug[par][:, t * 128:(t + 1) * 128],
                            qcur[par][:, cqv * 512:(cqv + 1) * 512],
                            start=True, stop=True, skip_group_check=True),
                            inc=self.s_sg, count=(si == 0))
                    sg_half[(t, hf)] = self.s_sg.n
                if i > 0:
                    rec_pv(t - 1)
                gi = gbase + t
                for hf in range(2):
                    P.wait("scalar", self.s_sg, sg_half[(t, hf)])
                    if hf == 0 and gi - 4 in self.pv_done:
                        P.wait("scalar", self.s_pv, self.pv_done[gi - 4])
                    P.emit("scalar", lambda E, t=t, hf=hf: E.activation(
                        pps(t, hf), self.S[t % 2][:, hf * 512:(hf + 1) * 512],
                        AF.Exp), inc=self.s_eg)
                    eg_half[(t, hf)] = self.s_eg.n
                    self.eg_hist.append(self.s_eg.n)
            rec_pv(sg * 8 + 7)

            def run_branch(E, sg=sg, A=bodyA.ops, B=bodyB.ops):
                E.reg_load(self.selreg, self.qtab_sb[0:1, sg:sg + 1])
                with E.If_eq(self.selreg, 0):
                    for f in A:
                        f(E)
                with E.Else():
                    for f in B:
                        f(E)
            P.q["tensor"].append(run_branch)

        def iter_info(t):
            i = t - NINT
            cq = 0 if i < 8 else 2
            rr = (i % 8) * 128
            if rr < 512:
                return [(0, cq, 1024 - rr), (1, cq + 1, None)]
            return [(1, cq + 1, 1024 - (rr - 512))]

        def rec_pv_diag(prev):
            t, info, (sem, val) = prev
            P.wait("tensor", sem, val)
            for idx, (hf, cqv, moff) in enumerate(info):
                last = idx == len(info) - 1
                P.emit("tensor", lambda E, t=t, par=par, hf=hf, cqv=cqv: E.matmul(
                    o_ps[0:65, cqv * 512:(cqv + 1) * 512],
                    v_sb[:, t * 130 + par * 65:t * 130 + par * 65 + 65],
                    pps(t, hf),
                    start=False, stop=False, skip_group_check=True),
                    inc=(self.s_pv if last else None))
            self.pv_done[gbase + t] = self.s_pv.n

        prev = None
        for t in range(NINT, NKS):
            info = iter_info(t)
            if len(self.eg_hist) >= 2:
                P.wait("tensor", self.s_eg, self.eg_hist[-2])
            for idx, (hf, cqv, moff) in enumerate(info):
                last = idx == len(info) - 1
                P.emit("tensor", lambda E, t=t, par=par, hf=hf, cqv=cqv: E.matmul(
                    self.sreg((t % 2) * 2 + hf),
                    kaug[par][:, t * 128:(t + 1) * 128],
                    qcur[par][:, cqv * 512:(cqv + 1) * 512],
                    start=True, stop=True, skip_group_check=True),
                    inc=(self.s_sg if last else None))
            if prev is not None:
                rec_pv_diag(prev)
            emit_exp(t, hf=(info[0][0] if len(info) == 1 else None))
            ready = (self.s_eg, self.eg_hist[-1])
            nm = 0
            for (hf, cqv, moff) in info:
                if moff is None:
                    continue
                P.wait("vector", self.s_eg)
                P.emit("vector", lambda E, t=t, hf=hf, moff=moff: E.tensor_mul(
                    pps(t, hf), pps(t, hf),
                    self.stair_sb[:, moff:moff + 512]), inc=self.s_mk)
                nm += 1
            if nm:
                ready = (self.s_mk, self.s_mk.n)
            prev = (t, info, ready)
        rec_pv_diag(prev)
        self.git_base += NKS
        for r in range(4):
            self.release_region(r, self.s_eg)

        P.wait("vector", self.s_pv)
        if len(self.od_at) >= 1:
            P.wait("vector", self.s_od, self.od_at[-1])
        P.emit("vector", lambda E, h=h: E.tensor_copy(
            aostg[:], o_ps[0:64, :]), inc=self.s_oe)
        P.emit("vector", lambda E: E.tensor_copy(
            dn_tmp[64:65, :], o_ps[64:65, :]), inc=self.s_oe)
        P.wait("gpsimd", self.s_oe)
        P.dma("gpsimd", lambda E, h=h: E.dma_start(
            out=self.ao_d[h], in_=aostg[:]), self.s_od)
        P.dma("gpsimd", lambda E, h=h: E.dma_start(
            out=self.denoms[h:h + 1, :], in_=dn_tmp[64:65, :]), self.s_od)
        self.od_at.append(self.s_od.n)

    def proj(self):
        nc, P, inp = self.nc, self.P, self.inp
        pj = contextlib.ExitStack()
        with pj:
            attn = self.sb(pj, "attn_all", [128, 8 * QL], BF16)
            wpj = self.sb(pj, "wproj_sb", [128, 8 * D], BF16)
            xown = [self.sb(pj, f"xown{i}", [128, D], F32) for i in range(2)]
            self.rrow = self.sb(pj, "rrow", [1, QL], F32)
            x2st = [self.sb(pj, f"x2st{i}", [128, D], F32) for i in range(2)]
            with (nc.psum_tensor("pj_ps", [128, 2048], F32) as pj_ps,
                  nc.psum_tensor("bc_ps", [128, 1024], F32) as bc_ps):
                self.phase_barrier(("sync", "gpsimd", "tensor"),
                                   ("eg", "pv", "oe", "od", "mh", "xv", "bn",
                                    "mk", "qe", "rc"))
                P.dma("gpsimd", lambda E: E.dma_start(
                    out=wpj[:].rearrange("p (j n) -> p j n", n=D),
                    in_=inp["w_proj"].rearrange("(j p) n -> p j n", p=128)), self.s_wc)
                P.wait("gpsimd", self.s_od)
                for j in range(8):
                    for sub in range(2):
                        P.dma("gpsimd", lambda E, j=j, sub=sub: E.dma_start(
                            out=attn[sub * 64:(sub + 1) * 64, j * QL:(j + 1) * QL],
                            in_=self.ao_d[2 * j + sub]), self.s_pj)
                P.wait("gpsimd", self.s_rc)
                ma_at = []
                for hh in range(H):
                    P.wait("gpsimd", self.s_pq)
                    P.dma("gpsimd", lambda E, hh=hh: E.dma_start(
                        out=self.rrow[:], in_=self.recip[hh:hh + 1, :]), self.s_pj)
                    P.wait("tensor", self.s_pj)
                    for k4 in range(4):
                        reg = k4 % 2
                        if len(ma_at) >= 2:
                            P.wait("tensor", self.s_ma, ma_at[-2])
                        P.emit("tensor", lambda E, k4=k4, reg=reg: E.matmul(
                            bc_ps[:, reg * 512:(reg + 1) * 512],
                            self.ones_row[:],
                            self.rrow[0:1, k4 * 512:(k4 + 1) * 512],
                            start=True, stop=True, skip_group_check=True),
                            inc=self.s_pq)
                        P.wait("vector", self.s_pq)
                        r0 = (hh % 2) * 64
                        P.emit("vector", lambda E, hh=hh, k4=k4, reg=reg, r0=r0:
                               E.tensor_mul(
                            attn[r0:r0 + 64,
                                 (hh // 2) * QL + k4 * 512:(hh // 2) * QL + (k4 + 1) * 512],
                            attn[r0:r0 + 64,
                                 (hh // 2) * QL + k4 * 512:(hh // 2) * QL + (k4 + 1) * 512],
                            bc_ps[r0:r0 + 64, reg * 512:(reg + 1) * 512]),
                            inc=self.s_ma)
                        ma_at.append(self.s_ma.n)
                if "pn" in self.dbg:
                    P.wait("gpsimd", self.s_ma)
                    P.dma("gpsimd", lambda E: E.dma_start(
                        out=self.dbg_t["pn"], in_=attn[:]), self.s_ou)
                    P.dma("gpsimd", lambda E: E.dma_start(
                        out=self.dbg_t["rcp"], in_=self.recip[:]), self.s_ou)
                P.wait("tensor", self.s_ma)
                P.wait("tensor", self.s_wc)
                adds_at = []
                for tt in range(16):
                    if tt >= 2:
                        P.wait("sync", self.s_vc, adds_at[tt - 2])
                    P.dma("sync", lambda E, tt=tt: E.dma_start(
                        out=xown[tt % 2][:],
                        in_=inp["xp"][(56 + tt) * 128:(57 + tt) * 128, :]), self.s_x)
                    P.wait("tensor", self.s_vc)
                    for j in range(8):
                        for dc in range(4):
                            P.emit("tensor", lambda E, tt=tt, j=j, dc=dc: E.matmul(
                                pj_ps[:, dc * 512:(dc + 1) * 512],
                                attn[:, j * QL + tt * 128:j * QL + (tt + 1) * 128],
                                wpj[:, j * D + dc * 512:j * D + (dc + 1) * 512],
                                start=(j == 0), stop=(j == 7)),
                                inc=(self.s_mm if j == 7 and dc == 3 else None))
                    P.wait("vector", self.s_mm)
                    P.wait("vector", self.s_x)
                    if tt >= 2:
                        P.wait("vector", self.s_x2, 16 * (tt - 1))
                    P.emit("vector", lambda E, tt=tt: E.tensor_add(
                        x2st[tt % 2][:], pj_ps[:], xown[tt % 2][:]), inc=self.s_vc)
                    adds_at.append(self.s_vc.n)
                    P.wait("sync", self.s_vc)
                    P.dma("sync", lambda E, tt=tt: E.dma_start(
                        out=self.x2_d[tt * 128:(tt + 1) * 128, :],
                        in_=x2st[tt % 2][:]), self.s_x2)

    def ffn(self):
        nc, P, inp = self.nc, self.P, self.inp
        ff = contextlib.ExitStack()
        with ff:
            x2t = [self.sb(ff, f"fx2t{i}", [128, D], F32) for i in range(2)]
            dump = self.sb(ff, "fdump", [128, D], BF16)
            xlnbf = [self.sb(ff, f"fxlnbf{i}", [128, D], BF16) for i in range(2)]
            xlnT = self.sb(ff, "fxlnT", [128, 16 * 512], BF16)
            hbuf = self.sb(ff, "hbuf", [128, 64 * 512], BF16)
            wg8 = [self.sb(ff, f"wg8{i}", [128, 2048], U8) for i in range(2)]
            wu8 = [self.sb(ff, f"wu8{i}", [128, 2048], U8) for i in range(2)]
            wgb = [self.sb(ff, f"wgb{i}", [128, 2048], BF16) for i in range(2)]
            wub = [self.sb(ff, f"wub{i}", [128, 2048], BF16) for i in range(2)]
            wd8 = self.sb(ff, "wd8", [128, 8192], U8)
            wdb = [self.sb(ff, f"wdb{i}", [128, 4096], BF16) for i in range(2)]
            sil = self.sb(ff, "sil", [128, 512], F32)
            ytmp = self.sb(ff, "ytmp", [128, 16 * 512], BF16)
            x3st = [self.sb(ff, f"x3st{i}", [128, D], F32) for i in range(2)]

            with (nc.psum_tensor("big_ps", [128, 2048], F32) as big_ps,
                  nc.psum_tensor("g_ps", [128, 512], F32) as g_ps,
                  nc.psum_tensor("u_ps", [128, 512], F32) as u_ps,
                  nc.psum_tensor("y_ps", [128, 1024], F32) as y_ps):
                self.phase_barrier(("sync", "gpsimd", "tensor"),
                                    ("x2", "vc", "mm", "ma", "pq", "x"))
                fo_at = []
                fw0 = self.s_fw.n
                fz0 = self.s_fz.n
                te_mark = [0]
                self.fd_marks = []

                def stage_a(pa):
                    for it in range(4):
                        g = pa * 4 + it
                        buf = g % 2
                        if g >= 2:
                            P.wait("gpsimd", self.s_ap, self.s_ap.n - 1)
                            if fo_at:
                                P.wait("gpsimd", self.s_fo, fo_at[-1])
                        P.dma("gpsimd", lambda E, g=g, buf=buf: E.dma_start(
                            out=x2t[buf][:],
                            in_=self.x2_d[g * 128:(g + 1) * 128, :]), self.s_x)
                        P.wait("scalar", self.s_x)
                        P.wait("vector", self.s_x)
                        P.wait("vector", self.s_tp, self.s_tp.n - 4)
                        self.emit_ln(x2t[buf][:], xlnbf[g % 2][:], dump, g % 2)
                        P.wait("tensor", self.s_ap)
                        if fo_at:
                            P.wait("tensor", self.s_fo, fo_at[-1])
                        self.emit_transpose(xlnbf[g % 2], xlnT, it, big_ps)
                    te_mark[0] = self.s_te.n

                def stage_b(pa):
                    P.wait("tensor", self.s_te)
                    P.dma("sync", lambda E: E.dma_start(
                        out=wg8[0][:], in_=inp["wg_t"][0]), self.s_fw)
                    P.dma("sync", lambda E: E.dma_start(
                        out=wu8[0][:], in_=inp["wu_t"][0]), self.s_fw)
                    for i in range(64):
                        if i + 1 < 64:
                            P.wait("sync", self.s_sc, self.cast_at[-1]
                                   if getattr(self, "cast_at", None) else 0)
                            P.dma("sync", lambda E, i=i: E.dma_start(
                                out=wg8[(i + 1) % 2][:], in_=inp["wg_t"][i + 1]),
                                self.s_fw)
                            P.dma("sync", lambda E, i=i: E.dma_start(
                                out=wu8[(i + 1) % 2][:], in_=inp["wu_t"][i + 1]),
                                self.s_fw)
                        P.wait("scalar", self.s_fw, fw0 + 32 * (64 * pa + i + 1))
                        P.wait("scalar", self.s_fg, self.s_fg.n - 2)
                        P.emit("scalar", lambda E, i=i: E.activation(
                            wgb[i % 2][:], wg8[i % 2][:], AF.Copy, bias=-128.0,
                            scale=1.0))
                        P.emit("scalar", lambda E, i=i: E.activation(
                            wub[i % 2][:], wu8[i % 2][:], AF.Copy, bias=-128.0,
                            scale=1.0), inc=self.s_sc)
                        if not hasattr(self, "cast_at"):
                            self.cast_at = []
                        self.cast_at.append(self.s_sc.n)
                        P.wait("tensor", self.s_sc)
                        for j in range(16):
                            P.emit("tensor", lambda E, i=i, j=j: E.matmul(
                                g_ps[:], wgb[i % 2][:, j * 128:(j + 1) * 128],
                                xlnT[:, j * 512:(j + 1) * 512],
                                start=(j == 0), stop=(j == 15)),
                                inc=(self.s_fg if j == 15 else None))
                        P.wait("tensor", self.s_fh)
                        for j in range(16):
                            P.emit("tensor", lambda E, i=i, j=j: E.matmul(
                                u_ps[:], wub[i % 2][:, j * 128:(j + 1) * 128],
                                xlnT[:, j * 512:(j + 1) * 512],
                                start=(j == 0), stop=(j == 15)),
                                inc=(self.s_fg if j == 15 else None))
                        P.wait("scalar", self.s_fg, self.s_fg.n - 1)
                        P.emit("scalar", lambda E, i=i: E.activation(
                            sil[:], g_ps[:], AF.Silu, bias=0.0,
                            scale=self.gsc_sb[:, i:i + 1]), inc=self.s_sc)
                        P.wait("vector", self.s_sc)
                        P.wait("vector", self.s_fg)
                        P.emit("vector", lambda E, i=i: E.scalar_tensor_tensor(
                            hbuf[:, i * 512:(i + 1) * 512], u_ps[:],
                            self.usc_sb[:, i:i + 1], sil[:], ALU.mult, ALU.mult),
                            inc=self.s_fh)

                def stage_c(pa):
                    P.dma("sync", lambda E: E.dma_start(
                        out=wd8[:], in_=inp["wd_t"][0]), self.s_fz)
                    dcast_at = []
                    for ht in range(16):
                        P.wait("scalar", self.s_fz, fz0 + 16 * (16 * pa + ht + 1))
                        for half in range(2):
                            if self.fd_marks:
                                sem, val = self.fd_marks[-1][half]
                                P.wait("scalar", sem, val)
                            P.emit("scalar", lambda E, ht=ht, half=half: E.activation(
                                wdb[half][:], wd8[:, half * 4096:(half + 1) * 4096],
                                AF.Copy, bias=-128.0, scale=1.0), inc=self.s_sc)
                            dcast_at.append(self.s_sc.n)
                        if ht + 1 < 16:
                            P.wait("sync", self.s_sc, dcast_at[-1])
                            P.dma("sync", lambda E, ht=ht: E.dma_start(
                                out=wd8[:], in_=inp["wd_t"][ht + 1]), self.s_fz)
                        P.wait("tensor", self.s_sc, dcast_at[-2])
                        P.wait("tensor", self.s_fh)
                        P.wait("tensor", self.s_fy, self.s_fy.n - 1)
                        yp = y_ps[:, (ht % 2) * 512:(ht % 2 + 1) * 512]
                        marks = [None, None]
                        for j in range(64):
                            if j == 32:
                                P.wait("tensor", self.s_sc, dcast_at[-1])
                            P.emit("tensor", lambda E, ht=ht, j=j, yp=yp: E.matmul(
                                yp, wdb[j // 32][:, (j % 32) * 128:(j % 32 + 1) * 128],
                                hbuf[:, j * 512:(j + 1) * 512],
                                start=(j == 0), stop=(j == 63)),
                                inc=(self.s_fd if j == 31 else
                                     (self.s_fy if j == 63 else None)))
                            if j == 31:
                                marks[0] = (self.s_fd, self.s_fd.n)
                            if j == 63:
                                marks[1] = (self.s_fy, self.s_fy.n)
                        self.fd_marks.append(marks)
                        P.wait("vector", self.s_fy)
                        P.emit("vector", lambda E, ht=ht, yp=yp: E.tensor_scalar(
                            ytmp[:, ht * 512:(ht + 1) * 512], yp,
                            self.dsc_sb[:, ht:ht + 1], None, ALU.mult), inc=self.s_fy)
                    if "ffx" in self.dbg and pa == 0:
                        P.wait("gpsimd", self.s_fy)
                        P.wait("gpsimd", self.s_te)
                        P.dma("gpsimd", lambda E: E.dma_start(
                            out=self.dbg_t["f1"], in_=xlnT[:]), self.s_ou)
                        P.dma("gpsimd", lambda E: E.dma_start(
                            out=self.dbg_t["f2"], in_=hbuf[:]), self.s_ou)
                        P.dma("gpsimd", lambda E: E.dma_start(
                            out=self.dbg_t["f3"], in_=ytmp[:]), self.s_ou)

                def stage_d(pa):
                    for it in range(4):
                        g = pa * 4 + it
                        P.wait("tensor", self.s_fy)
                        P.wait("tensor", self.s_te, te_mark[0])
                        if fo_at:
                            P.wait("tensor", self.s_fo, fo_at[-1])
                        for ht in range(16):
                            P.emit("tensor", lambda E, it=it, ht=ht: E.matmul(
                                big_ps[:, ht * 128:(ht + 1) * 128],
                                ytmp[:, ht * 512 + it * 128:ht * 512 + (it + 1) * 128],
                                self.ident_sb[:], start=True, stop=True),
                                inc=self.s_tp)
                        P.wait("gpsimd", self.s_ap)
                        P.wait("gpsimd", self.s_fo)
                        P.dma("gpsimd", lambda E, g=g: E.dma_start(
                            out=x2t[g % 2][:],
                            in_=self.x2_d[g * 128:(g + 1) * 128, :]), self.s_x)
                        P.wait("vector", self.s_tp)
                        P.wait("vector", self.s_x)
                        if len(fo_at) >= 2:
                            P.wait("vector", self.s_ou, 16 * (len(fo_at) - 1))
                        P.emit("vector", lambda E, g=g: E.tensor_add(
                            x3st[g % 2][:], big_ps[:], x2t[g % 2][:]), inc=self.s_fo)
                        fo_at.append(self.s_fo.n)
                        P.wait("gpsimd", self.s_fo)
                        P.dma("gpsimd", lambda E, g=g: E.dma_start(
                            out=self.out[g * 128:(g + 1) * 128, :],
                            in_=x3st[g % 2][:]), self.s_ou)

                stage_a(0)
                for pa in range(4):
                    stage_b(pa)
                    stage_c(pa)
                    if pa + 1 < 4:
                        stage_a(pa + 1)
                    stage_d(pa)


def kernel(**inputs):
    mk = MK(DBG)
    nc = mk.build()
    maps = [core_inputs(c, inputs) for c in range(8)]
    res = run_bass_kernel_spmd(nc, maps, core_ids=list(range(8)))
    outp = np.zeros((2, 8192, D), np.float32)
    for c in range(8):
        o = res.results[c]["out"]
        outp[0, 1024 * c:1024 * (c + 1)] = o[0:1024]
        outp[1, 1024 * (7 - c):1024 * (8 - c)] = o[1024:2048]
    return outp
